# version 17
# baseline (speedup 1.0000x reference)
import numpy as np
import ml_dtypes
import concourse.bass as bass
import concourse.mybir as mybir
from concourse.bass_utils import run_bass_kernel_spmd

F32 = mybir.dt.float32
BF16 = mybir.dt.bfloat16
AF = mybir.ActivationFunctionType
ALU = mybir.AluOpType
AX = mybir.AxisListType

T = 2048
C = 1024
H = 16
CS = 256
FF = 4096
NE = 8
NG = 8

SEMS = ("T", "V", "S", "G", "Y", "O", "CONST", "BIAS", "WR", "RIMG", "ROTL",
        "GS", "XR0", "XR1", "XB0", "XB1", "SG0", "SG1",
        "W2C0", "W2C1", "W2C2", "W2C3", "W2C4", "W2C5", "W2C6", "W2C7",
        "W2I0", "W2I1", "W2I2", "W2I3", "W2I4", "W2I5", "W2I6", "W2I7", "O2")


class Em:
    def __init__(self):
        self.q = {e: [] for e in ("T", "V", "S", "G", "Y")}
        self.cnt = {s: 0 for s in SEMS}
        self.synced = {e: {s: 0 for s in SEMS} for e in self.q}

    def op(self, eng, fn, deps=(), inc=0, inc_sem=None, hazard=False):
        need = {}
        for d in deps:
            if d is None:
                continue
            sem, val, dhaz = d
            if sem == eng and not dhaz:
                continue
            if val > need.get(sem, -1):
                need[sem] = val
        waits = []
        for sem, val in need.items():
            if self.synced[eng][sem] >= val:
                continue
            waits.append((sem, val))
            self.synced[eng][sem] = val
        if inc:
            s = inc_sem or eng
            self.cnt[s] += inc
            self.q[eng].append((waits, fn, s, inc))
            return (s, self.cnt[s], hazard)
        self.q[eng].append((waits, fn, None, 0))
        return None

    def replay(self, eng_name, eng, sems):
        for waits, fn, incsem, amt in self.q[eng_name]:
            for sem, val in waits:
                eng.wait_ge(sems[sem], val)
            inst = fn(eng)
            if incsem is not None:
                inst.then_inc(sems[incsem], amt)


def sap(t, dims, offset=0):
    return bass.AP(t, offset, [list(d) for d in dims])


def build_nc(n_halves=2, n_a=(4, 4), n_b=(4, 4), dump=None, attn_serial=False, no_gs=False, hwstag=False, skips=()):
    if isinstance(n_a, int):
        n_a = (n_a, n_a)
    if isinstance(n_b, int):
        n_b = (n_b, n_b)
    nc = bass.Bass("TRN2", target_bir_lowering=False,
                   dynamic_dma_scratch_size=16384)

    x_d = nc.dram_tensor("x", [T, C], F32, kind="ExternalInput")
    wqkv_d = nc.dram_tensor("wqkv", [3 * C, C], F32, kind="ExternalInput")
    wout_d = nc.dram_tensor("wout", [C, C], F32, kind="ExternalInput")
    bout_d = nc.dram_tensor("bout", [C], F32, kind="ExternalInput")
    wl1_d = nc.dram_tensor("wl1", [64, 1], F32, kind="ExternalInput")
    bl1_d = nc.dram_tensor("bl1", [64], F32, kind="ExternalInput")
    wl2_d = nc.dram_tensor("wl2", [H, 64], F32, kind="ExternalInput")
    bl2_d = nc.dram_tensor("bl2", [H], F32, kind="ExternalInput")
    wr_d = nc.dram_tensor("wr", [NE, C], F32, kind="ExternalInput")
    w1_d = nc.dram_tensor("w1", [C, FF], F32, kind="ExternalInput")
    b1_d = nc.dram_tensor("b1", [FF], F32, kind="ExternalInput")
    w2_d = nc.dram_tensor("w2", [FF, C], F32, kind="ExternalInput")
    b2_d = nc.dram_tensor("b2", [C], F32, kind="ExternalInput")
    out_d = nc.dram_tensor("out", [T, C], F32, kind="ExternalOutput")
    dbg_d = nc.dram_tensor("dbg", [128, 8192], F32, kind="ExternalOutput") if dump else None
    dball = {}
    if dump == "all":
        for nm, dt_ in (("d_xT", BF16), ("d_qT", BF16), ("d_kTs", BF16), ("d_vtok", BF16),
                        ("d_eT", BF16), ("d_aoT", BF16), ("d_lam", BF16), ("d_rot", BF16),
                        ("d_z1", F32), ("d_x1h", F32), ("d_st", F32)):
            shape = [128, 32768] if nm == "d_rot" else ([128, 2048] if dt_ == BF16 else [128, 2048])
            if nm == "d_lam":
                shape = [128, 512]
            if nm == "d_z1":
                shape = [128, 1024]
            if nm == "d_st":
                shape = [128, 16]
            if nm == "d_x1h":
                shape = [128, 8192]
            dball[nm] = nc.dram_tensor(nm, shape, dt_, kind="ExternalOutput")

    rotimg_d = nc.dram_tensor("rotimg", [128, 32768], BF16, kind="Internal")
    w2img_d = nc.dram_tensor("w2img", [128, 32768], BF16, kind="Internal")

    ident_bf_d = nc.inline_tensor(np.eye(128, dtype=ml_dtypes.bfloat16), name="cidbf")
    ident_f_d = nc.inline_tensor(np.eye(32, dtype=np.float32), name="cidf")
    ones_bf_d = nc.inline_tensor(np.ones((128, 128), dtype=ml_dtypes.bfloat16), name="conesbf")
    sel_np = np.zeros((H, 8 * 128), dtype=ml_dtypes.bfloat16)
    for cc in range(8):
        for p in range(128):
            sel_np[2 * cc + p // 64, cc * 128 + p] = 1.0
    sel_d = nc.inline_tensor(sel_np, name="cselbf")

    A = nc.alloc_sbuf_tensor
    rot = A("rot", [128, 32768], BF16)
    w1a = A("w1a", [128, 32768], BF16)
    x1h = A("x1h", [128, 8192], F32)
    xcast = A("xcast", [128, 2048], BF16)
    xT = A("xT", [128, 2048], BF16)
    xf32 = A("xf32", [128, 3072], F32)
    qT = A("qT", [128, 2048], BF16)
    kTs = A("kTs", [128, 2048], BF16)
    vtok = A("vtok", [128, 2048], BF16)
    eT = A("eT", [128, 2048], BF16)
    aoT = A("aoT", [128, 2048], BF16)
    lam_sb = A("lam_sb", [128, 512], BF16)
    stag = nc.alloc_sbuf_tensor_at("stag", [128, 1024], BF16,
                                   offset=nc.lookup_mloc(eT).addr)
    lambf = A("lambf", [16, 2048], BF16)
    identbf = A("identbf", [128, 128], BF16)
    identf = nc.alloc_sbuf_tensor_at("identf", [32, 32], F32,
                                     offset=nc.lookup_mloc(eT).addr + 2624)
    onesbf = A("onesbf", [128, 128], BF16)
    selbf = A("selbf", [16, 1024], BF16)

    b1col = A("b1col", [128, 32], F32)
    bl1col = nc.alloc_sbuf_tensor_at("bl1col", [64, 1], F32,
                                     offset=nc.lookup_mloc(eT).addr + 2752)
    bl2col = nc.alloc_sbuf_tensor_at("bl2col", [16, 1], F32,
                                     offset=nc.lookup_mloc(eT).addr + 2784)
    stz = A("stz", [128, 8], F32)
    wl1row = nc.alloc_sbuf_tensor_at("wl1row", [1, 64], F32,
                                     offset=nc.lookup_mloc(eT).addr + 2048)
    wl2nat = nc.alloc_sbuf_tensor_at("wl2nat", [16, 64], F32,
                                     offset=nc.lookup_mloc(eT).addr + 2304)
    wl2T = nc.alloc_sbuf_tensor_at("wl2T", [64, 16], F32,
                                   offset=nc.lookup_mloc(eT).addr + 2560)
    wrT = A("wrT", [128, 64], BF16)
    _et2 = nc.lookup_mloc(eT).addr
    er = nc.alloc_sbuf_tensor_at("er", [128, 8], F32, offset=_et2 + 2816)
    p7 = nc.alloc_sbuf_tensor_at("p7", [128, 4], F32, offset=_et2 + 2848)
    rmax = nc.alloc_sbuf_tensor_at("rmax", [128, 1], F32, offset=_et2 + 2880)
    rsum = nc.alloc_sbuf_tensor_at("rsum", [128, 1], F32, offset=_et2 + 2912)
    rrec = nc.alloc_sbuf_tensor_at("rrec", [128, 1], F32, offset=_et2 + 2944)
    stzb = [nc.alloc_sbuf_tensor_at(f"stzb{i}", [128, 8], F32,
                                    offset=_et2 + 2976 + 32 * i) for i in range(4)]
    cconst = A("cconst", [128, 2], F32)
    hT = nc.alloc_sbuf_tensor_at("hT", [128, 2048], BF16,
                                 offset=nc.lookup_mloc(aoT).addr)
    _lam_addr = nc.lookup_mloc(lambf).addr
    boutrow_t = nc.alloc_sbuf_tensor_at("boutrow", [33, 2048], BF16, offset=_lam_addr)
    b2row_t = nc.alloc_sbuf_tensor_at("b2row", [65, 2048], BF16, offset=_lam_addr)
    _sel_addr = nc.lookup_mloc(selbf).addr
    rTbf_t = nc.alloc_sbuf_tensor_at("rTbf", [33, 1024], BF16, offset=_sel_addr)
    boutrow = lambda c0, n: sap(boutrow_t, [[2048, 1], [1, n]], offset=32 * 2048 + c0)
    b2row = lambda c0, n: sap(b2row_t, [[2048, 1], [1, n]], offset=64 * 2048 + c0)
    rTbf = lambda c0, n: sap(rTbf_t, [[1024, 1], [1, n]], offset=32 * 1024 + c0)

    ps = [nc.alloc_psum_tensor(f"ps{i}", [128, 1024], F32) for i in range(4)]

    def pk(bank, c0, n, p0=0, np_=128):
        return bass.AP(ps[bank // 2], p0 * 1024 + (bank % 2) * 512 + c0,
                       [[1024, np_], [1, n]])

    sems = {s: nc.alloc_semaphore(name=f"sem{s}") for s in SEMS}
    E = Em()

    def TT_(fn, deps=(), inc=0, hz=False):
        return E.op("T", fn, deps, inc, None, hz)

    def VV(fn, deps=(), inc=0, hz=True):
        return E.op("V", fn, deps, inc, None, True)

    def SS(fn, deps=(), inc=0, hz=True):
        return E.op("S", fn, deps, inc, None, True)

    def GG(fn, deps=(), inc=0, hz=True):
        return E.op("G", fn, deps, inc, None, True)

    def dmaG(dst, src, deps=(), sem="G"):
        return E.op("G", lambda e, d=dst, s_=src: e.dma_start(d, s_), deps, 16, sem, True)

    def dmaY(dst, src, deps=(), sem="Y"):
        return E.op("Y", lambda e, d=dst, s_=src: e.dma_start(d, s_), deps, 16, sem, True)

    def dmaO(dst, src, deps=(), sem="O"):
        return E.op("G", lambda e, d=dst, s_=src: e.dma_start(d, s_), deps, 16, sem, True)

    pendT = {"amt": 0}

    def mm(out_ap, lhsT, rhs, first, last, deps=(), inc=0, new_gen=False,
           flush=False):
        st = bool(first and new_gen)

        def fn(e, o=out_ap, l=lhsT, r=rhs, s_=st, la=last):
            return e.matmul(o, l, r, start=s_, stop=la, skip_group_check=True)
        tok = None
        attach = 0
        if inc:
            attach = pendT["amt"]
            pendT["amt"] = inc
            E.cnt["T"] += inc
            tok = ("T", E.cnt["T"], False)
        need = {}
        for d in deps:
            if d is None:
                continue
            sem, val, dhaz = d
            if sem == "T" and not dhaz:
                continue
            if val > need.get(sem, -1):
                need[sem] = val
        waits = []
        for sem, val in need.items():
            if E.synced["T"][sem] >= val:
                continue
            waits.append((sem, val))
            E.synced["T"][sem] = val
        E.q["T"].append((waits, fn, "T" if attach else None, attach))
        if flush:
            mm_flush()
        return tok

    def mm_flush():
        if pendT["amt"]:
            a = pendT["amt"]
            pendT["amt"] = 0
            E.q["T"].append(([], lambda e: e.drain(), "T", a))

    t_id = dmaY(identbf[:, :], ident_bf_d.ap(), sem="CONST")
    t_idf = dmaY(identf[:, :], ident_f_d.ap(), sem="CONST")
    t_ones = dmaY(onesbf[:, :], ones_bf_d.ap(), sem="CONST")
    t_sel = dmaY(selbf[:, :], sel_d.ap(), sem="CONST")
    t_wl1 = dmaY(sap(wl1row, [[64, 1], [1, 64]]), bass.AP(wl1_d, 0, [[64, 1], [1, 64]]), sem="CONST")
    t_bl1 = dmaY(sap(bl1col, [[1, 64], [1, 1]]), bass.AP(bl1_d, 0, [[1, 64], [1, 1]]), sem="CONST")
    t_wl2 = dmaY(wl2nat[:, :], wl2_d.ap(), sem="CONST")
    t_bl2 = dmaY(sap(bl2col, [[1, 16], [1, 1]]), bass.AP(bl2_d, 0, [[1, 16], [1, 1]]), sem="CONST")
    wrnat = None
    stag_f = stag[:, :].bitcast(F32)
    t_b1n = dmaY(sap(stag, [[1024, 32], [1, 256]]).bitcast(F32),
                 bass.AP(b1_d, 0, [[128, 32], [1, 128]]), sem="CONST")
    rowtmp = sap(x1h, [[8192, 1], [1, 1024]], offset=64 * 8192 + 2048)
    t_bout = dmaY(rowtmp, bass.AP(bout_d, 0, [[1024, 1], [1, 1024]]), sem="CONST")
    tok_const = ("CONST", E.cnt["CONST"], True)
    t_id = t_idf = t_ones = t_sel = t_wl1 = t_bl1 = t_wl2 = t_bl2 = tok_const
    t_b1n = t_bout = tok_const

    t_c12 = VV(lambda e: e.memset(cconst[:, 0:1], -12.0), (), 1, True)
    t_c18 = VV(lambda e: e.memset(cconst[:, 1:2], 0.125), (), 1, True)

    SKIP = set(skips)
    logpos = sap(x1h, [[8192, 1], [1, 2048]], offset=0)
    if "lam" in SKIP:
        t_iota = t_logp = t_lambf = None
        t_h1 = [None] * 4
        t_lam = [None] * 4
        t_wl2T = None
    t_iota = GG(lambda e: e.iota(logpos, [[1, 2048]], base=0, channel_multiplier=0,
                                 allow_small_or_imprecise_dtypes=True), (), 1, True) if "lam" not in SKIP else None
    if "lam" not in SKIP:
        t_logp = SS(lambda e: e.activation(logpos, logpos, AF.Ln, bias=1.0, scale=1.0),
                    (t_iota,), 1)

    t_h1 = []
    for i in range(0 if "lam" in SKIP else 4):
        prev_ev = t_h1[-1] if t_h1 else None
        tokm = mm(pk(0, 0, 512, 0, 64), wl1row[0:1, :],
                  sap(x1h, [[8192, 1], [1, 512]], offset=i * 512),
                  True, True, (t_wl1, t_logp, prev_ev), 1, new_gen=True, flush=True)
        toke = SS(lambda e, i=i: e.activation(
            sap(x1h, [[8192, 64], [1, 512]], offset=2048 + i * 512),
            pk(0, 0, 512, 0, 64), AF.Relu, bias=bl1col[:, :], scale=1.0),
            (tokm, t_bl1), 1)
        t_h1.append(toke)

    if "lam" not in SKIP:
        t_wl2T_mm = mm(pk(2, 0, 16, 0, 64), wl2nat[:, :], identf[0:16, 0:16],
                       True, True, (t_wl2, t_idf), 1, new_gen=True, flush=True)
        t_wl2T = VV(lambda e: e.tensor_copy(wl2T[:, :], pk(2, 0, 16, 0, 64)), (t_wl2T_mm,), 1)
    else:
        t_wl2T = None

    t_lam = []
    for i in range(0 if "lam" in SKIP else 4):
        prev_ev = t_lam[-1] if t_lam else None
        tokm = mm(pk(1, 0, 512, 0, 16), wl2T[:, :],
                  sap(x1h, [[8192, 64], [1, 512]], offset=2048 + i * 512),
                  True, True, (t_wl2T, t_h1[i], prev_ev), 1, new_gen=True, flush=True)
        toke = SS(lambda e, i=i: e.activation(
            sap(x1h, [[8192, 16], [1, 512]], offset=96 * 8192 + 2048 + i * 512),
            pk(1, 0, 512, 0, 16), AF.Sigmoid, bias=bl2col[:, :], scale=1.0),
            (tokm, t_bl2), 1)
        t_lam.append(toke)
    t_lambf = VV(lambda e: e.tensor_copy(
        lambf[:, :], sap(x1h, [[8192, 16], [1, 2048]], offset=96 * 8192 + 2048)),
        tuple(t_lam), 1) if "lam" not in SKIP else None

    x1h_bf = x1h[:, :].bitcast(BF16)
    hi_bf = x1h_bf[64:65, 6144:7168]
    lo_bf = x1h_bf[64:65, 7168:8192]
    hilo_bf = x1h_bf[64:65, 6144:8192]
    if "bias" in SKIP:
        t_bo_hi = t_bo_lo0 = t_bo_lo = t_b2r = t_b2_hi = t_b2_lo0 = t_b2_lo = None
        tok_bias = None
    t_bo_hi = VV(lambda e: e.tensor_copy(hi_bf, rowtmp), (t_bout,), 1) if "bias" not in SKIP else None
    if "bias" not in SKIP:
        t_bo_lo0 = VV(lambda e: e.tensor_sub(lo_bf, rowtmp, hi_bf), (t_bo_hi,), 1)
        t_bo_lo = dmaY(boutrow(0, 2048), hilo_bf, (t_bo_lo0,), sem="BIAS")
        t_b2r = dmaY(rowtmp, bass.AP(b2_d, 0, [[1024, 1], [1, 1024]]), (t_bo_lo,), sem="BIAS")
        t_b2_hi = VV(lambda e: e.tensor_copy(hi_bf, rowtmp), (t_b2r, t_bo_lo), 1)
        t_b2_lo0 = VV(lambda e: e.tensor_sub(lo_bf, rowtmp, hi_bf), (t_b2_hi,), 1)
        t_b2_lo = dmaY(b2row(0, 2048), hilo_bf, (t_b2_lo0,), sem="BIAS")
        tok_bias = ("BIAS", E.cnt["BIAS"], True)
        t_bo_lo = t_b2_lo = tok_bias

    t_b1m = mm(pk(3, 0, 32), stag_f[0:32, 0:128], identf[0:32, 0:32],
               True, True, (t_b1n, t_idf), 1, new_gen=True, flush=True)
    t_b1 = VV(lambda e: e.tensor_copy(b1col[:, :], pk(3, 0, 32)), (t_b1m,), 1)
    t_wr = dmaY(sap(x1h, [[8192, 8], [1, 1024]], offset=4096), wr_d.ap(),
                (t_b1,), sem="WR")

    t_wrT = []
    prev_wrev = t_b1
    for cc in range(8):
        tokm = mm(pk(3, 0, 8), sap(x1h, [[8192, 8], [1, 128]], offset=4096 + cc * 128),
                  identf[0:8, 0:8],
                  True, True, (t_wr, t_idf, prev_wrev), 1, new_gen=True, flush=True)
        tokc = VV(lambda e, cc=cc: e.tensor_copy(wrT[:, cc * 8:(cc + 1) * 8], pk(3, 0, 8)),
                  (tokm,), 1)
        prev_wrev = tokc
        t_wrT.append(tokc)

    stag2 = nc.alloc_sbuf_tensor_at("stag2", [128, 1024], BF16,
                                    offset=nc.lookup_mloc(aoT).addr)
    stags = [stag, stag2]
    stag_tok = [t_wrT[-1], None]
    tr45_free = [None, None]
    rot_evac = []
    jc_glob = [0]

    def load_transpose(jc_total, src_dram, dst_base, dst_ccstride):
        for jc in range(jc_total):
            sl = jc_glob[0] % 2
            jc_glob[0] += 1
            src = bass.AP(src_dram, jc * 128 * 1024, [[1024, 128], [1, 1024]])
            dst = sap(stags[sl], [[1024, 128], [1, 1024]])
            tdma = dmaG(dst, src, (stag_tok[sl],), sem=f"SG{sl}")
            last_mm = None
            for g in range(2):
                bank = 4 + g
                mts = []
                for b in range(4):
                    cc = g * 4 + b
                    lhsT = sap(stags[sl], [[1024, 128], [1, 128]], offset=cc * 128)
                    deps = (tdma, t_id, tr45_free[g]) if b == 0 else ()
                    tokm = mm(pk(bank, b * 128, 128), lhsT, identbf[:, :],
                              b == 0, b == 3, deps, 1 if b == 3 else 0, new_gen=(b == 0),
                              flush=(b == 3))
                    mts.append(tokm)
                dst_ap = sap(rot, [[32768, 128], [dst_ccstride, 4], [1, 128]],
                             offset=dst_base + (g * 4) * dst_ccstride + jc * 128)
                src_ap = bass.AP(ps[2], (bank % 2) * 512, [[1024, 128], [128, 4], [1, 128]])
                tokev = VV(lambda e, d=dst_ap, s_=src_ap: e.tensor_copy(d, s_),
                           (mts[-1],), 1)
                tr45_free[g] = tokev
                rot_evac.append(tokev)
                last_mm = mts[-1]
            stag_tok[sl] = last_mm

    xft = [None] * 16
    x1tok = [None] * 16

    def xf_load(k, dep):
        slot = k % 3
        dst = sap(xf32, [[3072, 128], [1, 1024]], offset=slot * 1024)
        src = bass.AP(x_d, k * 128 * 1024, [[1024, 128], [1, 1024]])
        xft[k] = dmaY(dst, src, (dep,), sem=("XR0", "XR1", "XB0")[k % 3])

    for k0 in range(3):
        xf_load(k0, None)

    load_transpose(24, wqkv_d, 0, 3072)
    load_transpose(8, wout_d, 24576, 1024)
    rot_ready0 = tuple(rot_evac[-2:])

    t_w1 = []
    for q4 in range(0 if no_gs else 4):
        for cc in range(8):
            dst = sap(w1a, [[32768, 128], [1, 1024]], offset=cc * 4096 + q4 * 1024)
            src = bass.AP(w1_d, cc * 128 * 4096 + q4 * 1024, [[4096, 128], [1, 1024]])
            t_w1.append(dmaG(dst, src, sem="GS"))
    tok_w1 = ("GS", E.cnt["GS"], True)

    t_w2img = []
    for fc in range(0 if no_gs else 32):
        dst = bass.AP(w2img_d, fc * 1024, [[32768, 128], [1, 1024]])
        src = bass.AP(w2_d, fc * 128 * 1024, [[1024, 128], [1, 1024]])
        t_w2img.append(dmaG(dst, src, sem=f"W2I{fc // 4}"))
    t_w2img = [t_w2img[4 * g + 3] for g in range(8)] if t_w2img else []
    t_rotimg = dmaG(rotimg_d.ap(), rot[:, :], tuple(rot_evac), sem="RIMG") \
        if "rimg" not in SKIP else None
    tok_gs = ("GS", E.cnt["GS"], True)
    if no_gs:
        tok_w1 = tok_gs

    x1slot_consumed = [[] for _ in range(8)]
    x1slot_consumed[0] = [t_lambf]
    x1slot_consumed[1] = [t_lambf]
    x1slot_consumed[2] = [t_lambf, t_b2_lo]
    x1slot_consumed[3] = [t_lambf, t_b2_lo]
    x1slot_consumed[4] = [t_wrT[-1]]
    x1slot_ready = [None] * 8
    xT_free = None
    xcast_free = None
    y2_free = [None, None]
    tr_free = [t_h1[-1], t_lam[-1]]
    qk_free = [t_wl2T, t_wrT[-1]]
    wout_free45 = rot_evac[-1]
    wout_free67 = None
    last_sdA = None
    last_tailA = None
    a_cast = {}
    a_epi = {}
    b_cast = {}
    h_free = [None] * 4
    hbank_free = [None] * 4
    y_free = [None, None, None, None]
    ht_free = [None] * NG
    router_free = [None, None]
    av_free = [None, None]
    score_free = [None, None]
    sums_free = [None, None]
    eT_free = [None, None, None, None]
    aoT_free = None
    last_sd = None
    last_r0 = None
    last_te = None
    last_tp = None
    last_A_qkv_tok = None
    last_A_wout_tok = None
    last_B_ymm_tok = [None] * 32
    rot_ready = {0: rot_ready0}

    def ln_stats_a(z_ap, scratch_ap, stt_tok, extra_var_deps=(), st=None):
        nonlocal last_sd
        if st is None:
            st = stz
        tok_mu = VV(lambda e, st=st: e.tensor_scalar_mul(st[:, 1:2], st[:, 0:1],
                                                         1.0 / 1024.0),
                    (stt_tok,), 1, True)
        tok_var = VV(lambda e, z=z_ap, sc=scratch_ap, st=st:
                     e.scalar_tensor_tensor(sc, z, st[:, 1:2], z, ALU.subtract, ALU.mult,
                                            accum_out=st[:, 2:3]),
                     (tok_mu,) + tuple(extra_var_deps), 1, True)
        tok_ve = VV(lambda e, st=st: e.tensor_scalar(st[:, 3:4], st[:, 2:3], 1.0 / 1024.0,
                                                     1e-5, ALU.mult, ALU.add),
                    (tok_var, last_sd), 1, True)
        return tok_ve, tok_var

    def ln_stats_b(tok_ve, st=None):
        nonlocal last_sd, last_r0
        if st is None:
            st = stz
        tok_sd = SS(lambda e, st=st: e.activation(st[:, 4:5], st[:, 3:4], AF.Sqrt,
                                                  bias=0.0, scale=1.0),
                    (tok_ve, last_r0), 1)
        tok_r0 = VV(lambda e, st=st: e.reciprocal(st[:, 5:6], st[:, 4:5]),
                    (tok_sd,), 1, True)
        tok_aa = VV(lambda e, st=st: e.tensor_mul(st[:, 6:7], st[:, 5:6], st[:, 5:6]),
                    (tok_r0,), 1, True)
        tok_aa2 = VV(lambda e, st=st: e.tensor_mul(st[:, 6:7], st[:, 6:7], st[:, 3:4]),
                     (tok_aa,), 1, True)
        tok_aa3 = VV(lambda e, st=st: e.tensor_scalar(st[:, 6:7], st[:, 6:7], -0.5, 1.5,
                                                      ALU.mult, ALU.add),
                     (tok_aa2,), 1, True)
        tok_rs = VV(lambda e, st=st: e.tensor_mul(st[:, 7:8], st[:, 5:6], st[:, 6:7]),
                    (tok_aa3,), 1, True)
        last_sd = tok_sd
        last_r0 = tok_r0
        return tok_rs

    def ln_stats(z_ap, scratch_ap, stt_tok, extra_var_deps=(), st=None):
        tok_ve, tok_var = ln_stats_a(z_ap, scratch_ap, stt_tok, extra_var_deps, st)
        tok_rs = ln_stats_b(tok_ve, st)
        return tok_rs, tok_var

    for half in range(n_halves):
        if half == 1:
            toks = []
            for ch in range(8):
                gate = last_B_ymm_tok[4 * ch + 3]
                dst = sap(rot, [[32768, 128], [1, 4096]], offset=ch * 4096)
                src = bass.AP(rotimg_d, ch * 4096, [[32768, 128], [1, 4096]])
                toks.append(dmaY(dst, src, (gate, t_rotimg), sem="ROTL"))
            rot_ready[1] = tuple(toks)
        rr = rot_ready[half]


        def a_epi_tail(ti):
            nonlocal last_sdA, last_tailA
            ve_both = (a_epi[(ti, 0)][0], a_epi[(ti, 1)][0])
            for sb in range(2):
                tok_ve, x1dst, slot = a_epi.pop((ti, sb))
                tok_sd = SS(lambda e, sb=sb: e.activation(stz[:, 0:1],
                                                          stz[:, 3 + sb:4 + sb],
                                                          AF.Sqrt, bias=0.0, scale=1.0),
                            ve_both + (last_tailA,), 1)
                tok_r0 = VV(lambda e: e.reciprocal(stz[:, 1:2], stz[:, 0:1]),
                            (tok_sd,), 1, True)
                tok_aa = VV(lambda e: e.tensor_mul(stz[:, 2:3], stz[:, 1:2],
                                                   stz[:, 1:2]),
                            (tok_r0,), 1, True)
                tok_aa2 = VV(lambda e, sb=sb: e.tensor_mul(stz[:, 2:3], stz[:, 2:3],
                                                           stz[:, 3 + sb:4 + sb]),
                             (tok_aa,), 1, True)
                tok_aa3 = VV(lambda e: e.tensor_scalar(stz[:, 2:3], stz[:, 2:3],
                                                       -0.5, 1.5, ALU.mult, ALU.add),
                             (tok_aa2,), 1, True)
                tok_rs = VV(lambda e: e.tensor_mul(stz[:, 5:6], stz[:, 1:2],
                                                   stz[:, 2:3]),
                            (tok_aa3,), 1, True)
                tok_x1 = VV(lambda e, x1dst=x1dst:
                            e.tensor_scalar_mul(x1dst, x1dst, stz[:, 5:6]),
                            (tok_rs,), 1, True)
                x1slot_ready[slot] = tok_x1
                last_sdA = tok_sd
                last_tailA = tok_x1

        def mk_acast(ti, sb):
            kk = ti * 2 + sb
            dst = sap(xcast, [[2048, 128], [1, 1024]], offset=sb * 1024)
            src = sap(xf32, [[3072, 128], [1, 1024]], offset=(kk % 3) * 1024)
            return SS(lambda e, d=dst, s_=src: e.copy(d, s_),
                      (xft[kk], xcast_free), 1)

        for tl in range(n_a[half]):
            tile = half * 4 + tl
            n = tile
            t0 = tile * CS

            k0 = tile * 2
            if 3 <= k0 + 2 < 16:
                xf_load(k0 + 2, x1tok[k0 - 1])
            xbf_toks = []
            for s in range(2):
                k = k0 + s
                if (tile, s) in a_cast:
                    xbf_toks.append(a_cast.pop((tile, s)))
                else:
                    xbf_toks.append(mk_acast(tile, s))

            trev = []
            for g in range(4):
                s, cq = divmod(g, 2)
                bk = g % 2
                mts = []
                for b in range(4):
                    cc = cq * 4 + b
                    lhsT = sap(xcast, [[2048, 128], [1, 128]], offset=s * 1024 + cc * 128)
                    deps = ()
                    if b == 0:
                        deps = (xbf_toks[s], tr_free[bk], av_free[bk], t_id,
                                router_free[bk], hbank_free[bk])
                    tokm = mm(pk(bk, b * 128, 128), lhsT, identbf[:, :],
                              b == 0, b == 3, deps, 1 if b == 3 else 0, new_gen=(b == 0))
                    mts.append(tokm)
                dst_ap = sap(xT, [[2048, 128], [256, 4], [1, 128]],
                             offset=cq * 4 * 256 + s * 128)
                src_ap = bass.AP(ps[0], (bk % 2) * 512, [[1024, 128], [128, 4], [1, 128]])
                tokev = SS(lambda e, d=dst_ap, s_=src_ap: e.copy(d, s_),
                           (mts[-1], xT_free), 1)
                trev.append(tokev)
                tr_free[bk] = tokev
            xT_ready = (trev[-1],)
            xcast_free = trev[-1]

            lam_m = [None] * 4
            lam_copy = [None] * 4
            lam_banks = [4, 5, 6, 7]
            for pr in range(4):
                bkk = lam_banks[pr]
                deps0 = [t_sel, t_lambf]
                deps0 += [wout_free45 if bkk in (4, 5) else wout_free67]
                m0 = mm(pk(bkk, 0, 256), selbf[:, (2 * pr) * 128:(2 * pr + 1) * 128],
                        lambf[:, n * 256:(n + 1) * 256], True, True, tuple(deps0),
                        0, new_gen=True)
                m1 = mm(pk(bkk, 256, 256), selbf[:, (2 * pr + 1) * 128:(2 * pr + 2) * 128],
                        lambf[:, n * 256:(n + 1) * 256], True, True, (), 1, new_gen=False)
                lam_m[pr] = m1

            qev = []
            deps_half = list(rr) + [h_free[0], h_free[1]]
            for pi in range(4):
                bk = pi % 4
                grp_last = None
                for sub_j in range(2):
                    jc = pi * 2 + sub_j
                    for cc in range(8):
                        lhsT = sap(rot, [[32768, 128], [1, 128]], offset=cc * 3072 + jc * 128)
                        rhs = sap(xT, [[2048, 128], [1, 256]], offset=cc * 256)
                        first = (cc == 0)
                        deps = ()
                        if first and sub_j == 0:
                            if bk < 2:
                                bdep = (tr_free[bk], av_free[bk])
                            else:
                                bdep = (qk_free[bk - 2],)
                            deps = tuple(deps_half) + bdep + \
                                (hbank_free[bk],) + xT_ready
                        grp_last = mm(pk(bk, sub_j * 256, 256), lhsT, rhs, first, cc == 7,
                                      deps, 1 if (cc == 7 and sub_j == 1) else 0,
                                      new_gen=(first and sub_j == 0))
                tokev = SS(lambda e, pi=pi, bk=bk:
                           e.activation(qT[:, pi * 512:(pi + 1) * 512], pk(bk, 0, 512),
                                        AF.Copy, bias=0.0, scale=cconst[:, 1:2]),
                           (grp_last, y2_free[0], t_c18), 1)
                qev.append(tokev)

            kev = []
            lam_copy[0] = VV(lambda e: e.tensor_copy(lam_sb[:, 0:512],
                                                     pk(lam_banks[0], 0, 512)),
                             (lam_m[0],), 1)

            for pr in range(4):
                bk = pr % 4
                grp_last = None
                for sub_j in range(2):
                    jc = 8 + pr * 2 + sub_j
                    for cc in range(8):
                        lhsT = sap(rot, [[32768, 128], [1, 128]], offset=cc * 3072 + jc * 128)
                        rhs = sap(xT, [[2048, 128], [1, 256]], offset=cc * 256)
                        first = (cc == 0)
                        deps = (qev[pr],) if (first and sub_j == 0) else ()
                        grp_last = mm(pk(bk, sub_j * 256, 256), lhsT, rhs, first, cc == 7,
                                      deps, 1 if (cc == 7 and sub_j == 1) else 0,
                                      new_gen=(first and sub_j == 0))
                tokk = VV(lambda e, pr=pr, bk=bk:
                          e.tensor_mul(kTs[:, pr * 512:(pr + 1) * 512],
                                       pk(bk, 0, 512),
                                       lam_sb[:, 0:512]),
                          (grp_last, lam_copy[pr]), 1)
                kev.append(tokk)
                if pr + 1 < 4:
                    nxt = pr + 1
                    lam_copy[nxt] = VV(lambda e, nxt=nxt, bkk=lam_banks[nxt]:
                                       e.tensor_copy(lam_sb[:, 0:512],
                                                     pk(bkk, 0, 512)),
                                       (lam_m[nxt], tokk), 1)
            sums_free[0] = lam_copy[2]
            sums_free[1] = lam_copy[3]

            vev = []
            for g in range(4):
                s, dvh = divmod(g, 2)
                bk = 2 + (g % 2)
                grp_last = None
                for cc in range(8):
                    lhsT = sap(xT, [[2048, 128], [1, 128]], offset=cc * 256 + s * 128)
                    rhs = sap(rot, [[32768, 128], [1, 512]], offset=cc * 3072 + 2048 + dvh * 512)
                    deps = ()
                    if cc == 0:
                        deps = (kev[2 + (g % 2)],) if g < 2 else (vev[g - 2],)
                    grp_last = mm(pk(bk, 0, 512), lhsT, rhs, cc == 0, cc == 7,
                                  deps, 1 if cc == 7 else 0, new_gen=(cc == 0))
                tokev = SS(lambda e, s=s, dvh=dvh, bk=bk:
                           e.copy(vtok[:, s * 1024 + dvh * 512:s * 1024 + (dvh + 1) * 512],
                                  pk(bk, 0, 512)),
                           (grp_last, y2_free[1]), 1)
                vev.append(tokev)
            last_A_qkv_tok = vev[-1]
            xT_free = vev[-1]
            if tl >= 1:
                a_epi_tail(tile - 1)

            prev_scores = {}
            prev_exp = {}
            prev_sums = {}
            prev_recip = {}
            prev_av = {}
            rb_done = {}
            bulk_toks = []
            ao_toks = []

            def do_scores(h):
                par = h % 2
                bk = 4 + par
                cc_h, ho = divmod(h, 2)
                po = ho * 64
                last = None
                for ks in range(2):
                    lhsT = bass.AP(kTs, po * 2048 + cc_h * 256 + ks * 128,
                                   [[2048, 64], [1, 128]])
                    rhs = bass.AP(qT, po * 2048 + cc_h * 256, [[2048, 64], [1, 256]])
                    deps = ()
                    if ks == 0:
                        deps = (score_free[par], kev[cc_h // 2], qev[cc_h // 2],
                                lam_copy[par])
                    last = mm(pk(bk, ks * 256, 256), lhsT, rhs, True, True,
                              deps, 1 if ks == 1 else 0, new_gen=(ks == 0))
                prev_scores[h] = last

            def do_exp(h):
                par = h % 2
                slot = h % 4
                tok = SS(lambda e, par=par, slot=slot:
                         e.activation(bass.AP(eT, slot * 512, [[2048, 128], [1, 512]]),
                                      pk(4 + par, 0, 512), AF.Exp, bias=cconst[:, 0:1], scale=1.0),
                         (prev_scores[h], eT_free[slot], t_c12), 1)
                prev_exp[h] = tok
                score_free[par] = tok

            def do_sums(h):
                par = h % 2
                slot = h % 4
                last = None
                for ks in range(2):
                    rhs = bass.AP(eT, slot * 512 + ks * 256, [[2048, 128], [1, 256]])
                    deps = ()
                    if ks == 0:
                        deps = (prev_exp[h], sums_free[par], t_ones)
                    last = mm(pk(6 + par, 0, 256, 0, 1), onesbf[:, 0:1], rhs,
                              ks == 0, ks == 1, deps, 1 if ks == 1 else 0,
                              new_gen=(ks == 0))
                prev_sums[h] = last

            def do_recip(h):
                par = h % 2
                c0 = (h % 2) * 512 + ((h // 2) % 2) * 256
                tok = VV(lambda e, par=par, c0=c0:
                         e.reciprocal(rTbf(c0, 256),
                                      pk(6 + par, 0, 256, 0, 1)),
                         (prev_sums[h], rb_done.get(h // 4 - 1)), 1, True)
                prev_recip[h] = tok
                sums_free[par] = tok

            def do_bulk(g):
                bk = 2 + (g % 2)
                bdep = vev[2 + (g % 2)] if g < 2 else qk_free[g % 2]
                mm(pk(bk, 0, 512, 0, 64), onesbf[32:33, 0:64], rTbf(0, 512),
                   True, True, (prev_recip[4 * g + 3], bdep), 0,
                   new_gen=True)
                rbtok = mm(pk(bk, 0, 512, 64, 64), onesbf[32:33, 0:64],
                           rTbf(512, 512), True, True, (), 1, new_gen=True)
                rb_done[g] = rbtok
                src = bass.AP(aoT, g * 512, [[2048, 128], [1, 512]])
                rb = bass.AP(ps[1], (g % 2) * 512, [[1024, 128], [1, 512]])
                tok = VV(lambda e, src=src, rb=rb: e.tensor_mul(src, src, rb),
                         (rbtok, ao_toks[-1]), 1)
                qk_free[g % 2] = tok
                bulk_toks.append(tok)

            def do_av(h):
                par = h % 2
                slot = h % 4
                last = None
                for ks in range(2):
                    lhsT = bass.AP(vtok, ks * 1024 + h * 64, [[2048, 128], [1, 64]])
                    rhs = bass.AP(eT, slot * 512 + ks * 256, [[2048, 128], [1, 256]])
                    deps = ()
                    if ks == 0:
                        deps = (prev_exp[h], av_free[par], kev[par],
                                lam_copy[par], lam_copy[2 + par], vev[3],
                                router_free[par])
                    last = mm(pk(par, 0, 256, 0, 64), lhsT, rhs, ks == 0, ks == 1,
                              deps, 1 if ks == 1 else 0, new_gen=(ks == 0))
                prev_av[h] = last
                eT_free[slot] = last

            def do_ao(h):
                par = h % 2
                cc_h, ho = divmod(h, 2)
                dst = bass.AP(aoT, ho * 64 * 2048 + cc_h * 256, [[2048, 64], [1, 256]])
                tok = SS(lambda e, dst=dst, par=par:
                         e.copy(dst, pk(par, 0, 256, 0, 64)),
                         (prev_av[h], aoT_free), 1)
                av_free[par] = tok
                return tok

            if attn_serial:
                for h_ in range(H):
                    do_scores(h_)
                    mm_flush()
                    do_exp(h_)
                    do_sums(h_)
                    mm_flush()
                    do_recip(h_)
                    do_av(h_)
                    mm_flush()
                    ao_toks.append(do_ao(h_))
                    if h_ % 4 == 3:
                        do_bulk(h_ // 4)
                        mm_flush()
            else:
                for step in range(H + 2):
                    if step < H:
                        do_scores(step)
                        do_exp(step)
                    if 2 <= step < H + 2:
                        do_av(step - 2)
                        ao_toks.append(do_ao(step - 2))
                        if (step - 2) % 4 == 3:
                            do_bulk((step - 2) // 4)
                    if 1 <= step < H + 1:
                        do_sums(step - 1)
                        do_recip(step - 1)
            mm_flush()
            if tl + 1 < n_a[half]:
                a_cast[(tile + 1, 0)] = mk_acast(tile + 1, 0)

            for s in range(2):
                k = tile * 2 + s
                grp = []
                for jh in range(2):
                    bk = 4 + 2 * s + jh
                    last = None
                    for cc in range(8):
                        lhsT = sap(aoT, [[2048, 128], [1, 128]], offset=cc * 256 + s * 128)
                        rhs = sap(rot, [[32768, 128], [1, 512]],
                                  offset=24576 + cc * 1024 + jh * 512)
                        deps = ()
                        if cc == 0:
                            deps = ((score_free[jh], bulk_toks[-1]) if s == 0 else
                                    (sums_free[0], sums_free[1], bulk_toks[-1]))
                        last = mm(pk(bk, 0, 512), lhsT, rhs, cc == 0, False, deps, 0,
                                  new_gen=(cc == 0))
                    mm(pk(bk, 0, 512), onesbf[32:33, 0:128],
                       boutrow(jh * 512, 512), False, False,
                       (t_bo_lo, t_ones), 0, new_gen=False)
                    last = mm(pk(bk, 0, 512), onesbf[32:33, 0:128],
                              boutrow(1024 + jh * 512, 512),
                              False, True, (), 1, new_gen=False)
                    grp.append(last)
                mm_flush()
                last_A_wout_tok = grp[-1]
                pw = bass.AP(ps[2 + s], 0, [[1024, 128], [1, 1024]])
                z1 = sap(xf32, [[3072, 128], [1, 1024]], offset=(k % 3) * 1024)
                tok_z1 = VV(lambda e, pw=pw, z1=z1:
                            e.scalar_tensor_tensor(z1, pw, 0.0, z1, ALU.add, ALU.add,
                                                   accum_out=stz[:, 0:1]),
                            tuple(grp) + (xft[k], xbf_toks[s], last_sdA), 1, True)
                if s == 0:
                    wout_free45 = tok_z1
                else:
                    wout_free67 = tok_z1
                slot = tl * 2 + s
                tok_mu = VV(lambda e:
                            e.tensor_scalar_mul(stz[:, 1:2], stz[:, 0:1], 1.0 / 1024.0),
                            (tok_z1,), 1, True)
                x1dst = sap(x1h, [[8192, 128], [1, 1024]], offset=slot * 1024)
                tok_p1 = VV(lambda e, z1=z1, x1dst=x1dst:
                            e.tensor_scalar(x1dst, z1, stz[:, 1:2], None,
                                            ALU.subtract),
                            (tok_mu,) + tuple(x1slot_consumed[slot]), 1, True)
                scratch = eT[:, :].bitcast(F32)
                tok_var = VV(lambda e, sc=scratch, x1dst=x1dst:
                             e.scalar_tensor_tensor(sc, x1dst, 0.0, x1dst,
                                                    ALU.add, ALU.mult,
                                                    accum_out=stz[:, 2:3]),
                             (tok_p1, eT_free[0], eT_free[1],
                              eT_free[2], eT_free[3]), 1, True)
                tok_ve = VV(lambda e, s=s:
                            e.tensor_scalar(stz[:, 3 + s:4 + s], stz[:, 2:3],
                                            1.0 / 1024.0, 1e-5, ALU.mult, ALU.add),
                            (tok_var, last_tailA), 1, True)
                a_epi[(tile, s)] = (tok_ve, x1dst, slot)
                x1tok[k] = tok_p1
                if s == 0 and k + 3 < 16:
                    xf_load(k + 3, tok_p1)
                    if tl + 1 < n_a[half]:
                        a_cast[(tile + 1, 1)] = mk_acast(tile + 1, 1)
                for i_ in range(4):
                    eT_free[i_] = tok_var
        if n_b[half]:
            src2 = sap(x1h, [[8192, 128], [1, 2048]])
            b_cast[0] = SS(lambda e, s_=src2: e.copy(xcast[:, :], s_),
                           (x1slot_ready[0], x1slot_ready[1], xcast_free), 1)
            xcast_free = b_cast[0]
        if n_a[half]:
            a_epi_tail(half * 4 + n_a[half] - 1)
        for i in range(NG):
            ht_free[i] = wout_free67

        if dump == "x1h" and half == 0:
            E.op("Y", lambda e: e.dma_start(dbg_d.ap(), x1h[:, :]),
                 tuple(x1slot_ready), 16, "O", True)
        if dump == "ln" and half == 0:
            gates = (x1slot_ready[0], x1slot_ready[1])
            E.op("Y", lambda e: e.dma_start(bass.AP(dbg_d, 0, [[8192, 128], [1, 1024]]),
                                            xf32[:, 0:1024]), gates, 16, "O", True)
            E.op("Y", lambda e: e.dma_start(bass.AP(dbg_d, 2048, [[8192, 128], [1, 2048]]),
                                            x1h[:, 0:2048]), (), 16, "O", True)
            E.op("Y", lambda e: e.dma_start(
                bass.AP(dbg_d, 4096, [[8192, 128], [1, 8]]), stz[:, :],
                allow_slow_non_contiguous=True),
                (), 16, "O", True)
        if dump == "all" and half == 0:
            g0 = tuple(x1slot_ready[0:2])
            E.op("Y", lambda e: e.dma_start(dball["d_xT"].ap(), xT[:, :]), g0, 16, "O", True)
            E.op("Y", lambda e: e.dma_start(dball["d_qT"].ap(), qT[:, :]), (), 16, "O", True)
            E.op("Y", lambda e: e.dma_start(dball["d_kTs"].ap(), kTs[:, :]), (), 16, "O", True)
            E.op("Y", lambda e: e.dma_start(dball["d_vtok"].ap(), vtok[:, :]), (), 16, "O", True)
            E.op("Y", lambda e: e.dma_start(dball["d_eT"].ap(), eT[:, :]), (), 16, "O", True)
            E.op("Y", lambda e: e.dma_start(dball["d_aoT"].ap(), aoT[:, :]), (), 16, "O", True)
            E.op("Y", lambda e: e.dma_start(dball["d_lam"].ap(), lam_sb[:, :]), (), 16, "O", True)
            E.op("Y", lambda e: e.dma_start(dball["d_rot"].ap(), rot[:, :]), (), 16, "O", True)
            E.op("Y", lambda e: e.dma_start(dball["d_z1"].ap(), xf32[:, 0:1024]), (), 16, "O", True)
            E.op("Y", lambda e: e.dma_start(bass.AP(dball["d_x1h"], 0, [[8192, 128], [1, 2048]]),
                                            sap(x1h, [[8192, 128], [1, 2048]])), (), 16, "O", True)
            E.op("Y", lambda e: e.dma_start(
                bass.AP(dball["d_st"], 0, [[16, 128], [1, 1]]),
                sap(stz, [[8, 128], [1, 1]]), allow_slow_non_contiguous=True), (), 16, "O", True)
        if dump == "rot0" and half == 0:
            dbg_bf = dbg_d.ap().bitcast(BF16)
            E.op("Y", lambda e: e.dma_start(dbg_bf[:, 0:12288],
                                            sap(rot, [[32768, 128], [1, 12288]])),
                 rot_ready0, 16, "O", True)
            E.op("Y", lambda e: e.dma_start(dbg_bf[:, 12288:12416], identbf[:, :]),
                 (), 16, "O", True)
            E.op("Y", lambda e: e.dma_start(dbg_bf[:, 12416:13440], stag[:, :]),
                 (), 16, "O", True)
        if dump == "rot" and half == 0:
            gates = (x1slot_ready[0], x1slot_ready[1])
            for cc_ in range(8):
                E.op("G", lambda e, cc_=cc_: e.dma_start(
                    bass.AP(dbg_d, cc_ * 1024, [[8192, 128], [1, 1024]]),
                    sap(rot, [[32768, 128], [1, 1024]], offset=cc_ * 3072)),
                    gates, 16, "O", True)
        if dump == "mid" and half == 0:
            dd = dbg_d.ap()
            gates = tuple(t for t in x1slot_ready if t) + (t_lambf,)
            E.op("G", lambda e: e.dma_start(bass.AP(dbg_d, 0, [[8192, 128], [1, 2048]]), xT[:, :]),
                 gates, 16, "O", True)
            E.op("G", lambda e: e.dma_start(bass.AP(dbg_d, 2048, [[8192, 128], [1, 2048]]), qT[:, :]),
                 (), 16, "O", True)
            E.op("G", lambda e: e.dma_start(bass.AP(dbg_d, 4096, [[8192, 128], [1, 2048]]), kTs[:, :]),
                 (), 16, "O", True)
            E.op("G", lambda e: e.dma_start(bass.AP(dbg_d, 6144, [[8192, 128], [1, 2048]]),
                                            aoT[:, :]),
                 (), 16, "O", True)

        w2_chunks = []
        for ch in range(8 if n_b[half] else 0):
            gate = last_A_qkv_tok if ch < 6 else last_A_wout_tok
            dst = sap(rot, [[32768, 128], [1, 4096]], offset=ch * 4096)
            src = bass.AP(w2img_d, ch * 4096, [[32768, 128], [1, 4096]])
            gimg = t_w2img[ch] if t_w2img else tok_gs
            deps = (gate, gimg, t_rotimg)
            w2_chunks.append(dmaY(dst, src, deps, sem=f"W2C{ch}"))


        def mk_bcast(tl2):
            sl2 = tl2 * 2
            src2 = sap(x1h, [[8192, 128], [1, 2048]], offset=sl2 * 1024)
            b_cast[tl2] = SS(lambda e, s_=src2: e.copy(xcast[:, :], s_),
                             (x1slot_ready[sl2], x1slot_ready[sl2 + 1],
                              xcast_free), 1)

        def b_head1(tl):
            nonlocal xcast_free, xT_ready, last_te, last_tp
            tile = half * 4 + tl
            if tl not in b_cast:
                mk_bcast(tl)
            tok_cast = b_cast[tl]
            xcast_free = tok_cast

            trev = []
            for g in range(4):
                s, cq = divmod(g, 2)
                bk = g % 2
                mts = []
                for b in range(4):
                    cc = cq * 4 + b
                    lhsT = sap(xcast, [[2048, 128], [1, 128]], offset=s * 1024 + cc * 128)
                    deps = ()
                    if b == 0:
                        deps = (tok_cast, tr_free[bk], av_free[bk], router_free[bk],
                                hbank_free[bk])
                    tokm = mm(pk(bk, b * 128, 128), lhsT, identbf[:, :],
                              b == 0, b == 3, deps, 1 if b == 3 else 0, new_gen=(b == 0))
                    mts.append(tokm)
                dst_ap = sap(xT, [[2048, 128], [256, 4], [1, 128]],
                             offset=cq * 4 * 256 + s * 128)
                src_ap = bass.AP(ps[0], (bk % 2) * 512, [[1024, 128], [128, 4], [1, 128]])
                tokev = SS(lambda e, d=dst_ap, s_=src_ap: e.copy(d, s_),
                           (mts[-1], xT_free), 1)
                trev.append(tokev)
                tr_free[bk] = tokev
            mm_flush()
            xT_ready = (trev[-1],)
            xcast_free = trev[-1]

            p7_toks = []
            rtm_toks = []
            for s in range(2):
                bk = s
                last = None
                for cc in range(8):
                    lhsT = sap(xT, [[2048, 128], [1, 128]], offset=cc * 256 + s * 128)
                    rhs = sap(wrT, [[64, 128], [1, 8]], offset=cc * 8)
                    deps = ()
                    if cc == 0:
                        deps = (tr_free[bk], t_wrT[-1]) + xT_ready
                    last = mm(pk(bk, 0, 8), lhsT, rhs, cc == 0, cc == 7, deps,
                              1 if cc == 7 else 0, new_gen=(cc == 0))
                tok_m = VV(lambda e, bk=bk:
                           e.tensor_reduce(rmax[:, :], pk(bk, 0, 8), AX.X, ALU.max,
                                           negate=True),
                           (last, last_te), 1, True)
                rtm_toks.append(tok_m)
                tok_e = SS(lambda e, bk=bk:
                           e.activation(er[:, :], pk(bk, 0, 8), AF.Exp,
                                        bias=rmax[:, :], scale=1.0),
                           (tok_m, last_tp), 1)
                router_free[bk] = tok_e
                last_te = tok_e
                tok_s = VV(lambda e: e.reduce_sum(rsum[:, :], er[:, :], AX.X),
                           (tok_e,), 1, True)
                tok_r = VV(lambda e: e.reciprocal(rrec[:, :], rsum[:, :]), (tok_s,), 1, True)
                pcol = (tl % 2) * 2 + s
                tok_p = VV(lambda e, pcol=pcol: e.tensor_mul(p7[:, pcol:pcol + 1],
                                                             er[:, 7:8], rrec[:, :]),
                           (tok_r,), 1, True)
                last_tp = tok_p
                p7_toks.append(tok_p)
            epi_data[tl] = {"p7": p7_toks, "rtm": rtm_toks}

        def b_head2(tl):
            nonlocal xT_free, aoT_free
            gelu_toks = [None] * 32
            for fcs in range(36):
                if fcs < 32:
                    fc = fcs
                    bk = fc % 4
                    last = None
                    for cc in range(8):
                        lhsT = sap(w1a, [[32768, 128], [1, 128]], offset=cc * 4096 + fc * 128)
                        rhs = sap(xT, [[2048, 128], [1, 256]], offset=cc * 256)
                        deps = ()
                        if cc == 0:
                            deps = (h_free[fc % 4], qk_free[0], qk_free[1],
                                    tok_w1) + xT_ready
                            if fc < 2:
                                deps = deps + (tr_free[fc], router_free[fc],
                                               epi_data[tl]["rtm"][fc])
                        last = mm(pk(bk, 0, 256), lhsT, rhs, cc == 0, cc == 7, deps,
                                  1 if cc == 7 else 0, new_gen=(cc == 0))
                    slot = fc % NG
                    gelu_toks[fc] = SS(lambda e, slot=slot, fc=fc, bk=bk:
                               e.activation(hT[:, slot * 256:(slot + 1) * 256],
                                            pk(bk, 0, 256), AF.Gelu,
                                            bias=b1col[:, fc:fc + 1], scale=1.0),
                               (last, ht_free[slot], t_b1), 1)
                    h_free[fc % 4] = gelu_toks[fc]
                    hbank_free[fc % 4] = gelu_toks[fc]
                    if fc == 31:
                        xT_free = last
                if fcs >= 4:
                    fc = fcs - 4
                    slot = fc % NG
                    w2ch = w2_chunks[fc // 4]
                    ylast = None
                    for s in range(2):
                        for jh in range(2):
                            ybk = 4 + 2 * s + jh
                            lhsT = sap(hT, [[2048, 128], [1, 128]], offset=slot * 256 + s * 128)
                            rhs = sap(rot, [[32768, 128], [1, 512]], offset=fc * 1024 + jh * 512)
                            deps = [gelu_toks[fc], w2ch]
                            if fc == 0:
                                deps += [y_free[2 * s + jh], score_free[jh],
                                         sums_free[jh], wout_free45, wout_free67]
                            is_last = (s == 1 and jh == 1)
                            ylast = mm(pk(ybk, 0, 512), lhsT, rhs, fc == 0, False,
                                       tuple(deps), 1 if is_last else 0, new_gen=(fc == 0))
                    ht_free[slot] = ylast
                    last_B_ymm_tok[fc] = ylast
            mm_flush()
            aoT_free = last_B_ymm_tok[31]
            if tl + 1 < n_b[half]:
                mk_bcast(tl + 1)

            for s in range(2):
                grp = []
                for jh in range(2):
                    ybk = 4 + 2 * s + jh
                    mm(pk(ybk, 0, 512), onesbf[64:65, 0:128],
                       b2row(jh * 512, 512), False, False,
                       (t_b2_lo, t_ones), 0, new_gen=False)
                    tok = mm(pk(ybk, 0, 512), onesbf[64:65, 0:128],
                             b2row(1024 + jh * 512, 512),
                             False, True, (), 1, new_gen=False)
                    grp.append(tok)
                mm_flush()
                epi_data[tl][f"grp{s}"] = grp

        def b_epi1(tl):
            nonlocal wout_free45, wout_free67
            dat = epi_data[tl]
            for s in range(2):
                grp = dat[f"grp{s}"]
                slot = tl * 2 + s
                buf = slot % 2
                pcol = (tl % 2) * 2 + s
                py = bass.AP(ps[2 + s], 0, [[1024, 128], [1, 1024]])
                x1src = sap(x1h, [[8192, 128], [1, 1024]], offset=slot * 1024)
                y2ap = (qT if buf == 0 else vtok)[:, :].bitcast(F32)
                stb = stzb[(tl % 2) * 2 + s]
                tok_z2 = VV(lambda e, py=py, x1src=x1src, y2ap=y2ap, pcol=pcol, stb=stb:
                            e.scalar_tensor_tensor(y2ap, py, p7[:, pcol:pcol + 1], x1src,
                                                   ALU.mult, ALU.add,
                                                   accum_out=stb[:, 0:1]),
                            tuple(grp) + (dat["p7"][s], y2_free[buf],
                                          x1slot_ready[slot]),
                            1, True)
                dat[f"z2_{s}"] = tok_z2
                x1slot_consumed[slot] = [tok_z2]
                y_free[2 * s] = tok_z2
                y_free[2 * s + 1] = tok_z2
                if s == 0:
                    wout_free45 = tok_z2
                else:
                    wout_free67 = tok_z2

        def b_epi2a(tl):
            dat = epi_data[tl]
            for s in range(2):
                buf = (tl * 2 + s) % 2
                y2ap = (qT if buf == 0 else vtok)[:, :].bitcast(F32)
                scratch = kTs[:, :].bitcast(F32)
                tok_ve, _ = ln_stats_a(y2ap, scratch, dat[f"z2_{s}"],
                                       st=stzb[(tl % 2) * 2 + s])
                dat[f"ve_{s}"] = tok_ve

        def b_epi2b(tl):
            tile = half * 4 + tl
            t0 = tile * CS
            final = (half == n_halves - 1 and tl == n_b[half] - 1)
            dat = epi_data[tl]
            for s in range(2):
                buf = (tl * 2 + s) % 2
                y2ap = (qT if buf == 0 else vtok)[:, :].bitcast(F32)
                stb = stzb[(tl % 2) * 2 + s]
                tok_rs = ln_stats_b(dat[f"ve_{s}"], st=stb)
                tok_o = VV(lambda e, y2ap=y2ap, stb=stb:
                           e.tensor_scalar(y2ap, y2ap, stb[:, 1:2],
                                           stb[:, 7:8],
                                           ALU.subtract, ALU.mult),
                           (tok_rs,), 1)
                dst = bass.AP(out_d, (t0 + s * 128) * 1024, [[1024, 128], [1, 1024]])
                sem = "O" if buf == 0 else "O2"
                if final:
                    tok_dma = E.op("Y", lambda e, d=dst, s_=y2ap: e.dma_start(d, s_),
                                   (tok_o,), 16, sem, True)
                else:
                    tok_dma = dmaO(dst, y2ap, (tok_o,), sem=sem)
                y2_free[buf] = tok_dma

        epi_data = {}
        for tl in range(n_b[half]):
            b_head1(tl)
            if tl >= 2:
                b_epi2b(tl - 2)
            if tl > 0:
                b_epi1(tl - 1)
                b_epi2a(tl - 1)
            b_head2(tl)
        if n_b[half]:
            L = n_b[half] - 1
            if L >= 1:
                b_epi2b(L - 1)
            b_epi1(L)
            b_epi2a(L)
            if half + 1 < n_halves:
                a_cast[(4, 0)] = mk_acast(4, 0)
                a_cast[(4, 1)] = mk_acast(4, 1)
            b_epi2b(L)

    E.op("Y", lambda e: e.nop(),
         (("O", E.cnt["O"], True), ("O2", E.cnt["O2"], True)), 0)

    with nc.Block() as block, nc.allow_low_precision(
            reason="bf16 matmul operands by design; fp32 accumulation stays in PSUM"):
        @block.tensor
        def _(eng):
            E.replay("T", eng, sems)

        @block.vector
        def _(eng):
            E.replay("V", eng, sems)

        @block.scalar
        def _(eng):
            E.replay("S", eng, sems)

        @block.gpsimd
        def _(eng):
            E.replay("G", eng, sems)

        @block.sync
        def _(eng):
            E.replay("Y", eng, sems)

    return nc


_NC_CACHE = None


def kernel(L0_attn_x, L0_attn_Wqkv, L0_attn_Wout, L0_attn_bout, L0_attn_Wl1,
           L0_attn_bl1, L0_attn_Wl2, L0_attn_bl2, L0_ffn_Wr, L0_ffn_W1, L0_ffn_b1,
           L0_ffn_W2, L0_ffn_b2):
    nc = build_nc()
    x = np.ascontiguousarray(np.asarray(L0_attn_x, dtype=np.float32))
    common = {
        "wqkv": np.ascontiguousarray(np.asarray(L0_attn_Wqkv, np.float32)),
        "wout": np.ascontiguousarray(np.asarray(L0_attn_Wout, np.float32)),
        "bout": np.ascontiguousarray(np.asarray(L0_attn_bout, np.float32)),
        "wl1": np.ascontiguousarray(np.asarray(L0_attn_Wl1, np.float32)),
        "bl1": np.ascontiguousarray(np.asarray(L0_attn_bl1, np.float32)),
        "wl2": np.ascontiguousarray(np.asarray(L0_attn_Wl2, np.float32)),
        "bl2": np.ascontiguousarray(np.asarray(L0_attn_bl2, np.float32)),
        "wr": np.ascontiguousarray(np.asarray(L0_ffn_Wr, np.float32)),
        "w1": np.ascontiguousarray(np.asarray(L0_ffn_W1, np.float32)[NE - 1]),
        "b1": np.ascontiguousarray(np.asarray(L0_ffn_b1, np.float32)[NE - 1]),
        "w2": np.ascontiguousarray(np.asarray(L0_ffn_W2, np.float32)[NE - 1]),
        "b2": np.ascontiguousarray(np.asarray(L0_ffn_b2, np.float32)[NE - 1]),
    }
    in_maps = [dict(common, x=np.ascontiguousarray(x[b])) for b in range(8)]
    res = run_bass_kernel_spmd(nc, in_maps, core_ids=list(range(8)))
    out = np.stack([np.asarray(res.results[b]["out"], dtype=np.float32)
                    for b in range(8)], axis=0)
    return out


# version 18
# speedup vs baseline: 1.0073x; 1.0073x over previous
import numpy as np
import ml_dtypes
import concourse.bass as bass
import concourse.mybir as mybir
from concourse.bass_utils import run_bass_kernel_spmd

F32 = mybir.dt.float32
BF16 = mybir.dt.bfloat16
AF = mybir.ActivationFunctionType
ALU = mybir.AluOpType
AX = mybir.AxisListType

T = 2048
C = 1024
H = 16
CS = 256
FF = 4096
NE = 8
NG = 8

SEMS = ("T", "V", "S", "G", "Y", "O", "CONST", "BIAS", "WR", "RIMG", "ROTL",
        "GS", "XR0", "XR1", "XB0", "XB1", "SG0", "SG1",
        "W2C0", "W2C1", "W2C2", "W2C3", "W2C4", "W2C5", "W2C6", "W2C7",
        "W2I0", "W2I1", "W2I2", "W2I3", "W2I4", "W2I5", "W2I6", "W2I7", "O2",
        "OF0", "OF1")


class Em:
    def __init__(self):
        self.q = {e: [] for e in ("T", "V", "S", "G", "Y")}
        self.cnt = {s: 0 for s in SEMS}
        self.synced = {e: {s: 0 for s in SEMS} for e in self.q}

    def op(self, eng, fn, deps=(), inc=0, inc_sem=None, hazard=False):
        need = {}
        for d in deps:
            if d is None:
                continue
            sem, val, dhaz = d
            if sem == eng and not dhaz:
                continue
            if val > need.get(sem, -1):
                need[sem] = val
        waits = []
        for sem, val in need.items():
            if self.synced[eng][sem] >= val:
                continue
            waits.append((sem, val))
            self.synced[eng][sem] = val
        if inc:
            s = inc_sem or eng
            self.cnt[s] += inc
            self.q[eng].append((waits, fn, s, inc))
            return (s, self.cnt[s], hazard)
        self.q[eng].append((waits, fn, None, 0))
        return None

    def replay(self, eng_name, eng, sems):
        for waits, fn, incsem, amt in self.q[eng_name]:
            for sem, val in waits:
                eng.wait_ge(sems[sem], val)
            inst = fn(eng)
            if incsem is not None:
                inst.then_inc(sems[incsem], amt)


def sap(t, dims, offset=0):
    return bass.AP(t, offset, [list(d) for d in dims])


def build_nc(n_halves=2, n_a=(4, 4), n_b=(4, 4), dump=None, attn_serial=False, no_gs=False, hwstag=False, skips=()):
    if isinstance(n_a, int):
        n_a = (n_a, n_a)
    if isinstance(n_b, int):
        n_b = (n_b, n_b)
    nc = bass.Bass("TRN2", target_bir_lowering=False,
                   dynamic_dma_scratch_size=16384)

    x_d = nc.dram_tensor("x", [T, C], F32, kind="ExternalInput")
    wqkv_d = nc.dram_tensor("wqkv", [3 * C, C], F32, kind="ExternalInput")
    wout_d = nc.dram_tensor("wout", [C, C], F32, kind="ExternalInput")
    bout_d = nc.dram_tensor("bout", [C], F32, kind="ExternalInput")
    wl1_d = nc.dram_tensor("wl1", [64, 1], F32, kind="ExternalInput")
    bl1_d = nc.dram_tensor("bl1", [64], F32, kind="ExternalInput")
    wl2_d = nc.dram_tensor("wl2", [H, 64], F32, kind="ExternalInput")
    bl2_d = nc.dram_tensor("bl2", [H], F32, kind="ExternalInput")
    wr_d = nc.dram_tensor("wr", [NE, C], F32, kind="ExternalInput")
    w1_d = nc.dram_tensor("w1", [C, FF], F32, kind="ExternalInput")
    b1_d = nc.dram_tensor("b1", [FF], F32, kind="ExternalInput")
    w2_d = nc.dram_tensor("w2", [FF, C], F32, kind="ExternalInput")
    b2_d = nc.dram_tensor("b2", [C], F32, kind="ExternalInput")
    out_d = nc.dram_tensor("out", [T, C], F32, kind="ExternalOutput")
    dbg_d = nc.dram_tensor("dbg", [128, 8192], F32, kind="ExternalOutput") if dump else None
    dball = {}
    if dump == "all":
        for nm, dt_ in (("d_xT", BF16), ("d_qT", BF16), ("d_kTs", BF16), ("d_vtok", BF16),
                        ("d_eT", BF16), ("d_aoT", BF16), ("d_lam", BF16), ("d_rot", BF16),
                        ("d_z1", F32), ("d_x1h", F32), ("d_st", F32)):
            shape = [128, 32768] if nm == "d_rot" else ([128, 2048] if dt_ == BF16 else [128, 2048])
            if nm == "d_lam":
                shape = [128, 512]
            if nm == "d_z1":
                shape = [128, 1024]
            if nm == "d_st":
                shape = [128, 16]
            if nm == "d_x1h":
                shape = [128, 8192]
            dball[nm] = nc.dram_tensor(nm, shape, dt_, kind="ExternalOutput")

    rotimg_d = nc.dram_tensor("rotimg", [128, 32768], BF16, kind="Internal")
    w2img_d = nc.dram_tensor("w2img", [128, 32768], BF16, kind="Internal")

    ident_bf_d = nc.inline_tensor(np.eye(128, dtype=ml_dtypes.bfloat16), name="cidbf")
    ident_f_d = nc.inline_tensor(np.eye(32, dtype=np.float32), name="cidf")
    ones_bf_d = nc.inline_tensor(np.ones((128, 128), dtype=ml_dtypes.bfloat16), name="conesbf")
    sel_np = np.zeros((H, 8 * 128), dtype=ml_dtypes.bfloat16)
    for cc in range(8):
        for p in range(128):
            sel_np[2 * cc + p // 64, cc * 128 + p] = 1.0
    sel_d = nc.inline_tensor(sel_np, name="cselbf")

    A = nc.alloc_sbuf_tensor
    rot = A("rot", [128, 32768], BF16)
    w1a = A("w1a", [128, 32768], BF16)
    x1h = A("x1h", [128, 8192], F32)
    xcast = A("xcast", [128, 2048], BF16)
    xT = A("xT", [128, 2048], BF16)
    xf32 = A("xf32", [128, 3072], F32)
    qT = A("qT", [128, 2048], BF16)
    kTs = A("kTs", [128, 2048], BF16)
    vtok = A("vtok", [128, 2048], BF16)
    eT = A("eT", [128, 2048], BF16)
    aoT = A("aoT", [128, 2048], BF16)
    lam_sb = A("lam_sb", [128, 512], BF16)
    stag = nc.alloc_sbuf_tensor_at("stag", [128, 1024], BF16,
                                   offset=nc.lookup_mloc(eT).addr)
    lambf = A("lambf", [16, 2048], BF16)
    identbf = A("identbf", [128, 128], BF16)
    identf = nc.alloc_sbuf_tensor_at("identf", [32, 32], F32,
                                     offset=nc.lookup_mloc(eT).addr + 2624)
    onesbf = A("onesbf", [128, 128], BF16)
    selbf = A("selbf", [16, 1024], BF16)

    b1col = A("b1col", [128, 32], F32)
    bl1col = nc.alloc_sbuf_tensor_at("bl1col", [64, 1], F32,
                                     offset=nc.lookup_mloc(eT).addr + 2752)
    bl2col = nc.alloc_sbuf_tensor_at("bl2col", [16, 1], F32,
                                     offset=nc.lookup_mloc(eT).addr + 2784)
    stz = A("stz", [128, 8], F32)
    wl1row = nc.alloc_sbuf_tensor_at("wl1row", [1, 64], F32,
                                     offset=nc.lookup_mloc(eT).addr + 2048)
    wl2nat = nc.alloc_sbuf_tensor_at("wl2nat", [16, 64], F32,
                                     offset=nc.lookup_mloc(eT).addr + 2304)
    wl2T = nc.alloc_sbuf_tensor_at("wl2T", [64, 16], F32,
                                   offset=nc.lookup_mloc(eT).addr + 2560)
    wrT = A("wrT", [128, 64], BF16)
    _et2 = nc.lookup_mloc(eT).addr
    er = nc.alloc_sbuf_tensor_at("er", [128, 8], F32, offset=_et2 + 2816)
    p7 = nc.alloc_sbuf_tensor_at("p7", [128, 4], F32, offset=_et2 + 2848)
    rmax = nc.alloc_sbuf_tensor_at("rmax", [128, 1], F32, offset=_et2 + 2880)
    rsum = nc.alloc_sbuf_tensor_at("rsum", [128, 1], F32, offset=_et2 + 2912)
    rrec = nc.alloc_sbuf_tensor_at("rrec", [128, 1], F32, offset=_et2 + 2944)
    stzb = [nc.alloc_sbuf_tensor_at(f"stzb{i}", [128, 8], F32,
                                    offset=_et2 + 2976 + 32 * i) for i in range(4)]
    cconst = A("cconst", [128, 2], F32)
    hT = nc.alloc_sbuf_tensor_at("hT", [128, 2048], BF16,
                                 offset=nc.lookup_mloc(aoT).addr)
    _lam_addr = nc.lookup_mloc(lambf).addr
    boutrow_t = nc.alloc_sbuf_tensor_at("boutrow", [33, 2048], BF16, offset=_lam_addr)
    b2row_t = nc.alloc_sbuf_tensor_at("b2row", [65, 2048], BF16, offset=_lam_addr)
    _sel_addr = nc.lookup_mloc(selbf).addr
    rTbf_t = nc.alloc_sbuf_tensor_at("rTbf", [33, 1024], BF16, offset=_sel_addr)
    boutrow = lambda c0, n: sap(boutrow_t, [[2048, 1], [1, n]], offset=32 * 2048 + c0)
    b2row = lambda c0, n: sap(b2row_t, [[2048, 1], [1, n]], offset=64 * 2048 + c0)
    rTbf = lambda c0, n: sap(rTbf_t, [[1024, 1], [1, n]], offset=32 * 1024 + c0)

    ps = [nc.alloc_psum_tensor(f"ps{i}", [128, 1024], F32) for i in range(4)]

    def pk(bank, c0, n, p0=0, np_=128):
        return bass.AP(ps[bank // 2], p0 * 1024 + (bank % 2) * 512 + c0,
                       [[1024, np_], [1, n]])

    sems = {s: nc.alloc_semaphore(name=f"sem{s}") for s in SEMS}
    E = Em()

    def TT_(fn, deps=(), inc=0, hz=False):
        return E.op("T", fn, deps, inc, None, hz)

    def VV(fn, deps=(), inc=0, hz=True):
        return E.op("V", fn, deps, inc, None, True)

    def SS(fn, deps=(), inc=0, hz=True):
        return E.op("S", fn, deps, inc, None, True)

    def GG(fn, deps=(), inc=0, hz=True):
        return E.op("G", fn, deps, inc, None, True)

    def dmaG(dst, src, deps=(), sem="G"):
        return E.op("G", lambda e, d=dst, s_=src: e.dma_start(d, s_), deps, 16, sem, True)

    def dmaY(dst, src, deps=(), sem="Y"):
        return E.op("Y", lambda e, d=dst, s_=src: e.dma_start(d, s_), deps, 16, sem, True)

    def dmaO(dst, src, deps=(), sem="O"):
        return E.op("G", lambda e, d=dst, s_=src: e.dma_start(d, s_), deps, 16, sem, True)

    pendT = {"amt": 0}

    def mm(out_ap, lhsT, rhs, first, last, deps=(), inc=0, new_gen=False,
           flush=False):
        st = bool(first and new_gen)

        def fn(e, o=out_ap, l=lhsT, r=rhs, s_=st, la=last):
            return e.matmul(o, l, r, start=s_, stop=la, skip_group_check=True)
        tok = None
        attach = 0
        if inc:
            attach = pendT["amt"]
            pendT["amt"] = inc
            E.cnt["T"] += inc
            tok = ("T", E.cnt["T"], False)
        need = {}
        for d in deps:
            if d is None:
                continue
            sem, val, dhaz = d
            if sem == "T" and not dhaz:
                continue
            if val > need.get(sem, -1):
                need[sem] = val
        waits = []
        for sem, val in need.items():
            if E.synced["T"][sem] >= val:
                continue
            waits.append((sem, val))
            E.synced["T"][sem] = val
        E.q["T"].append((waits, fn, "T" if attach else None, attach))
        if flush:
            mm_flush()
        return tok

    def mm_flush():
        if pendT["amt"]:
            a = pendT["amt"]
            pendT["amt"] = 0
            E.q["T"].append(([], lambda e: e.drain(), "T", a))

    t_id = dmaY(identbf[:, :], ident_bf_d.ap(), sem="CONST")
    t_idf = dmaY(identf[:, :], ident_f_d.ap(), sem="CONST")
    t_ones = dmaY(onesbf[:, :], ones_bf_d.ap(), sem="CONST")
    t_sel = dmaY(selbf[:, :], sel_d.ap(), sem="CONST")
    t_wl1 = dmaY(sap(wl1row, [[64, 1], [1, 64]]), bass.AP(wl1_d, 0, [[64, 1], [1, 64]]), sem="CONST")
    t_bl1 = dmaY(sap(bl1col, [[1, 64], [1, 1]]), bass.AP(bl1_d, 0, [[1, 64], [1, 1]]), sem="CONST")
    t_wl2 = dmaY(wl2nat[:, :], wl2_d.ap(), sem="CONST")
    t_bl2 = dmaY(sap(bl2col, [[1, 16], [1, 1]]), bass.AP(bl2_d, 0, [[1, 16], [1, 1]]), sem="CONST")
    wrnat = None
    stag_f = stag[:, :].bitcast(F32)
    t_b1n = dmaY(sap(stag, [[1024, 32], [1, 256]]).bitcast(F32),
                 bass.AP(b1_d, 0, [[128, 32], [1, 128]]), sem="CONST")
    rowtmp = sap(x1h, [[8192, 1], [1, 1024]], offset=64 * 8192 + 2048)
    t_bout = dmaY(rowtmp, bass.AP(bout_d, 0, [[1024, 1], [1, 1024]]), sem="CONST")
    tok_const = ("CONST", E.cnt["CONST"], True)
    t_id = t_idf = t_ones = t_sel = t_wl1 = t_bl1 = t_wl2 = t_bl2 = tok_const
    t_b1n = t_bout = tok_const

    t_c12 = VV(lambda e: e.memset(cconst[:, 0:1], -12.0), (), 1, True)
    t_c18 = VV(lambda e: e.memset(cconst[:, 1:2], 0.125), (), 1, True)

    SKIP = set(skips)
    logpos = sap(x1h, [[8192, 1], [1, 2048]], offset=0)
    if "lam" in SKIP:
        t_iota = t_logp = t_lambf = None
        t_h1 = [None] * 4
        t_lam = [None] * 4
        t_wl2T = None
    t_iota = GG(lambda e: e.iota(logpos, [[1, 2048]], base=0, channel_multiplier=0,
                                 allow_small_or_imprecise_dtypes=True), (), 1, True) if "lam" not in SKIP else None
    if "lam" not in SKIP:
        t_logp = SS(lambda e: e.activation(logpos, logpos, AF.Ln, bias=1.0, scale=1.0),
                    (t_iota,), 1)

    t_h1 = []
    for i in range(0 if "lam" in SKIP else 4):
        prev_ev = t_h1[-1] if t_h1 else None
        tokm = mm(pk(0, 0, 512, 0, 64), wl1row[0:1, :],
                  sap(x1h, [[8192, 1], [1, 512]], offset=i * 512),
                  True, True, (t_wl1, t_logp, prev_ev), 1, new_gen=True, flush=True)
        toke = SS(lambda e, i=i: e.activation(
            sap(x1h, [[8192, 64], [1, 512]], offset=2048 + i * 512),
            pk(0, 0, 512, 0, 64), AF.Relu, bias=bl1col[:, :], scale=1.0),
            (tokm, t_bl1), 1)
        t_h1.append(toke)

    if "lam" not in SKIP:
        t_wl2T_mm = mm(pk(2, 0, 16, 0, 64), wl2nat[:, :], identf[0:16, 0:16],
                       True, True, (t_wl2, t_idf), 1, new_gen=True, flush=True)
        t_wl2T = VV(lambda e: e.tensor_copy(wl2T[:, :], pk(2, 0, 16, 0, 64)), (t_wl2T_mm,), 1)
    else:
        t_wl2T = None

    t_lam = []
    for i in range(0 if "lam" in SKIP else 4):
        prev_ev = t_lam[-1] if t_lam else None
        tokm = mm(pk(1, 0, 512, 0, 16), wl2T[:, :],
                  sap(x1h, [[8192, 64], [1, 512]], offset=2048 + i * 512),
                  True, True, (t_wl2T, t_h1[i], prev_ev), 1, new_gen=True, flush=True)
        toke = SS(lambda e, i=i: e.activation(
            sap(x1h, [[8192, 16], [1, 512]], offset=96 * 8192 + 2048 + i * 512),
            pk(1, 0, 512, 0, 16), AF.Sigmoid, bias=bl2col[:, :], scale=1.0),
            (tokm, t_bl2), 1)
        t_lam.append(toke)
    t_lambf = VV(lambda e: e.tensor_copy(
        lambf[:, :], sap(x1h, [[8192, 16], [1, 2048]], offset=96 * 8192 + 2048)),
        tuple(t_lam), 1) if "lam" not in SKIP else None

    x1h_bf = x1h[:, :].bitcast(BF16)
    hi_bf = x1h_bf[64:65, 6144:7168]
    lo_bf = x1h_bf[64:65, 7168:8192]
    hilo_bf = x1h_bf[64:65, 6144:8192]
    if "bias" in SKIP:
        t_bo_hi = t_bo_lo0 = t_bo_lo = t_b2r = t_b2_hi = t_b2_lo0 = t_b2_lo = None
        tok_bias = None
    t_bo_hi = VV(lambda e: e.tensor_copy(hi_bf, rowtmp), (t_bout,), 1) if "bias" not in SKIP else None
    if "bias" not in SKIP:
        t_bo_lo0 = VV(lambda e: e.tensor_sub(lo_bf, rowtmp, hi_bf), (t_bo_hi,), 1)
        t_bo_lo = dmaY(boutrow(0, 2048), hilo_bf, (t_bo_lo0,), sem="BIAS")
        t_b2r = dmaY(rowtmp, bass.AP(b2_d, 0, [[1024, 1], [1, 1024]]), (t_bo_lo,), sem="BIAS")
        t_b2_hi = VV(lambda e: e.tensor_copy(hi_bf, rowtmp), (t_b2r, t_bo_lo), 1)
        t_b2_lo0 = VV(lambda e: e.tensor_sub(lo_bf, rowtmp, hi_bf), (t_b2_hi,), 1)
        t_b2_lo = dmaY(b2row(0, 2048), hilo_bf, (t_b2_lo0,), sem="BIAS")
        tok_bias = ("BIAS", E.cnt["BIAS"], True)
        t_bo_lo = t_b2_lo = tok_bias

    t_b1m = mm(pk(3, 0, 32), stag_f[0:32, 0:128], identf[0:32, 0:32],
               True, True, (t_b1n, t_idf), 1, new_gen=True, flush=True)
    t_b1 = VV(lambda e: e.tensor_copy(b1col[:, :], pk(3, 0, 32)), (t_b1m,), 1)
    t_wr = dmaY(sap(x1h, [[8192, 8], [1, 1024]], offset=4096), wr_d.ap(),
                (t_b1,), sem="WR")

    t_wrT = []
    prev_wrev = t_b1
    for cc in range(8):
        tokm = mm(pk(3, 0, 8), sap(x1h, [[8192, 8], [1, 128]], offset=4096 + cc * 128),
                  identf[0:8, 0:8],
                  True, True, (t_wr, t_idf, prev_wrev), 1, new_gen=True, flush=True)
        tokc = VV(lambda e, cc=cc: e.tensor_copy(wrT[:, cc * 8:(cc + 1) * 8], pk(3, 0, 8)),
                  (tokm,), 1)
        prev_wrev = tokc
        t_wrT.append(tokc)

    stag2 = nc.alloc_sbuf_tensor_at("stag2", [128, 1024], BF16,
                                    offset=nc.lookup_mloc(aoT).addr)
    stags = [stag, stag2]
    stag_tok = [t_wrT[-1], None]
    tr45_free = [None, None]
    rot_evac = []
    jc_glob = [0]

    def load_transpose(jc_total, src_dram, dst_base, dst_ccstride):
        for jc in range(jc_total):
            sl = jc_glob[0] % 2
            jc_glob[0] += 1
            src = bass.AP(src_dram, jc * 128 * 1024, [[1024, 128], [1, 1024]])
            dst = sap(stags[sl], [[1024, 128], [1, 1024]])
            tdma = dmaG(dst, src, (stag_tok[sl],), sem=f"SG{sl}")
            last_mm = None
            for g in range(2):
                bank = 4 + g
                mts = []
                for b in range(4):
                    cc = g * 4 + b
                    lhsT = sap(stags[sl], [[1024, 128], [1, 128]], offset=cc * 128)
                    deps = (tdma, t_id, tr45_free[g]) if b == 0 else ()
                    tokm = mm(pk(bank, b * 128, 128), lhsT, identbf[:, :],
                              b == 0, b == 3, deps, 1 if b == 3 else 0, new_gen=(b == 0),
                              flush=(b == 3))
                    mts.append(tokm)
                dst_ap = sap(rot, [[32768, 128], [dst_ccstride, 4], [1, 128]],
                             offset=dst_base + (g * 4) * dst_ccstride + jc * 128)
                src_ap = bass.AP(ps[2], (bank % 2) * 512, [[1024, 128], [128, 4], [1, 128]])
                tokev = VV(lambda e, d=dst_ap, s_=src_ap: e.tensor_copy(d, s_),
                           (mts[-1],), 1)
                tr45_free[g] = tokev
                rot_evac.append(tokev)
                last_mm = mts[-1]
            stag_tok[sl] = last_mm

    xft = [None] * 16
    x1tok = [None] * 16

    def xf_load(k, dep):
        slot = k % 3
        dst = sap(xf32, [[3072, 128], [1, 1024]], offset=slot * 1024)
        src = bass.AP(x_d, k * 128 * 1024, [[1024, 128], [1, 1024]])
        xft[k] = dmaY(dst, src, (dep,), sem=("XR0", "XR1", "XB0")[k % 3])

    for k0 in range(3):
        xf_load(k0, None)

    load_transpose(24, wqkv_d, 0, 3072)
    load_transpose(8, wout_d, 24576, 1024)
    rot_ready0 = tuple(rot_evac[-2:])

    t_w1 = []
    for q4 in range(0 if no_gs else 4):
        for cc in range(8):
            dst = sap(w1a, [[32768, 128], [1, 1024]], offset=cc * 4096 + q4 * 1024)
            src = bass.AP(w1_d, cc * 128 * 4096 + q4 * 1024, [[4096, 128], [1, 1024]])
            t_w1.append(dmaG(dst, src, sem="GS"))
    tok_w1 = ("GS", E.cnt["GS"], True)

    t_w2img = []
    for fc in range(0 if no_gs else 32):
        dst = bass.AP(w2img_d, fc * 1024, [[32768, 128], [1, 1024]])
        src = bass.AP(w2_d, fc * 128 * 1024, [[1024, 128], [1, 1024]])
        t_w2img.append(dmaG(dst, src, sem=f"W2I{fc // 4}"))
    t_w2img = [t_w2img[4 * g + 3] for g in range(8)] if t_w2img else []
    t_rotimg = dmaG(rotimg_d.ap(), rot[:, :], tuple(rot_evac), sem="RIMG") \
        if "rimg" not in SKIP else None
    tok_gs = ("GS", E.cnt["GS"], True)
    if no_gs:
        tok_w1 = tok_gs

    x1slot_consumed = [[] for _ in range(8)]
    x1slot_consumed[0] = [t_lambf]
    x1slot_consumed[1] = [t_lambf]
    x1slot_consumed[2] = [t_lambf, t_b2_lo]
    x1slot_consumed[3] = [t_lambf, t_b2_lo]
    x1slot_consumed[4] = [t_wrT[-1]]
    x1slot_ready = [None] * 8
    xT_free = None
    xcast_free = None
    y2_free = [None, None]
    tr_free = [t_h1[-1], t_lam[-1]]
    qk_free = [t_wl2T, t_wrT[-1]]
    wout_free45 = rot_evac[-1]
    wout_free67 = None
    last_sdA = None
    last_tailA = None
    a_cast = {}
    a_epi = {}
    b_cast = {}
    h_free = [None] * 4
    hbank_free = [None] * 4
    y_free = [None, None, None, None]
    ht_free = [None] * NG
    router_free = [None, None]
    av_free = [None, None]
    score_free = [None, None]
    sums_free = [None, None]
    eT_free = [None, None, None, None]
    aoT_free = None
    last_sd = None
    last_r0 = None
    last_te = None
    last_tp = None
    last_A_qkv_tok = None
    last_A_wout_tok = None
    last_B_ymm_tok = [None] * 32
    rot_ready = {0: rot_ready0}

    def ln_stats_a(z_ap, scratch_ap, stt_tok, extra_var_deps=(), st=None):
        nonlocal last_sd
        if st is None:
            st = stz
        tok_mu = VV(lambda e, st=st: e.tensor_scalar_mul(st[:, 1:2], st[:, 0:1],
                                                         1.0 / 1024.0),
                    (stt_tok,), 1, True)
        tok_var = VV(lambda e, z=z_ap, sc=scratch_ap, st=st:
                     e.scalar_tensor_tensor(sc, z, st[:, 1:2], z, ALU.subtract, ALU.mult,
                                            accum_out=st[:, 2:3]),
                     (tok_mu,) + tuple(extra_var_deps), 1, True)
        tok_ve = VV(lambda e, st=st: e.tensor_scalar(st[:, 3:4], st[:, 2:3], 1.0 / 1024.0,
                                                     1e-5, ALU.mult, ALU.add),
                    (tok_var, last_sd), 1, True)
        return tok_ve, tok_var

    def ln_stats_b(tok_ve, st=None):
        nonlocal last_sd, last_r0
        if st is None:
            st = stz
        tok_sd = SS(lambda e, st=st: e.activation(st[:, 4:5], st[:, 3:4], AF.Sqrt,
                                                  bias=0.0, scale=1.0),
                    (tok_ve, last_r0), 1)
        tok_r0 = VV(lambda e, st=st: e.reciprocal(st[:, 5:6], st[:, 4:5]),
                    (tok_sd,), 1, True)
        tok_aa = VV(lambda e, st=st: e.tensor_mul(st[:, 6:7], st[:, 5:6], st[:, 5:6]),
                    (tok_r0,), 1, True)
        tok_aa2 = VV(lambda e, st=st: e.tensor_mul(st[:, 6:7], st[:, 6:7], st[:, 3:4]),
                     (tok_aa,), 1, True)
        tok_aa3 = VV(lambda e, st=st: e.tensor_scalar(st[:, 6:7], st[:, 6:7], -0.5, 1.5,
                                                      ALU.mult, ALU.add),
                     (tok_aa2,), 1, True)
        tok_rs = VV(lambda e, st=st: e.tensor_mul(st[:, 7:8], st[:, 5:6], st[:, 6:7]),
                    (tok_aa3,), 1, True)
        last_sd = tok_sd
        last_r0 = tok_r0
        return tok_rs

    def ln_stats(z_ap, scratch_ap, stt_tok, extra_var_deps=(), st=None):
        tok_ve, tok_var = ln_stats_a(z_ap, scratch_ap, stt_tok, extra_var_deps, st)
        tok_rs = ln_stats_b(tok_ve, st)
        return tok_rs, tok_var

    for half in range(n_halves):
        if half == 1:
            toks = []
            for ch in range(8):
                gate = last_B_ymm_tok[4 * ch + 3]
                dst = sap(rot, [[32768, 128], [1, 4096]], offset=ch * 4096)
                src = bass.AP(rotimg_d, ch * 4096, [[32768, 128], [1, 4096]])
                toks.append(dmaY(dst, src, (gate, t_rotimg), sem="ROTL"))
            rot_ready[1] = tuple(toks)
        rr = rot_ready[half]


        def a_epi_tail(ti):
            nonlocal last_sdA, last_tailA
            ve_both = (a_epi[(ti, 0)][0], a_epi[(ti, 1)][0])
            for sb in range(2):
                tok_ve, x1dst, slot = a_epi.pop((ti, sb))
                tok_sd = SS(lambda e, sb=sb: e.activation(stz[:, 0:1],
                                                          stz[:, 3 + sb:4 + sb],
                                                          AF.Sqrt, bias=0.0, scale=1.0),
                            ve_both + (last_tailA,), 1)
                tok_r0 = VV(lambda e: e.reciprocal(stz[:, 1:2], stz[:, 0:1]),
                            (tok_sd,), 1, True)
                tok_aa = VV(lambda e: e.tensor_mul(stz[:, 2:3], stz[:, 1:2],
                                                   stz[:, 1:2]),
                            (tok_r0,), 1, True)
                tok_aa2 = VV(lambda e, sb=sb: e.tensor_mul(stz[:, 2:3], stz[:, 2:3],
                                                           stz[:, 3 + sb:4 + sb]),
                             (tok_aa,), 1, True)
                tok_aa3 = VV(lambda e: e.tensor_scalar(stz[:, 2:3], stz[:, 2:3],
                                                       -0.5, 1.5, ALU.mult, ALU.add),
                             (tok_aa2,), 1, True)
                tok_rs = VV(lambda e: e.tensor_mul(stz[:, 5:6], stz[:, 1:2],
                                                   stz[:, 2:3]),
                            (tok_aa3,), 1, True)
                tok_x1 = VV(lambda e, x1dst=x1dst:
                            e.tensor_scalar_mul(x1dst, x1dst, stz[:, 5:6]),
                            (tok_rs,), 1, True)
                x1slot_ready[slot] = tok_x1
                last_sdA = tok_sd
                last_tailA = tok_x1

        def mk_acast(ti, sb):
            kk = ti * 2 + sb
            dst = sap(xcast, [[2048, 128], [1, 1024]], offset=sb * 1024)
            src = sap(xf32, [[3072, 128], [1, 1024]], offset=(kk % 3) * 1024)
            return SS(lambda e, d=dst, s_=src: e.copy(d, s_),
                      (xft[kk], xcast_free), 1)

        for tl in range(n_a[half]):
            tile = half * 4 + tl
            n = tile
            t0 = tile * CS

            k0 = tile * 2
            if 3 <= k0 + 2 < 16:
                xf_load(k0 + 2, x1tok[k0 - 1])
            xbf_toks = []
            for s in range(2):
                k = k0 + s
                if (tile, s) in a_cast:
                    xbf_toks.append(a_cast.pop((tile, s)))
                else:
                    xbf_toks.append(mk_acast(tile, s))

            trev = []
            for g in range(4):
                s, cq = divmod(g, 2)
                bk = g % 2
                mts = []
                for b in range(4):
                    cc = cq * 4 + b
                    lhsT = sap(xcast, [[2048, 128], [1, 128]], offset=s * 1024 + cc * 128)
                    deps = ()
                    if b == 0:
                        deps = (xbf_toks[s], tr_free[bk], av_free[bk], t_id,
                                router_free[bk], hbank_free[bk])
                    tokm = mm(pk(bk, b * 128, 128), lhsT, identbf[:, :],
                              b == 0, b == 3, deps, 1 if b == 3 else 0, new_gen=(b == 0))
                    mts.append(tokm)
                dst_ap = sap(xT, [[2048, 128], [256, 4], [1, 128]],
                             offset=cq * 4 * 256 + s * 128)
                src_ap = bass.AP(ps[0], (bk % 2) * 512, [[1024, 128], [128, 4], [1, 128]])
                tokev = SS(lambda e, d=dst_ap, s_=src_ap: e.copy(d, s_),
                           (mts[-1], xT_free), 1)
                trev.append(tokev)
                tr_free[bk] = tokev
            xT_ready = (trev[-1],)
            xcast_free = trev[-1]

            lam_m = [None] * 4
            lam_copy = [None] * 4
            lam_banks = [4, 5, 6, 7]
            for pr in range(4):
                bkk = lam_banks[pr]
                deps0 = [t_sel, t_lambf]
                deps0 += [wout_free45 if bkk in (4, 5) else wout_free67]
                m0 = mm(pk(bkk, 0, 256), selbf[:, (2 * pr) * 128:(2 * pr + 1) * 128],
                        lambf[:, n * 256:(n + 1) * 256], True, True, tuple(deps0),
                        0, new_gen=True)
                m1 = mm(pk(bkk, 256, 256), selbf[:, (2 * pr + 1) * 128:(2 * pr + 2) * 128],
                        lambf[:, n * 256:(n + 1) * 256], True, True, (), 1, new_gen=False)
                lam_m[pr] = m1

            qev = []
            deps_half = list(rr) + [h_free[0], h_free[1]]
            for pi in range(4):
                bk = pi % 4
                grp_last = None
                for sub_j in range(2):
                    jc = pi * 2 + sub_j
                    for cc in range(8):
                        lhsT = sap(rot, [[32768, 128], [1, 128]], offset=cc * 3072 + jc * 128)
                        rhs = sap(xT, [[2048, 128], [1, 256]], offset=cc * 256)
                        first = (cc == 0)
                        deps = ()
                        if first and sub_j == 0:
                            if bk < 2:
                                bdep = (tr_free[bk], av_free[bk])
                            else:
                                bdep = (qk_free[bk - 2],)
                            deps = tuple(deps_half) + bdep + \
                                (hbank_free[bk],) + xT_ready
                        grp_last = mm(pk(bk, sub_j * 256, 256), lhsT, rhs, first, cc == 7,
                                      deps, 1 if (cc == 7 and sub_j == 1) else 0,
                                      new_gen=(first and sub_j == 0))
                tokev = SS(lambda e, pi=pi, bk=bk:
                           e.activation(qT[:, pi * 512:(pi + 1) * 512], pk(bk, 0, 512),
                                        AF.Copy, bias=0.0, scale=cconst[:, 1:2]),
                           (grp_last, y2_free[0], t_c18), 1)
                qev.append(tokev)

            kev = []
            lam_copy[0] = VV(lambda e: e.tensor_copy(lam_sb[:, 0:512],
                                                     pk(lam_banks[0], 0, 512)),
                             (lam_m[0],), 1)

            for pr in range(4):
                bk = pr % 4
                grp_last = None
                for sub_j in range(2):
                    jc = 8 + pr * 2 + sub_j
                    for cc in range(8):
                        lhsT = sap(rot, [[32768, 128], [1, 128]], offset=cc * 3072 + jc * 128)
                        rhs = sap(xT, [[2048, 128], [1, 256]], offset=cc * 256)
                        first = (cc == 0)
                        deps = (qev[pr],) if (first and sub_j == 0) else ()
                        grp_last = mm(pk(bk, sub_j * 256, 256), lhsT, rhs, first, cc == 7,
                                      deps, 1 if (cc == 7 and sub_j == 1) else 0,
                                      new_gen=(first and sub_j == 0))
                tokk = VV(lambda e, pr=pr, bk=bk:
                          e.tensor_mul(kTs[:, pr * 512:(pr + 1) * 512],
                                       pk(bk, 0, 512),
                                       lam_sb[:, 0:512]),
                          (grp_last, lam_copy[pr]), 1)
                kev.append(tokk)
                if pr + 1 < 4:
                    nxt = pr + 1
                    lam_copy[nxt] = VV(lambda e, nxt=nxt, bkk=lam_banks[nxt]:
                                       e.tensor_copy(lam_sb[:, 0:512],
                                                     pk(bkk, 0, 512)),
                                       (lam_m[nxt], tokk), 1)
            sums_free[0] = lam_copy[2]
            sums_free[1] = lam_copy[3]

            vev = []
            for g in range(4):
                s, dvh = divmod(g, 2)
                bk = 2 + (g % 2)
                grp_last = None
                for cc in range(8):
                    lhsT = sap(xT, [[2048, 128], [1, 128]], offset=cc * 256 + s * 128)
                    rhs = sap(rot, [[32768, 128], [1, 512]], offset=cc * 3072 + 2048 + dvh * 512)
                    deps = ()
                    if cc == 0:
                        deps = (kev[2 + (g % 2)],) if g < 2 else (vev[g - 2],)
                    grp_last = mm(pk(bk, 0, 512), lhsT, rhs, cc == 0, cc == 7,
                                  deps, 1 if cc == 7 else 0, new_gen=(cc == 0))
                tokev = SS(lambda e, s=s, dvh=dvh, bk=bk:
                           e.copy(vtok[:, s * 1024 + dvh * 512:s * 1024 + (dvh + 1) * 512],
                                  pk(bk, 0, 512)),
                           (grp_last, y2_free[1]), 1)
                vev.append(tokev)
            last_A_qkv_tok = vev[-1]
            xT_free = vev[-1]
            if tl >= 1:
                a_epi_tail(tile - 1)

            prev_scores = {}
            prev_exp = {}
            prev_sums = {}
            prev_recip = {}
            prev_av = {}
            rb_done = {}
            bulk_toks = []
            ao_toks = []

            def do_scores(h):
                par = h % 2
                bk = 4 + par
                cc_h, ho = divmod(h, 2)
                po = ho * 64
                last = None
                for ks in range(2):
                    lhsT = bass.AP(kTs, po * 2048 + cc_h * 256 + ks * 128,
                                   [[2048, 64], [1, 128]])
                    rhs = bass.AP(qT, po * 2048 + cc_h * 256, [[2048, 64], [1, 256]])
                    deps = ()
                    if ks == 0:
                        deps = (score_free[par], kev[cc_h // 2], qev[cc_h // 2],
                                lam_copy[par])
                    last = mm(pk(bk, ks * 256, 256), lhsT, rhs, True, True,
                              deps, 1 if ks == 1 else 0, new_gen=(ks == 0))
                prev_scores[h] = last

            def do_exp(h):
                par = h % 2
                slot = h % 4
                tok = SS(lambda e, par=par, slot=slot:
                         e.activation(bass.AP(eT, slot * 512, [[2048, 128], [1, 512]]),
                                      pk(4 + par, 0, 512), AF.Exp, bias=cconst[:, 0:1], scale=1.0),
                         (prev_scores[h], eT_free[slot], t_c12), 1)
                prev_exp[h] = tok
                score_free[par] = tok

            def do_sums(h):
                par = h % 2
                slot = h % 4
                last = None
                for ks in range(2):
                    rhs = bass.AP(eT, slot * 512 + ks * 256, [[2048, 128], [1, 256]])
                    deps = ()
                    if ks == 0:
                        deps = (prev_exp[h], sums_free[par], t_ones)
                    last = mm(pk(6 + par, 0, 256, 0, 1), onesbf[:, 0:1], rhs,
                              ks == 0, ks == 1, deps, 1 if ks == 1 else 0,
                              new_gen=(ks == 0))
                prev_sums[h] = last

            def do_recip(h):
                par = h % 2
                c0 = (h % 2) * 512 + ((h // 2) % 2) * 256
                tok = VV(lambda e, par=par, c0=c0:
                         e.reciprocal(rTbf(c0, 256),
                                      pk(6 + par, 0, 256, 0, 1)),
                         (prev_sums[h], rb_done.get(h // 4 - 1)), 1, True)
                prev_recip[h] = tok
                sums_free[par] = tok

            def do_bulk(g):
                bk = 2 + (g % 2)
                bdep = vev[2 + (g % 2)] if g < 2 else qk_free[g % 2]
                mm(pk(bk, 0, 512, 0, 64), onesbf[32:33, 0:64], rTbf(0, 512),
                   True, True, (prev_recip[4 * g + 3], bdep), 0,
                   new_gen=True)
                rbtok = mm(pk(bk, 0, 512, 64, 64), onesbf[32:33, 0:64],
                           rTbf(512, 512), True, True, (), 1, new_gen=True)
                rb_done[g] = rbtok
                src = bass.AP(aoT, g * 512, [[2048, 128], [1, 512]])
                rb = bass.AP(ps[1], (g % 2) * 512, [[1024, 128], [1, 512]])
                tok = VV(lambda e, src=src, rb=rb: e.tensor_mul(src, src, rb),
                         (rbtok, ao_toks[-1]), 1)
                qk_free[g % 2] = tok
                bulk_toks.append(tok)

            def do_av(h):
                par = h % 2
                slot = h % 4
                last = None
                for ks in range(2):
                    lhsT = bass.AP(vtok, ks * 1024 + h * 64, [[2048, 128], [1, 64]])
                    rhs = bass.AP(eT, slot * 512 + ks * 256, [[2048, 128], [1, 256]])
                    deps = ()
                    if ks == 0:
                        deps = (prev_exp[h], av_free[par], kev[par],
                                lam_copy[par], lam_copy[2 + par], vev[3],
                                router_free[par])
                    last = mm(pk(par, 0, 256, 0, 64), lhsT, rhs, ks == 0, ks == 1,
                              deps, 1 if ks == 1 else 0, new_gen=(ks == 0))
                prev_av[h] = last
                eT_free[slot] = last

            def do_ao(h):
                par = h % 2
                cc_h, ho = divmod(h, 2)
                dst = bass.AP(aoT, ho * 64 * 2048 + cc_h * 256, [[2048, 64], [1, 256]])
                tok = SS(lambda e, dst=dst, par=par:
                         e.copy(dst, pk(par, 0, 256, 0, 64)),
                         (prev_av[h], aoT_free), 1)
                av_free[par] = tok
                return tok

            if attn_serial:
                for h_ in range(H):
                    do_scores(h_)
                    mm_flush()
                    do_exp(h_)
                    do_sums(h_)
                    mm_flush()
                    do_recip(h_)
                    do_av(h_)
                    mm_flush()
                    ao_toks.append(do_ao(h_))
                    if h_ % 4 == 3:
                        do_bulk(h_ // 4)
                        mm_flush()
            else:
                for step in range(H + 2):
                    if step < H:
                        do_scores(step)
                        do_exp(step)
                    if 2 <= step < H + 2:
                        do_av(step - 2)
                        ao_toks.append(do_ao(step - 2))
                        if (step - 2) % 4 == 3:
                            do_bulk((step - 2) // 4)
                    if 1 <= step < H + 1:
                        do_sums(step - 1)
                        do_recip(step - 1)
            mm_flush()
            if tl + 1 < n_a[half]:
                a_cast[(tile + 1, 0)] = mk_acast(tile + 1, 0)

            for s in range(2):
                k = tile * 2 + s
                grp = []
                for jh in range(2):
                    bk = 4 + 2 * s + jh
                    last = None
                    for cc in range(8):
                        lhsT = sap(aoT, [[2048, 128], [1, 128]], offset=cc * 256 + s * 128)
                        rhs = sap(rot, [[32768, 128], [1, 512]],
                                  offset=24576 + cc * 1024 + jh * 512)
                        deps = ()
                        if cc == 0:
                            deps = ((score_free[jh], bulk_toks[-1]) if s == 0 else
                                    (sums_free[0], sums_free[1], bulk_toks[-1]))
                        last = mm(pk(bk, 0, 512), lhsT, rhs, cc == 0, False, deps, 0,
                                  new_gen=(cc == 0))
                    mm(pk(bk, 0, 512), onesbf[32:33, 0:128],
                       boutrow(jh * 512, 512), False, False,
                       (t_bo_lo, t_ones), 0, new_gen=False)
                    last = mm(pk(bk, 0, 512), onesbf[32:33, 0:128],
                              boutrow(1024 + jh * 512, 512),
                              False, True, (), 1, new_gen=False)
                    grp.append(last)
                mm_flush()
                last_A_wout_tok = grp[-1]
                pw = bass.AP(ps[2 + s], 0, [[1024, 128], [1, 1024]])
                z1 = sap(xf32, [[3072, 128], [1, 1024]], offset=(k % 3) * 1024)
                tok_z1 = VV(lambda e, pw=pw, z1=z1:
                            e.scalar_tensor_tensor(z1, pw, 0.0, z1, ALU.add, ALU.add,
                                                   accum_out=stz[:, 0:1]),
                            tuple(grp) + (xft[k], xbf_toks[s], last_sdA), 1, True)
                if s == 0:
                    wout_free45 = tok_z1
                else:
                    wout_free67 = tok_z1
                slot = tl * 2 + s
                tok_mu = VV(lambda e:
                            e.tensor_scalar_mul(stz[:, 1:2], stz[:, 0:1], 1.0 / 1024.0),
                            (tok_z1,), 1, True)
                x1dst = sap(x1h, [[8192, 128], [1, 1024]], offset=slot * 1024)
                tok_p1 = VV(lambda e, z1=z1, x1dst=x1dst:
                            e.tensor_scalar(x1dst, z1, stz[:, 1:2], None,
                                            ALU.subtract),
                            (tok_mu,) + tuple(x1slot_consumed[slot]), 1, True)
                scratch = eT[:, :].bitcast(F32)
                tok_var = VV(lambda e, sc=scratch, x1dst=x1dst:
                             e.scalar_tensor_tensor(sc, x1dst, 0.0, x1dst,
                                                    ALU.add, ALU.mult,
                                                    accum_out=stz[:, 2:3]),
                             (tok_p1, eT_free[0], eT_free[1],
                              eT_free[2], eT_free[3]), 1, True)
                tok_ve = VV(lambda e, s=s:
                            e.tensor_scalar(stz[:, 3 + s:4 + s], stz[:, 2:3],
                                            1.0 / 1024.0, 1e-5, ALU.mult, ALU.add),
                            (tok_var, last_tailA), 1, True)
                a_epi[(tile, s)] = (tok_ve, x1dst, slot)
                x1tok[k] = tok_p1
                if s == 0 and k + 3 < 16:
                    xf_load(k + 3, tok_p1)
                    if tl + 1 < n_a[half]:
                        a_cast[(tile + 1, 1)] = mk_acast(tile + 1, 1)
                for i_ in range(4):
                    eT_free[i_] = tok_var
        if n_b[half]:
            src2 = sap(x1h, [[8192, 128], [1, 2048]])
            b_cast[0] = SS(lambda e, s_=src2: e.copy(xcast[:, :], s_),
                           (x1slot_ready[0], x1slot_ready[1], xcast_free), 1)
            xcast_free = b_cast[0]
        if n_a[half]:
            a_epi_tail(half * 4 + n_a[half] - 1)
        for i in range(NG):
            ht_free[i] = wout_free67

        if dump == "x1h" and half == 0:
            E.op("Y", lambda e: e.dma_start(dbg_d.ap(), x1h[:, :]),
                 tuple(x1slot_ready), 16, "O", True)
        if dump == "ln" and half == 0:
            gates = (x1slot_ready[0], x1slot_ready[1])
            E.op("Y", lambda e: e.dma_start(bass.AP(dbg_d, 0, [[8192, 128], [1, 1024]]),
                                            xf32[:, 0:1024]), gates, 16, "O", True)
            E.op("Y", lambda e: e.dma_start(bass.AP(dbg_d, 2048, [[8192, 128], [1, 2048]]),
                                            x1h[:, 0:2048]), (), 16, "O", True)
            E.op("Y", lambda e: e.dma_start(
                bass.AP(dbg_d, 4096, [[8192, 128], [1, 8]]), stz[:, :],
                allow_slow_non_contiguous=True),
                (), 16, "O", True)
        if dump == "all" and half == 0:
            g0 = tuple(x1slot_ready[0:2])
            E.op("Y", lambda e: e.dma_start(dball["d_xT"].ap(), xT[:, :]), g0, 16, "O", True)
            E.op("Y", lambda e: e.dma_start(dball["d_qT"].ap(), qT[:, :]), (), 16, "O", True)
            E.op("Y", lambda e: e.dma_start(dball["d_kTs"].ap(), kTs[:, :]), (), 16, "O", True)
            E.op("Y", lambda e: e.dma_start(dball["d_vtok"].ap(), vtok[:, :]), (), 16, "O", True)
            E.op("Y", lambda e: e.dma_start(dball["d_eT"].ap(), eT[:, :]), (), 16, "O", True)
            E.op("Y", lambda e: e.dma_start(dball["d_aoT"].ap(), aoT[:, :]), (), 16, "O", True)
            E.op("Y", lambda e: e.dma_start(dball["d_lam"].ap(), lam_sb[:, :]), (), 16, "O", True)
            E.op("Y", lambda e: e.dma_start(dball["d_rot"].ap(), rot[:, :]), (), 16, "O", True)
            E.op("Y", lambda e: e.dma_start(dball["d_z1"].ap(), xf32[:, 0:1024]), (), 16, "O", True)
            E.op("Y", lambda e: e.dma_start(bass.AP(dball["d_x1h"], 0, [[8192, 128], [1, 2048]]),
                                            sap(x1h, [[8192, 128], [1, 2048]])), (), 16, "O", True)
            E.op("Y", lambda e: e.dma_start(
                bass.AP(dball["d_st"], 0, [[16, 128], [1, 1]]),
                sap(stz, [[8, 128], [1, 1]]), allow_slow_non_contiguous=True), (), 16, "O", True)
        if dump == "rot0" and half == 0:
            dbg_bf = dbg_d.ap().bitcast(BF16)
            E.op("Y", lambda e: e.dma_start(dbg_bf[:, 0:12288],
                                            sap(rot, [[32768, 128], [1, 12288]])),
                 rot_ready0, 16, "O", True)
            E.op("Y", lambda e: e.dma_start(dbg_bf[:, 12288:12416], identbf[:, :]),
                 (), 16, "O", True)
            E.op("Y", lambda e: e.dma_start(dbg_bf[:, 12416:13440], stag[:, :]),
                 (), 16, "O", True)
        if dump == "rot" and half == 0:
            gates = (x1slot_ready[0], x1slot_ready[1])
            for cc_ in range(8):
                E.op("G", lambda e, cc_=cc_: e.dma_start(
                    bass.AP(dbg_d, cc_ * 1024, [[8192, 128], [1, 1024]]),
                    sap(rot, [[32768, 128], [1, 1024]], offset=cc_ * 3072)),
                    gates, 16, "O", True)
        if dump == "mid" and half == 0:
            dd = dbg_d.ap()
            gates = tuple(t for t in x1slot_ready if t) + (t_lambf,)
            E.op("G", lambda e: e.dma_start(bass.AP(dbg_d, 0, [[8192, 128], [1, 2048]]), xT[:, :]),
                 gates, 16, "O", True)
            E.op("G", lambda e: e.dma_start(bass.AP(dbg_d, 2048, [[8192, 128], [1, 2048]]), qT[:, :]),
                 (), 16, "O", True)
            E.op("G", lambda e: e.dma_start(bass.AP(dbg_d, 4096, [[8192, 128], [1, 2048]]), kTs[:, :]),
                 (), 16, "O", True)
            E.op("G", lambda e: e.dma_start(bass.AP(dbg_d, 6144, [[8192, 128], [1, 2048]]),
                                            aoT[:, :]),
                 (), 16, "O", True)

        w2_chunks = []
        for ch in range(8 if n_b[half] else 0):
            gate = last_A_qkv_tok if ch < 6 else last_A_wout_tok
            dst = sap(rot, [[32768, 128], [1, 4096]], offset=ch * 4096)
            src = bass.AP(w2img_d, ch * 4096, [[32768, 128], [1, 4096]])
            gimg = t_w2img[ch] if t_w2img else tok_gs
            deps = (gate, gimg, t_rotimg)
            w2_chunks.append(dmaY(dst, src, deps, sem=f"W2C{ch}"))


        def mk_bcast(tl2):
            sl2 = tl2 * 2
            src2 = sap(x1h, [[8192, 128], [1, 2048]], offset=sl2 * 1024)
            b_cast[tl2] = SS(lambda e, s_=src2: e.copy(xcast[:, :], s_),
                             (x1slot_ready[sl2], x1slot_ready[sl2 + 1],
                              xcast_free), 1)

        def b_head1(tl):
            nonlocal xcast_free, xT_ready, last_te, last_tp
            tile = half * 4 + tl
            if tl not in b_cast:
                mk_bcast(tl)
            tok_cast = b_cast[tl]
            xcast_free = tok_cast

            trev = []
            for g in range(4):
                s, cq = divmod(g, 2)
                bk = g % 2
                mts = []
                for b in range(4):
                    cc = cq * 4 + b
                    lhsT = sap(xcast, [[2048, 128], [1, 128]], offset=s * 1024 + cc * 128)
                    deps = ()
                    if b == 0:
                        deps = (tok_cast, tr_free[bk], av_free[bk], router_free[bk],
                                hbank_free[bk])
                    tokm = mm(pk(bk, b * 128, 128), lhsT, identbf[:, :],
                              b == 0, b == 3, deps, 1 if b == 3 else 0, new_gen=(b == 0))
                    mts.append(tokm)
                dst_ap = sap(xT, [[2048, 128], [256, 4], [1, 128]],
                             offset=cq * 4 * 256 + s * 128)
                src_ap = bass.AP(ps[0], (bk % 2) * 512, [[1024, 128], [128, 4], [1, 128]])
                tokev = SS(lambda e, d=dst_ap, s_=src_ap: e.copy(d, s_),
                           (mts[-1], xT_free), 1)
                trev.append(tokev)
                tr_free[bk] = tokev
            mm_flush()
            xT_ready = (trev[-1],)
            xcast_free = trev[-1]

            p7_toks = []
            rtm_toks = []
            for s in range(2):
                bk = s
                last = None
                for cc in range(8):
                    lhsT = sap(xT, [[2048, 128], [1, 128]], offset=cc * 256 + s * 128)
                    rhs = sap(wrT, [[64, 128], [1, 8]], offset=cc * 8)
                    deps = ()
                    if cc == 0:
                        deps = (tr_free[bk], t_wrT[-1]) + xT_ready
                    last = mm(pk(bk, 0, 8), lhsT, rhs, cc == 0, cc == 7, deps,
                              1 if cc == 7 else 0, new_gen=(cc == 0))
                tok_m = VV(lambda e, bk=bk:
                           e.tensor_reduce(rmax[:, :], pk(bk, 0, 8), AX.X, ALU.max,
                                           negate=True),
                           (last, last_te), 1, True)
                rtm_toks.append(tok_m)
                tok_e = SS(lambda e, bk=bk:
                           e.activation(er[:, :], pk(bk, 0, 8), AF.Exp,
                                        bias=rmax[:, :], scale=1.0),
                           (tok_m, last_tp), 1)
                router_free[bk] = tok_e
                last_te = tok_e
                tok_s = VV(lambda e: e.reduce_sum(rsum[:, :], er[:, :], AX.X),
                           (tok_e,), 1, True)
                tok_r = VV(lambda e: e.reciprocal(rrec[:, :], rsum[:, :]), (tok_s,), 1, True)
                pcol = (tl % 2) * 2 + s
                tok_p = VV(lambda e, pcol=pcol: e.tensor_mul(p7[:, pcol:pcol + 1],
                                                             er[:, 7:8], rrec[:, :]),
                           (tok_r,), 1, True)
                last_tp = tok_p
                p7_toks.append(tok_p)
            epi_data[tl] = {"p7": p7_toks, "rtm": rtm_toks}

        def b_head2(tl):
            nonlocal xT_free, aoT_free
            gelu_toks = [None] * 32
            for fcs in range(36):
                if fcs < 32:
                    fc = fcs
                    bk = fc % 4
                    last = None
                    for cc in range(8):
                        lhsT = sap(w1a, [[32768, 128], [1, 128]], offset=cc * 4096 + fc * 128)
                        rhs = sap(xT, [[2048, 128], [1, 256]], offset=cc * 256)
                        deps = ()
                        if cc == 0:
                            deps = (h_free[fc % 4], qk_free[0], qk_free[1],
                                    tok_w1) + xT_ready
                            if fc < 2:
                                deps = deps + (tr_free[fc], router_free[fc],
                                               epi_data[tl]["rtm"][fc])
                        last = mm(pk(bk, 0, 256), lhsT, rhs, cc == 0, cc == 7, deps,
                                  1 if cc == 7 else 0, new_gen=(cc == 0))
                    slot = fc % NG
                    gelu_toks[fc] = SS(lambda e, slot=slot, fc=fc, bk=bk:
                               e.activation(hT[:, slot * 256:(slot + 1) * 256],
                                            pk(bk, 0, 256), AF.Gelu,
                                            bias=b1col[:, fc:fc + 1], scale=1.0),
                               (last, ht_free[slot], t_b1), 1)
                    h_free[fc % 4] = gelu_toks[fc]
                    hbank_free[fc % 4] = gelu_toks[fc]
                    if fc == 31:
                        xT_free = last
                if fcs >= 4:
                    fc = fcs - 4
                    slot = fc % NG
                    w2ch = w2_chunks[fc // 4]
                    ylast = None
                    for s in range(2):
                        for jh in range(2):
                            ybk = 4 + 2 * s + jh
                            lhsT = sap(hT, [[2048, 128], [1, 128]], offset=slot * 256 + s * 128)
                            rhs = sap(rot, [[32768, 128], [1, 512]], offset=fc * 1024 + jh * 512)
                            deps = [gelu_toks[fc], w2ch]
                            if fc == 0:
                                deps += [y_free[2 * s + jh], score_free[jh],
                                         sums_free[jh], wout_free45, wout_free67]
                            is_last = (s == 1 and jh == 1)
                            ylast = mm(pk(ybk, 0, 512), lhsT, rhs, fc == 0, False,
                                       tuple(deps), 1 if is_last else 0, new_gen=(fc == 0))
                    ht_free[slot] = ylast
                    last_B_ymm_tok[fc] = ylast
            mm_flush()
            aoT_free = last_B_ymm_tok[31]
            if tl + 1 < n_b[half]:
                mk_bcast(tl + 1)

            for s in range(2):
                grp = []
                for jh in range(2):
                    ybk = 4 + 2 * s + jh
                    mm(pk(ybk, 0, 512), onesbf[64:65, 0:128],
                       b2row(jh * 512, 512), False, False,
                       (t_b2_lo, t_ones), 0, new_gen=False)
                    tok = mm(pk(ybk, 0, 512), onesbf[64:65, 0:128],
                             b2row(1024 + jh * 512, 512),
                             False, True, (), 1, new_gen=False)
                    grp.append(tok)
                mm_flush()
                epi_data[tl][f"grp{s}"] = grp

        def b_epi1(tl):
            nonlocal wout_free45, wout_free67
            dat = epi_data[tl]
            for s in range(2):
                grp = dat[f"grp{s}"]
                slot = tl * 2 + s
                buf = slot % 2
                pcol = (tl % 2) * 2 + s
                py = bass.AP(ps[2 + s], 0, [[1024, 128], [1, 1024]])
                x1src = sap(x1h, [[8192, 128], [1, 1024]], offset=slot * 1024)
                y2ap = (qT if buf == 0 else vtok)[:, :].bitcast(F32)
                stb = stzb[(tl % 2) * 2 + s]
                tok_z2 = VV(lambda e, py=py, x1src=x1src, y2ap=y2ap, pcol=pcol, stb=stb:
                            e.scalar_tensor_tensor(y2ap, py, p7[:, pcol:pcol + 1], x1src,
                                                   ALU.mult, ALU.add,
                                                   accum_out=stb[:, 0:1]),
                            tuple(grp) + (dat["p7"][s], y2_free[buf],
                                          x1slot_ready[slot]),
                            1, True)
                dat[f"z2_{s}"] = tok_z2
                x1slot_consumed[slot] = [tok_z2]
                y_free[2 * s] = tok_z2
                y_free[2 * s + 1] = tok_z2
                if s == 0:
                    wout_free45 = tok_z2
                else:
                    wout_free67 = tok_z2

        def b_epi2a(tl):
            dat = epi_data[tl]
            for s in range(2):
                buf = (tl * 2 + s) % 2
                y2ap = (qT if buf == 0 else vtok)[:, :].bitcast(F32)
                scratch = kTs[:, :].bitcast(F32)
                tok_ve, _ = ln_stats_a(y2ap, scratch, dat[f"z2_{s}"],
                                       st=stzb[(tl % 2) * 2 + s])
                dat[f"ve_{s}"] = tok_ve

        def b_epi2b(tl):
            tile = half * 4 + tl
            t0 = tile * CS
            final = (half == n_halves - 1 and tl == n_b[half] - 1)
            dat = epi_data[tl]
            for s in range(2):
                buf = (tl * 2 + s) % 2
                y2ap = (qT if buf == 0 else vtok)[:, :].bitcast(F32)
                stb = stzb[(tl % 2) * 2 + s]
                tok_rs = ln_stats_b(dat[f"ve_{s}"], st=stb)
                tok_o = VV(lambda e, y2ap=y2ap, stb=stb:
                           e.tensor_scalar(y2ap, y2ap, stb[:, 1:2],
                                           stb[:, 7:8],
                                           ALU.subtract, ALU.mult),
                           (tok_rs,), 1)
                dst = bass.AP(out_d, (t0 + s * 128) * 1024, [[1024, 128], [1, 1024]])
                if final:
                    tok_dma = E.op("Y", lambda e, d=dst, s_=y2ap: e.dma_start(d, s_),
                                   (tok_o,), 16, f"OF{s}", True)
                else:
                    tok_dma = dmaO(dst, y2ap, (tok_o,),
                                   sem=("O" if buf == 0 else "O2"))
                y2_free[buf] = tok_dma

        epi_data = {}
        for tl in range(n_b[half]):
            b_head1(tl)
            if tl >= 2:
                b_epi2b(tl - 2)
            if tl > 0:
                b_epi1(tl - 1)
                b_epi2a(tl - 1)
            b_head2(tl)
        if n_b[half]:
            L = n_b[half] - 1
            if L >= 1:
                b_epi2b(L - 1)
            b_epi1(L)
            b_epi2a(L)
            if half + 1 < n_halves:
                a_cast[(4, 0)] = mk_acast(4, 0)
                a_cast[(4, 1)] = mk_acast(4, 1)
            b_epi2b(L)

    E.op("Y", lambda e: e.nop(),
         (("O", E.cnt["O"], True), ("O2", E.cnt["O2"], True),
          ("OF0", E.cnt["OF0"], True), ("OF1", E.cnt["OF1"], True)), 0)

    with nc.Block() as block, nc.allow_low_precision(
            reason="bf16 matmul operands by design; fp32 accumulation stays in PSUM"):
        @block.tensor
        def _(eng):
            E.replay("T", eng, sems)

        @block.vector
        def _(eng):
            E.replay("V", eng, sems)

        @block.scalar
        def _(eng):
            E.replay("S", eng, sems)

        @block.gpsimd
        def _(eng):
            E.replay("G", eng, sems)

        @block.sync
        def _(eng):
            E.replay("Y", eng, sems)

    return nc


_NC_CACHE = None


def kernel(L0_attn_x, L0_attn_Wqkv, L0_attn_Wout, L0_attn_bout, L0_attn_Wl1,
           L0_attn_bl1, L0_attn_Wl2, L0_attn_bl2, L0_ffn_Wr, L0_ffn_W1, L0_ffn_b1,
           L0_ffn_W2, L0_ffn_b2):
    nc = build_nc()
    x = np.ascontiguousarray(np.asarray(L0_attn_x, dtype=np.float32))
    common = {
        "wqkv": np.ascontiguousarray(np.asarray(L0_attn_Wqkv, np.float32)),
        "wout": np.ascontiguousarray(np.asarray(L0_attn_Wout, np.float32)),
        "bout": np.ascontiguousarray(np.asarray(L0_attn_bout, np.float32)),
        "wl1": np.ascontiguousarray(np.asarray(L0_attn_Wl1, np.float32)),
        "bl1": np.ascontiguousarray(np.asarray(L0_attn_bl1, np.float32)),
        "wl2": np.ascontiguousarray(np.asarray(L0_attn_Wl2, np.float32)),
        "bl2": np.ascontiguousarray(np.asarray(L0_attn_bl2, np.float32)),
        "wr": np.ascontiguousarray(np.asarray(L0_ffn_Wr, np.float32)),
        "w1": np.ascontiguousarray(np.asarray(L0_ffn_W1, np.float32)[NE - 1]),
        "b1": np.ascontiguousarray(np.asarray(L0_ffn_b1, np.float32)[NE - 1]),
        "w2": np.ascontiguousarray(np.asarray(L0_ffn_W2, np.float32)[NE - 1]),
        "b2": np.ascontiguousarray(np.asarray(L0_ffn_b2, np.float32)[NE - 1]),
    }
    in_maps = [dict(common, x=np.ascontiguousarray(x[b])) for b in range(8)]
    res = run_bass_kernel_spmd(nc, in_maps, core_ids=list(range(8)))
    out = np.stack([np.asarray(res.results[b]["out"], dtype=np.float32)
                    for b in range(8)], axis=0)
    return out


# version 19 (final)
# speedup vs baseline: 1.0264x; 1.0189x over previous
import numpy as np
import ml_dtypes
import concourse.bass as bass
import concourse.mybir as mybir
from concourse.bass_utils import run_bass_kernel_spmd

F32 = mybir.dt.float32
BF16 = mybir.dt.bfloat16
AF = mybir.ActivationFunctionType
ALU = mybir.AluOpType
AX = mybir.AxisListType

T = 2048
C = 1024
H = 16
CS = 256
FF = 4096
NE = 8
NG = 8

SEMS = ("T", "V", "S", "G", "Y", "O", "CONST", "BIAS", "WR", "RIMG", "ROTL",
        "GS", "XR0", "XR1", "XB0", "XB1", "SG0", "SG1",
        "W2C0", "W2C1", "W2C2", "W2C3", "W2C4", "W2C5", "W2C6", "W2C7",
        "W2I0", "W2I1", "W2I2", "W2I3", "W2I4", "W2I5", "W2I6", "W2I7", "O2",
        "SG2")


class Em:
    def __init__(self):
        self.q = {e: [] for e in ("T", "V", "S", "G", "Y")}
        self.cnt = {s: 0 for s in SEMS}
        self.synced = {e: {s: 0 for s in SEMS} for e in self.q}

    def op(self, eng, fn, deps=(), inc=0, inc_sem=None, hazard=False):
        need = {}
        for d in deps:
            if d is None:
                continue
            sem, val, dhaz = d
            if sem == eng and not dhaz:
                continue
            if val > need.get(sem, -1):
                need[sem] = val
        waits = []
        for sem, val in need.items():
            if self.synced[eng][sem] >= val:
                continue
            waits.append((sem, val))
            self.synced[eng][sem] = val
        if inc:
            s = inc_sem or eng
            self.cnt[s] += inc
            self.q[eng].append((waits, fn, s, inc))
            return (s, self.cnt[s], hazard)
        self.q[eng].append((waits, fn, None, 0))
        return None

    def replay(self, eng_name, eng, sems):
        for waits, fn, incsem, amt in self.q[eng_name]:
            for sem, val in waits:
                eng.wait_ge(sems[sem], val)
            inst = fn(eng)
            if incsem is not None:
                inst.then_inc(sems[incsem], amt)


def sap(t, dims, offset=0):
    return bass.AP(t, offset, [list(d) for d in dims])


def build_nc(n_halves=2, n_a=(4, 4), n_b=(4, 4), dump=None, attn_serial=False, no_gs=False, hwstag=False, skips=()):
    if isinstance(n_a, int):
        n_a = (n_a, n_a)
    if isinstance(n_b, int):
        n_b = (n_b, n_b)
    nc = bass.Bass("TRN2", target_bir_lowering=False,
                   dynamic_dma_scratch_size=16384)

    x_d = nc.dram_tensor("x", [T, C], F32, kind="ExternalInput")
    wqkv_d = nc.dram_tensor("wqkv", [3 * C, C], F32, kind="ExternalInput")
    wout_d = nc.dram_tensor("wout", [C, C], F32, kind="ExternalInput")
    bout_d = nc.dram_tensor("bout", [C], F32, kind="ExternalInput")
    wl1_d = nc.dram_tensor("wl1", [64, 1], F32, kind="ExternalInput")
    bl1_d = nc.dram_tensor("bl1", [64], F32, kind="ExternalInput")
    wl2_d = nc.dram_tensor("wl2", [H, 64], F32, kind="ExternalInput")
    bl2_d = nc.dram_tensor("bl2", [H], F32, kind="ExternalInput")
    wr_d = nc.dram_tensor("wr", [NE, C], F32, kind="ExternalInput")
    w1_d = nc.dram_tensor("w1", [C, FF], F32, kind="ExternalInput")
    b1_d = nc.dram_tensor("b1", [FF], F32, kind="ExternalInput")
    w2_d = nc.dram_tensor("w2", [FF, C], F32, kind="ExternalInput")
    b2_d = nc.dram_tensor("b2", [C], F32, kind="ExternalInput")
    out_d = nc.dram_tensor("out", [T, C], F32, kind="ExternalOutput")
    dbg_d = nc.dram_tensor("dbg", [128, 8192], F32, kind="ExternalOutput") if dump else None
    dball = {}
    if dump == "all":
        for nm, dt_ in (("d_xT", BF16), ("d_qT", BF16), ("d_kTs", BF16), ("d_vtok", BF16),
                        ("d_eT", BF16), ("d_aoT", BF16), ("d_lam", BF16), ("d_rot", BF16),
                        ("d_z1", F32), ("d_x1h", F32), ("d_st", F32)):
            shape = [128, 32768] if nm == "d_rot" else ([128, 2048] if dt_ == BF16 else [128, 2048])
            if nm == "d_lam":
                shape = [128, 512]
            if nm == "d_z1":
                shape = [128, 1024]
            if nm == "d_st":
                shape = [128, 16]
            if nm == "d_x1h":
                shape = [128, 8192]
            dball[nm] = nc.dram_tensor(nm, shape, dt_, kind="ExternalOutput")

    rotimg_d = nc.dram_tensor("rotimg", [128, 32768], BF16, kind="Internal")
    w2img_d = nc.dram_tensor("w2img", [128, 32768], BF16, kind="Internal")

    ident_bf_d = nc.inline_tensor(np.eye(128, dtype=ml_dtypes.bfloat16), name="cidbf")
    ident_f_d = nc.inline_tensor(np.eye(32, dtype=np.float32), name="cidf")
    ones_bf_d = nc.inline_tensor(np.ones((128, 128), dtype=ml_dtypes.bfloat16), name="conesbf")
    sel_np = np.zeros((H, 8 * 128), dtype=ml_dtypes.bfloat16)
    for cc in range(8):
        for p in range(128):
            sel_np[2 * cc + p // 64, cc * 128 + p] = 1.0
    sel_d = nc.inline_tensor(sel_np, name="cselbf")

    A = nc.alloc_sbuf_tensor
    rot = A("rot", [128, 32768], BF16)
    w1a = A("w1a", [128, 32768], BF16)
    x1h = A("x1h", [128, 8192], F32)
    xcast = A("xcast", [128, 2048], BF16)
    xT = A("xT", [128, 2048], BF16)
    xf32 = A("xf32", [128, 3072], F32)
    qT = A("qT", [128, 2048], BF16)
    kTs = A("kTs", [128, 2048], BF16)
    vtok = A("vtok", [128, 2048], BF16)
    eT = A("eT", [128, 2048], BF16)
    aoT = A("aoT", [128, 2048], BF16)
    lam_sb = A("lam_sb", [128, 512], BF16)
    stag = nc.alloc_sbuf_tensor_at("stag", [128, 1024], BF16,
                                   offset=nc.lookup_mloc(eT).addr)
    lambf = A("lambf", [16, 2048], BF16)
    identbf = A("identbf", [128, 128], BF16)
    identf = nc.alloc_sbuf_tensor_at("identf", [32, 32], F32,
                                     offset=nc.lookup_mloc(eT).addr + 2624)
    onesbf = A("onesbf", [128, 128], BF16)
    selbf = A("selbf", [16, 1024], BF16)

    b1col = A("b1col", [128, 32], F32)
    bl1col = nc.alloc_sbuf_tensor_at("bl1col", [64, 1], F32,
                                     offset=nc.lookup_mloc(eT).addr + 2752)
    bl2col = nc.alloc_sbuf_tensor_at("bl2col", [16, 1], F32,
                                     offset=nc.lookup_mloc(eT).addr + 2784)
    stz = A("stz", [128, 8], F32)
    wl1row = nc.alloc_sbuf_tensor_at("wl1row", [1, 64], F32,
                                     offset=nc.lookup_mloc(eT).addr + 2048)
    wl2nat = nc.alloc_sbuf_tensor_at("wl2nat", [16, 64], F32,
                                     offset=nc.lookup_mloc(eT).addr + 2304)
    wl2T = nc.alloc_sbuf_tensor_at("wl2T", [64, 16], F32,
                                   offset=nc.lookup_mloc(eT).addr + 2560)
    wrT = A("wrT", [128, 64], BF16)
    _et2 = nc.lookup_mloc(eT).addr
    er = nc.alloc_sbuf_tensor_at("er", [128, 8], F32, offset=_et2 + 2816)
    p7 = nc.alloc_sbuf_tensor_at("p7", [128, 4], F32, offset=_et2 + 2848)
    rmax = nc.alloc_sbuf_tensor_at("rmax", [128, 1], F32, offset=_et2 + 2880)
    rsum = nc.alloc_sbuf_tensor_at("rsum", [128, 1], F32, offset=_et2 + 2912)
    rrec = nc.alloc_sbuf_tensor_at("rrec", [128, 1], F32, offset=_et2 + 2944)
    stzb = [nc.alloc_sbuf_tensor_at(f"stzb{i}", [128, 8], F32,
                                    offset=_et2 + 2976 + 32 * i) for i in range(4)]
    cconst = A("cconst", [128, 2], F32)
    hT = nc.alloc_sbuf_tensor_at("hT", [128, 2048], BF16,
                                 offset=nc.lookup_mloc(aoT).addr)
    _lam_addr = nc.lookup_mloc(lambf).addr
    boutrow_t = nc.alloc_sbuf_tensor_at("boutrow", [33, 2048], BF16, offset=_lam_addr)
    b2row_t = nc.alloc_sbuf_tensor_at("b2row", [65, 2048], BF16, offset=_lam_addr)
    _sel_addr = nc.lookup_mloc(selbf).addr
    rTbf_t = nc.alloc_sbuf_tensor_at("rTbf", [33, 1024], BF16, offset=_sel_addr)
    boutrow = lambda c0, n: sap(boutrow_t, [[2048, 1], [1, n]], offset=32 * 2048 + c0)
    b2row = lambda c0, n: sap(b2row_t, [[2048, 1], [1, n]], offset=64 * 2048 + c0)
    rTbf = lambda c0, n: sap(rTbf_t, [[1024, 1], [1, n]], offset=32 * 1024 + c0)

    ps = [nc.alloc_psum_tensor(f"ps{i}", [128, 1024], F32) for i in range(4)]

    def pk(bank, c0, n, p0=0, np_=128):
        return bass.AP(ps[bank // 2], p0 * 1024 + (bank % 2) * 512 + c0,
                       [[1024, np_], [1, n]])

    sems = {s: nc.alloc_semaphore(name=f"sem{s}") for s in SEMS}
    E = Em()

    def TT_(fn, deps=(), inc=0, hz=False):
        return E.op("T", fn, deps, inc, None, hz)

    def VV(fn, deps=(), inc=0, hz=True):
        return E.op("V", fn, deps, inc, None, True)

    def SS(fn, deps=(), inc=0, hz=True):
        return E.op("S", fn, deps, inc, None, True)

    def GG(fn, deps=(), inc=0, hz=True):
        return E.op("G", fn, deps, inc, None, True)

    def dmaG(dst, src, deps=(), sem="G"):
        return E.op("G", lambda e, d=dst, s_=src: e.dma_start(d, s_), deps, 16, sem, True)

    def dmaY(dst, src, deps=(), sem="Y"):
        return E.op("Y", lambda e, d=dst, s_=src: e.dma_start(d, s_), deps, 16, sem, True)

    def dmaO(dst, src, deps=(), sem="O"):
        return E.op("G", lambda e, d=dst, s_=src: e.dma_start(d, s_), deps, 16, sem, True)

    pendT = {"amt": 0}

    def mm(out_ap, lhsT, rhs, first, last, deps=(), inc=0, new_gen=False,
           flush=False):
        st = bool(first and new_gen)

        def fn(e, o=out_ap, l=lhsT, r=rhs, s_=st, la=last):
            return e.matmul(o, l, r, start=s_, stop=la, skip_group_check=True)
        tok = None
        attach = 0
        if inc:
            attach = pendT["amt"]
            pendT["amt"] = inc
            E.cnt["T"] += inc
            tok = ("T", E.cnt["T"], False)
        need = {}
        for d in deps:
            if d is None:
                continue
            sem, val, dhaz = d
            if sem == "T" and not dhaz:
                continue
            if val > need.get(sem, -1):
                need[sem] = val
        waits = []
        for sem, val in need.items():
            if E.synced["T"][sem] >= val:
                continue
            waits.append((sem, val))
            E.synced["T"][sem] = val
        E.q["T"].append((waits, fn, "T" if attach else None, attach))
        if flush:
            mm_flush()
        return tok

    def mm_flush():
        if pendT["amt"]:
            a = pendT["amt"]
            pendT["amt"] = 0
            E.q["T"].append(([], lambda e: e.drain(), "T", a))

    t_id = dmaY(identbf[:, :], ident_bf_d.ap(), sem="CONST")
    t_idf = dmaY(identf[:, :], ident_f_d.ap(), sem="CONST")
    t_ones = dmaY(onesbf[:, :], ones_bf_d.ap(), sem="CONST")
    t_sel = dmaY(selbf[:, :], sel_d.ap(), sem="CONST")
    t_wl1 = dmaY(sap(wl1row, [[64, 1], [1, 64]]), bass.AP(wl1_d, 0, [[64, 1], [1, 64]]), sem="CONST")
    t_bl1 = dmaY(sap(bl1col, [[1, 64], [1, 1]]), bass.AP(bl1_d, 0, [[1, 64], [1, 1]]), sem="CONST")
    t_wl2 = dmaY(wl2nat[:, :], wl2_d.ap(), sem="CONST")
    t_bl2 = dmaY(sap(bl2col, [[1, 16], [1, 1]]), bass.AP(bl2_d, 0, [[1, 16], [1, 1]]), sem="CONST")
    wrnat = None
    stag_f = stag[:, :].bitcast(F32)
    t_b1n = dmaY(sap(stag, [[1024, 32], [1, 256]]).bitcast(F32),
                 bass.AP(b1_d, 0, [[128, 32], [1, 128]]), sem="CONST")
    rowtmp = sap(x1h, [[8192, 1], [1, 1024]], offset=64 * 8192 + 2048)
    t_bout = dmaY(rowtmp, bass.AP(bout_d, 0, [[1024, 1], [1, 1024]]), sem="CONST")
    tok_const = ("CONST", E.cnt["CONST"], True)
    t_id = t_idf = t_ones = t_sel = t_wl1 = t_bl1 = t_wl2 = t_bl2 = tok_const
    t_b1n = t_bout = tok_const

    t_c12 = VV(lambda e: e.memset(cconst[:, 0:1], -12.0), (), 1, True)
    t_c18 = VV(lambda e: e.memset(cconst[:, 1:2], 0.125), (), 1, True)

    SKIP = set(skips)
    logpos = sap(x1h, [[8192, 1], [1, 2048]], offset=0)
    if "lam" in SKIP:
        t_iota = t_logp = t_lambf = None
        t_h1 = [None] * 4
        t_lam = [None] * 4
        t_wl2T = None
    t_iota = GG(lambda e: e.iota(logpos, [[1, 2048]], base=0, channel_multiplier=0,
                                 allow_small_or_imprecise_dtypes=True), (), 1, True) if "lam" not in SKIP else None
    if "lam" not in SKIP:
        t_logp = SS(lambda e: e.activation(logpos, logpos, AF.Ln, bias=1.0, scale=1.0),
                    (t_iota,), 1)

    t_h1 = []
    for i in range(0 if "lam" in SKIP else 4):
        prev_ev = t_h1[-1] if t_h1 else None
        tokm = mm(pk(0, 0, 512, 0, 64), wl1row[0:1, :],
                  sap(x1h, [[8192, 1], [1, 512]], offset=i * 512),
                  True, True, (t_wl1, t_logp, prev_ev), 1, new_gen=True, flush=True)
        toke = SS(lambda e, i=i: e.activation(
            sap(x1h, [[8192, 64], [1, 512]], offset=2048 + i * 512),
            pk(0, 0, 512, 0, 64), AF.Relu, bias=bl1col[:, :], scale=1.0),
            (tokm, t_bl1), 1)
        t_h1.append(toke)

    if "lam" not in SKIP:
        t_wl2T_mm = mm(pk(2, 0, 16, 0, 64), wl2nat[:, :], identf[0:16, 0:16],
                       True, True, (t_wl2, t_idf), 1, new_gen=True, flush=True)
        t_wl2T = VV(lambda e: e.tensor_copy(wl2T[:, :], pk(2, 0, 16, 0, 64)), (t_wl2T_mm,), 1)
    else:
        t_wl2T = None

    t_lam = []
    for i in range(0 if "lam" in SKIP else 4):
        prev_ev = t_lam[-1] if t_lam else None
        tokm = mm(pk(1, 0, 512, 0, 16), wl2T[:, :],
                  sap(x1h, [[8192, 64], [1, 512]], offset=2048 + i * 512),
                  True, True, (t_wl2T, t_h1[i], prev_ev), 1, new_gen=True, flush=True)
        toke = SS(lambda e, i=i: e.activation(
            sap(x1h, [[8192, 16], [1, 512]], offset=96 * 8192 + 2048 + i * 512),
            pk(1, 0, 512, 0, 16), AF.Sigmoid, bias=bl2col[:, :], scale=1.0),
            (tokm, t_bl2), 1)
        t_lam.append(toke)
    t_lambf = VV(lambda e: e.tensor_copy(
        lambf[:, :], sap(x1h, [[8192, 16], [1, 2048]], offset=96 * 8192 + 2048)),
        tuple(t_lam), 1) if "lam" not in SKIP else None

    x1h_bf = x1h[:, :].bitcast(BF16)
    hi_bf = x1h_bf[64:65, 6144:7168]
    lo_bf = x1h_bf[64:65, 7168:8192]
    hilo_bf = x1h_bf[64:65, 6144:8192]
    if "bias" in SKIP:
        t_bo_hi = t_bo_lo0 = t_bo_lo = t_b2r = t_b2_hi = t_b2_lo0 = t_b2_lo = None
        tok_bias = None
    t_bo_hi = VV(lambda e: e.tensor_copy(hi_bf, rowtmp), (t_bout,), 1) if "bias" not in SKIP else None
    if "bias" not in SKIP:
        t_bo_lo0 = VV(lambda e: e.tensor_sub(lo_bf, rowtmp, hi_bf), (t_bo_hi,), 1)
        t_bo_lo = dmaY(boutrow(0, 2048), hilo_bf, (t_bo_lo0,), sem="BIAS")
        t_b2r = dmaY(rowtmp, bass.AP(b2_d, 0, [[1024, 1], [1, 1024]]), (t_bo_lo,), sem="BIAS")
        t_b2_hi = VV(lambda e: e.tensor_copy(hi_bf, rowtmp), (t_b2r, t_bo_lo), 1)
        t_b2_lo0 = VV(lambda e: e.tensor_sub(lo_bf, rowtmp, hi_bf), (t_b2_hi,), 1)
        t_b2_lo = dmaY(b2row(0, 2048), hilo_bf, (t_b2_lo0,), sem="BIAS")
        tok_bias = ("BIAS", E.cnt["BIAS"], True)
        t_bo_lo = t_b2_lo = tok_bias

    t_b1m = mm(pk(3, 0, 32), stag_f[0:32, 0:128], identf[0:32, 0:32],
               True, True, (t_b1n, t_idf), 1, new_gen=True, flush=True)
    t_b1 = VV(lambda e: e.tensor_copy(b1col[:, :], pk(3, 0, 32)), (t_b1m,), 1)
    t_wr = dmaY(sap(x1h, [[8192, 8], [1, 1024]], offset=4096), wr_d.ap(),
                (t_b1,), sem="WR")

    t_wrT = []
    prev_wrev = t_b1
    for cc in range(8):
        tokm = mm(pk(3, 0, 8), sap(x1h, [[8192, 8], [1, 128]], offset=4096 + cc * 128),
                  identf[0:8, 0:8],
                  True, True, (t_wr, t_idf, prev_wrev), 1, new_gen=True, flush=True)
        tokc = VV(lambda e, cc=cc: e.tensor_copy(wrT[:, cc * 8:(cc + 1) * 8], pk(3, 0, 8)),
                  (tokm,), 1)
        prev_wrev = tokc
        t_wrT.append(tokc)

    stag2 = nc.alloc_sbuf_tensor_at("stag2", [128, 1024], BF16,
                                    offset=nc.lookup_mloc(aoT).addr)
    stag3 = nc.alloc_sbuf_tensor_at("stag3", [128, 1024], BF16,
                                    offset=nc.lookup_mloc(aoT).addr + 2048)
    stags = [stag, stag2, stag3]
    stag_tok = [t_wrT[-1], None, None]
    tr45_free = [None, None]
    rot_evac = []
    jc_glob = [0]

    def load_transpose(jc_total, src_dram, dst_base, dst_ccstride):
        for jc in range(jc_total):
            sl = jc_glob[0] % 3
            jc_glob[0] += 1
            src = bass.AP(src_dram, jc * 128 * 1024, [[1024, 128], [1, 1024]])
            dst = sap(stags[sl], [[1024, 128], [1, 1024]])
            tdma = dmaG(dst, src, (stag_tok[sl],), sem=f"SG{sl}")
            last_mm = None
            for g in range(2):
                bank = 4 + g
                mts = []
                for b in range(4):
                    cc = g * 4 + b
                    lhsT = sap(stags[sl], [[1024, 128], [1, 128]], offset=cc * 128)
                    deps = (tdma, t_id, tr45_free[g]) if b == 0 else ()
                    tokm = mm(pk(bank, b * 128, 128), lhsT, identbf[:, :],
                              b == 0, b == 3, deps, 1 if b == 3 else 0, new_gen=(b == 0),
                              flush=(b == 3))
                    mts.append(tokm)
                dst_ap = sap(rot, [[32768, 128], [dst_ccstride, 4], [1, 128]],
                             offset=dst_base + (g * 4) * dst_ccstride + jc * 128)
                src_ap = bass.AP(ps[2], (bank % 2) * 512, [[1024, 128], [128, 4], [1, 128]])
                tokev = VV(lambda e, d=dst_ap, s_=src_ap: e.tensor_copy(d, s_),
                           (mts[-1],), 1)
                tr45_free[g] = tokev
                rot_evac.append(tokev)
                last_mm = mts[-1]
            stag_tok[sl] = last_mm

    xft = [None] * 16
    x1tok = [None] * 16

    def xf_load(k, dep):
        slot = k % 3
        dst = sap(xf32, [[3072, 128], [1, 1024]], offset=slot * 1024)
        src = bass.AP(x_d, k * 128 * 1024, [[1024, 128], [1, 1024]])
        xft[k] = dmaY(dst, src, (dep,), sem=("XR0", "XR1", "XB0")[k % 3])

    for k0 in range(3):
        xf_load(k0, None)

    load_transpose(24, wqkv_d, 0, 3072)
    load_transpose(8, wout_d, 24576, 1024)
    rot_ready0 = tuple(rot_evac[-2:])

    t_w1 = []
    for q4 in range(0 if no_gs else 4):
        for cc in range(8):
            dst = sap(w1a, [[32768, 128], [1, 1024]], offset=cc * 4096 + q4 * 1024)
            src = bass.AP(w1_d, cc * 128 * 4096 + q4 * 1024, [[4096, 128], [1, 1024]])
            t_w1.append(dmaG(dst, src, sem="GS"))
    tok_w1 = ("GS", E.cnt["GS"], True)

    t_w2img = []
    for fc in range(0 if no_gs else 32):
        dst = bass.AP(w2img_d, fc * 1024, [[32768, 128], [1, 1024]])
        src = bass.AP(w2_d, fc * 128 * 1024, [[1024, 128], [1, 1024]])
        t_w2img.append(dmaG(dst, src, sem=f"W2I{fc // 4}"))
    t_w2img = [t_w2img[4 * g + 3] for g in range(8)] if t_w2img else []
    t_rotimg = dmaG(rotimg_d.ap(), rot[:, :], tuple(rot_evac), sem="RIMG") \
        if "rimg" not in SKIP else None
    tok_gs = ("GS", E.cnt["GS"], True)
    if no_gs:
        tok_w1 = tok_gs

    x1slot_consumed = [[] for _ in range(8)]
    x1slot_consumed[0] = [t_lambf]
    x1slot_consumed[1] = [t_lambf]
    x1slot_consumed[2] = [t_lambf, t_b2_lo]
    x1slot_consumed[3] = [t_lambf, t_b2_lo]
    x1slot_consumed[4] = [t_wrT[-1]]
    x1slot_ready = [None] * 8
    xT_free = None
    xcast_free = None
    y2_free = [None, None]
    tr_free = [t_h1[-1], t_lam[-1]]
    qk_free = [t_wl2T, t_wrT[-1]]
    wout_free45 = rot_evac[-1]
    wout_free67 = None
    last_sdA = None
    last_tailA = None
    a_cast = {}
    a_epi = {}
    b_cast = {}
    h_free = [None] * 4
    hbank_free = [None] * 4
    y_free = [None, None, None, None]
    ht_free = [None] * NG
    router_free = [None, None]
    av_free = [None, None]
    score_free = [None, None]
    sums_free = [None, None]
    eT_free = [None, None, None, None]
    aoT_free = None
    last_sd = None
    last_r0 = None
    last_te = None
    last_tp = None
    last_A_qkv_tok = None
    last_A_wout_tok = None
    last_B_ymm_tok = [None] * 32
    rot_ready = {0: rot_ready0}

    def ln_stats_a(z_ap, scratch_ap, stt_tok, extra_var_deps=(), st=None):
        nonlocal last_sd
        if st is None:
            st = stz
        tok_mu = VV(lambda e, st=st: e.tensor_scalar_mul(st[:, 1:2], st[:, 0:1],
                                                         1.0 / 1024.0),
                    (stt_tok,), 1, True)
        tok_var = VV(lambda e, z=z_ap, sc=scratch_ap, st=st:
                     e.scalar_tensor_tensor(sc, z, st[:, 1:2], z, ALU.subtract, ALU.mult,
                                            accum_out=st[:, 2:3]),
                     (tok_mu,) + tuple(extra_var_deps), 1, True)
        tok_ve = VV(lambda e, st=st: e.tensor_scalar(st[:, 3:4], st[:, 2:3], 1.0 / 1024.0,
                                                     1e-5, ALU.mult, ALU.add),
                    (tok_var, last_sd), 1, True)
        return tok_ve, tok_var

    def ln_stats_b(tok_ve, st=None):
        nonlocal last_sd, last_r0
        if st is None:
            st = stz
        tok_sd = SS(lambda e, st=st: e.activation(st[:, 4:5], st[:, 3:4], AF.Sqrt,
                                                  bias=0.0, scale=1.0),
                    (tok_ve, last_r0), 1)
        tok_r0 = VV(lambda e, st=st: e.reciprocal(st[:, 5:6], st[:, 4:5]),
                    (tok_sd,), 1, True)
        tok_aa = VV(lambda e, st=st: e.tensor_mul(st[:, 6:7], st[:, 5:6], st[:, 5:6]),
                    (tok_r0,), 1, True)
        tok_aa2 = VV(lambda e, st=st: e.tensor_mul(st[:, 6:7], st[:, 6:7], st[:, 3:4]),
                     (tok_aa,), 1, True)
        tok_aa3 = VV(lambda e, st=st: e.tensor_scalar(st[:, 6:7], st[:, 6:7], -0.5, 1.5,
                                                      ALU.mult, ALU.add),
                     (tok_aa2,), 1, True)
        tok_rs = VV(lambda e, st=st: e.tensor_mul(st[:, 7:8], st[:, 5:6], st[:, 6:7]),
                    (tok_aa3,), 1, True)
        last_sd = tok_sd
        last_r0 = tok_r0
        return tok_rs

    def ln_stats(z_ap, scratch_ap, stt_tok, extra_var_deps=(), st=None):
        tok_ve, tok_var = ln_stats_a(z_ap, scratch_ap, stt_tok, extra_var_deps, st)
        tok_rs = ln_stats_b(tok_ve, st)
        return tok_rs, tok_var

    for half in range(n_halves):
        if half == 1:
            toks = []
            for ch in range(8):
                gate = last_B_ymm_tok[4 * ch + 3]
                dst = sap(rot, [[32768, 128], [1, 4096]], offset=ch * 4096)
                src = bass.AP(rotimg_d, ch * 4096, [[32768, 128], [1, 4096]])
                toks.append(dmaY(dst, src, (gate, t_rotimg), sem="ROTL"))
            rot_ready[1] = tuple(toks)
        rr = rot_ready[half]


        def a_epi_tail(ti):
            nonlocal last_sdA, last_tailA
            ve_both = (a_epi[(ti, 0)][0], a_epi[(ti, 1)][0])
            for sb in range(2):
                tok_ve, x1dst, slot = a_epi.pop((ti, sb))
                tok_sd = SS(lambda e, sb=sb: e.activation(stz[:, 0:1],
                                                          stz[:, 3 + sb:4 + sb],
                                                          AF.Sqrt, bias=0.0, scale=1.0),
                            ve_both + (last_tailA,), 1)
                tok_r0 = VV(lambda e: e.reciprocal(stz[:, 1:2], stz[:, 0:1]),
                            (tok_sd,), 1, True)
                tok_aa = VV(lambda e: e.tensor_mul(stz[:, 2:3], stz[:, 1:2],
                                                   stz[:, 1:2]),
                            (tok_r0,), 1, True)
                tok_aa2 = VV(lambda e, sb=sb: e.tensor_mul(stz[:, 2:3], stz[:, 2:3],
                                                           stz[:, 3 + sb:4 + sb]),
                             (tok_aa,), 1, True)
                tok_aa3 = VV(lambda e: e.tensor_scalar(stz[:, 2:3], stz[:, 2:3],
                                                       -0.5, 1.5, ALU.mult, ALU.add),
                             (tok_aa2,), 1, True)
                tok_rs = VV(lambda e: e.tensor_mul(stz[:, 5:6], stz[:, 1:2],
                                                   stz[:, 2:3]),
                            (tok_aa3,), 1, True)
                tok_x1 = VV(lambda e, x1dst=x1dst:
                            e.tensor_scalar_mul(x1dst, x1dst, stz[:, 5:6]),
                            (tok_rs,), 1, True)
                x1slot_ready[slot] = tok_x1
                last_sdA = tok_sd
                last_tailA = tok_x1

        def mk_acast(ti, sb):
            kk = ti * 2 + sb
            dst = sap(xcast, [[2048, 128], [1, 1024]], offset=sb * 1024)
            src = sap(xf32, [[3072, 128], [1, 1024]], offset=(kk % 3) * 1024)
            return SS(lambda e, d=dst, s_=src: e.copy(d, s_),
                      (xft[kk], xcast_free), 1)

        for tl in range(n_a[half]):
            tile = half * 4 + tl
            n = tile
            t0 = tile * CS

            k0 = tile * 2
            if 3 <= k0 + 2 < 16:
                xf_load(k0 + 2, x1tok[k0 - 1])
            xbf_toks = []
            for s in range(2):
                k = k0 + s
                if (tile, s) in a_cast:
                    xbf_toks.append(a_cast.pop((tile, s)))
                else:
                    xbf_toks.append(mk_acast(tile, s))

            trev = []
            for g in range(4):
                s, cq = divmod(g, 2)
                bk = g % 2
                mts = []
                for b in range(4):
                    cc = cq * 4 + b
                    lhsT = sap(xcast, [[2048, 128], [1, 128]], offset=s * 1024 + cc * 128)
                    deps = ()
                    if b == 0:
                        deps = (xbf_toks[s], tr_free[bk], av_free[bk], t_id,
                                router_free[bk], hbank_free[bk])
                    tokm = mm(pk(bk, b * 128, 128), lhsT, identbf[:, :],
                              b == 0, b == 3, deps, 1 if b == 3 else 0, new_gen=(b == 0))
                    mts.append(tokm)
                dst_ap = sap(xT, [[2048, 128], [256, 4], [1, 128]],
                             offset=cq * 4 * 256 + s * 128)
                src_ap = bass.AP(ps[0], (bk % 2) * 512, [[1024, 128], [128, 4], [1, 128]])
                tokev = SS(lambda e, d=dst_ap, s_=src_ap: e.copy(d, s_),
                           (mts[-1], xT_free), 1)
                trev.append(tokev)
                tr_free[bk] = tokev
            xT_ready = (trev[-1],)
            xcast_free = trev[-1]

            lam_m = [None] * 4
            lam_copy = [None] * 4
            lam_banks = [4, 5, 6, 7]
            for pr in range(4):
                bkk = lam_banks[pr]
                deps0 = [t_sel, t_lambf]
                deps0 += [wout_free45 if bkk in (4, 5) else wout_free67]
                m0 = mm(pk(bkk, 0, 256), selbf[:, (2 * pr) * 128:(2 * pr + 1) * 128],
                        lambf[:, n * 256:(n + 1) * 256], True, True, tuple(deps0),
                        0, new_gen=True)
                m1 = mm(pk(bkk, 256, 256), selbf[:, (2 * pr + 1) * 128:(2 * pr + 2) * 128],
                        lambf[:, n * 256:(n + 1) * 256], True, True, (), 1, new_gen=False)
                lam_m[pr] = m1

            qev = []
            deps_half = list(rr) + [h_free[0], h_free[1]]
            for pi in range(4):
                bk = pi % 4
                grp_last = None
                for sub_j in range(2):
                    jc = pi * 2 + sub_j
                    for cc in range(8):
                        lhsT = sap(rot, [[32768, 128], [1, 128]], offset=cc * 3072 + jc * 128)
                        rhs = sap(xT, [[2048, 128], [1, 256]], offset=cc * 256)
                        first = (cc == 0)
                        deps = ()
                        if first and sub_j == 0:
                            if bk < 2:
                                bdep = (tr_free[bk], av_free[bk])
                            else:
                                bdep = (qk_free[bk - 2],)
                            deps = tuple(deps_half) + bdep + \
                                (hbank_free[bk],) + xT_ready
                        grp_last = mm(pk(bk, sub_j * 256, 256), lhsT, rhs, first, cc == 7,
                                      deps, 1 if (cc == 7 and sub_j == 1) else 0,
                                      new_gen=(first and sub_j == 0))
                tokev = SS(lambda e, pi=pi, bk=bk:
                           e.activation(qT[:, pi * 512:(pi + 1) * 512], pk(bk, 0, 512),
                                        AF.Copy, bias=0.0, scale=cconst[:, 1:2]),
                           (grp_last, y2_free[0], t_c18), 1)
                qev.append(tokev)

            kev = []
            lam_copy[0] = VV(lambda e: e.tensor_copy(lam_sb[:, 0:512],
                                                     pk(lam_banks[0], 0, 512)),
                             (lam_m[0],), 1)

            for pr in range(4):
                bk = pr % 4
                grp_last = None
                for sub_j in range(2):
                    jc = 8 + pr * 2 + sub_j
                    for cc in range(8):
                        lhsT = sap(rot, [[32768, 128], [1, 128]], offset=cc * 3072 + jc * 128)
                        rhs = sap(xT, [[2048, 128], [1, 256]], offset=cc * 256)
                        first = (cc == 0)
                        deps = (qev[pr],) if (first and sub_j == 0) else ()
                        grp_last = mm(pk(bk, sub_j * 256, 256), lhsT, rhs, first, cc == 7,
                                      deps, 1 if (cc == 7 and sub_j == 1) else 0,
                                      new_gen=(first and sub_j == 0))
                tokk = VV(lambda e, pr=pr, bk=bk:
                          e.tensor_mul(kTs[:, pr * 512:(pr + 1) * 512],
                                       pk(bk, 0, 512),
                                       lam_sb[:, 0:512]),
                          (grp_last, lam_copy[pr]), 1)
                kev.append(tokk)
                if pr + 1 < 4:
                    nxt = pr + 1
                    lam_copy[nxt] = VV(lambda e, nxt=nxt, bkk=lam_banks[nxt]:
                                       e.tensor_copy(lam_sb[:, 0:512],
                                                     pk(bkk, 0, 512)),
                                       (lam_m[nxt], tokk), 1)
            sums_free[0] = lam_copy[2]
            sums_free[1] = lam_copy[3]

            vev = []
            for g in range(4):
                s, dvh = divmod(g, 2)
                bk = 2 + (g % 2)
                grp_last = None
                for cc in range(8):
                    lhsT = sap(xT, [[2048, 128], [1, 128]], offset=cc * 256 + s * 128)
                    rhs = sap(rot, [[32768, 128], [1, 512]], offset=cc * 3072 + 2048 + dvh * 512)
                    deps = ()
                    if cc == 0:
                        deps = (kev[2 + (g % 2)],) if g < 2 else (vev[g - 2],)
                    grp_last = mm(pk(bk, 0, 512), lhsT, rhs, cc == 0, cc == 7,
                                  deps, 1 if cc == 7 else 0, new_gen=(cc == 0))
                tokev = SS(lambda e, s=s, dvh=dvh, bk=bk:
                           e.copy(vtok[:, s * 1024 + dvh * 512:s * 1024 + (dvh + 1) * 512],
                                  pk(bk, 0, 512)),
                           (grp_last, y2_free[1]), 1)
                vev.append(tokev)
            last_A_qkv_tok = vev[-1]
            xT_free = vev[-1]

            prev_scores = {}
            prev_exp = {}
            prev_sums = {}
            prev_recip = {}
            prev_av = {}
            rb_done = {}
            bulk_toks = []
            ao_toks = []

            def do_scores(h):
                par = h % 2
                bk = 4 + par
                cc_h, ho = divmod(h, 2)
                po = ho * 64
                last = None
                for ks in range(2):
                    lhsT = bass.AP(kTs, po * 2048 + cc_h * 256 + ks * 128,
                                   [[2048, 64], [1, 128]])
                    rhs = bass.AP(qT, po * 2048 + cc_h * 256, [[2048, 64], [1, 256]])
                    deps = ()
                    if ks == 0:
                        deps = (score_free[par], kev[cc_h // 2], qev[cc_h // 2],
                                lam_copy[par])
                    last = mm(pk(bk, ks * 256, 256), lhsT, rhs, True, True,
                              deps, 1 if ks == 1 else 0, new_gen=(ks == 0))
                prev_scores[h] = last

            def do_exp(h):
                par = h % 2
                slot = h % 4
                tok = SS(lambda e, par=par, slot=slot:
                         e.activation(bass.AP(eT, slot * 512, [[2048, 128], [1, 512]]),
                                      pk(4 + par, 0, 512), AF.Exp, bias=cconst[:, 0:1], scale=1.0),
                         (prev_scores[h], eT_free[slot], t_c12), 1)
                prev_exp[h] = tok
                score_free[par] = tok

            def do_sums(h):
                par = h % 2
                slot = h % 4
                last = None
                for ks in range(2):
                    rhs = bass.AP(eT, slot * 512 + ks * 256, [[2048, 128], [1, 256]])
                    deps = ()
                    if ks == 0:
                        deps = (prev_exp[h], sums_free[par], t_ones)
                    last = mm(pk(6 + par, 0, 256, 0, 1), onesbf[:, 0:1], rhs,
                              ks == 0, ks == 1, deps, 1 if ks == 1 else 0,
                              new_gen=(ks == 0))
                prev_sums[h] = last

            def do_recip(h):
                par = h % 2
                c0 = (h % 2) * 512 + ((h // 2) % 2) * 256
                tok = VV(lambda e, par=par, c0=c0:
                         e.reciprocal(rTbf(c0, 256),
                                      pk(6 + par, 0, 256, 0, 1)),
                         (prev_sums[h], rb_done.get(h // 4 - 1)), 1, True)
                prev_recip[h] = tok
                sums_free[par] = tok

            def do_bulk(g):
                bk = 2 + (g % 2)
                bdep = vev[2 + (g % 2)] if g < 2 else qk_free[g % 2]
                mm(pk(bk, 0, 512, 0, 64), onesbf[32:33, 0:64], rTbf(0, 512),
                   True, True, (prev_recip[4 * g + 3], bdep), 0,
                   new_gen=True)
                rbtok = mm(pk(bk, 0, 512, 64, 64), onesbf[32:33, 0:64],
                           rTbf(512, 512), True, True, (), 1, new_gen=True)
                rb_done[g] = rbtok
                src = bass.AP(aoT, g * 512, [[2048, 128], [1, 512]])
                rb = bass.AP(ps[1], (g % 2) * 512, [[1024, 128], [1, 512]])
                tok = VV(lambda e, src=src, rb=rb: e.tensor_mul(src, src, rb),
                         (rbtok, ao_toks[-1]), 1)
                qk_free[g % 2] = tok
                bulk_toks.append(tok)

            def do_av(h):
                par = h % 2
                slot = h % 4
                last = None
                for ks in range(2):
                    lhsT = bass.AP(vtok, ks * 1024 + h * 64, [[2048, 128], [1, 64]])
                    rhs = bass.AP(eT, slot * 512 + ks * 256, [[2048, 128], [1, 256]])
                    deps = ()
                    if ks == 0:
                        deps = (prev_exp[h], av_free[par], kev[par],
                                lam_copy[par], lam_copy[2 + par], vev[3],
                                router_free[par])
                    last = mm(pk(par, 0, 256, 0, 64), lhsT, rhs, ks == 0, ks == 1,
                              deps, 1 if ks == 1 else 0, new_gen=(ks == 0))
                prev_av[h] = last
                eT_free[slot] = last

            def do_ao(h):
                par = h % 2
                cc_h, ho = divmod(h, 2)
                dst = bass.AP(aoT, ho * 64 * 2048 + cc_h * 256, [[2048, 64], [1, 256]])
                tok = SS(lambda e, dst=dst, par=par:
                         e.copy(dst, pk(par, 0, 256, 0, 64)),
                         (prev_av[h], aoT_free), 1)
                av_free[par] = tok
                return tok

            if attn_serial:
                for h_ in range(H):
                    do_scores(h_)
                    mm_flush()
                    do_exp(h_)
                    do_sums(h_)
                    mm_flush()
                    do_recip(h_)
                    do_av(h_)
                    mm_flush()
                    ao_toks.append(do_ao(h_))
                    if h_ % 4 == 3:
                        do_bulk(h_ // 4)
                        mm_flush()
            else:
                for step in range(H + 2):
                    if step < H:
                        do_scores(step)
                        do_exp(step)
                    if 2 <= step < H + 2:
                        do_av(step - 2)
                        ao_toks.append(do_ao(step - 2))
                        if (step - 2) % 4 == 3:
                            do_bulk((step - 2) // 4)
                    if 1 <= step < H + 1:
                        do_sums(step - 1)
                        do_recip(step - 1)
            mm_flush()
            if tl + 1 < n_a[half]:
                a_cast[(tile + 1, 0)] = mk_acast(tile + 1, 0)
            if tl >= 1:
                a_epi_tail(tile - 1)

            for s in range(2):
                k = tile * 2 + s
                grp = []
                for jh in range(2):
                    bk = 4 + 2 * s + jh
                    last = None
                    for cc in range(8):
                        lhsT = sap(aoT, [[2048, 128], [1, 128]], offset=cc * 256 + s * 128)
                        rhs = sap(rot, [[32768, 128], [1, 512]],
                                  offset=24576 + cc * 1024 + jh * 512)
                        deps = ()
                        if cc == 0:
                            deps = ((score_free[jh], bulk_toks[-1]) if s == 0 else
                                    (sums_free[0], sums_free[1], bulk_toks[-1]))
                        last = mm(pk(bk, 0, 512), lhsT, rhs, cc == 0, False, deps, 0,
                                  new_gen=(cc == 0))
                    mm(pk(bk, 0, 512), onesbf[32:33, 0:128],
                       boutrow(jh * 512, 512), False, False,
                       (t_bo_lo, t_ones), 0, new_gen=False)
                    last = mm(pk(bk, 0, 512), onesbf[32:33, 0:128],
                              boutrow(1024 + jh * 512, 512),
                              False, True, (), 1, new_gen=False)
                    grp.append(last)
                mm_flush()
                last_A_wout_tok = grp[-1]
                pw = bass.AP(ps[2 + s], 0, [[1024, 128], [1, 1024]])
                z1 = sap(xf32, [[3072, 128], [1, 1024]], offset=(k % 3) * 1024)
                tok_z1 = VV(lambda e, pw=pw, z1=z1:
                            e.scalar_tensor_tensor(z1, pw, 0.0, z1, ALU.add, ALU.add,
                                                   accum_out=stz[:, 0:1]),
                            tuple(grp) + (xft[k], xbf_toks[s], last_sdA), 1, True)
                if s == 0:
                    wout_free45 = tok_z1
                else:
                    wout_free67 = tok_z1
                slot = tl * 2 + s
                tok_mu = VV(lambda e:
                            e.tensor_scalar_mul(stz[:, 1:2], stz[:, 0:1], 1.0 / 1024.0),
                            (tok_z1,), 1, True)
                x1dst = sap(x1h, [[8192, 128], [1, 1024]], offset=slot * 1024)
                tok_p1 = VV(lambda e, z1=z1, x1dst=x1dst:
                            e.tensor_scalar(x1dst, z1, stz[:, 1:2], None,
                                            ALU.subtract),
                            (tok_mu,) + tuple(x1slot_consumed[slot]), 1, True)
                scratch = eT[:, :].bitcast(F32)
                tok_var = VV(lambda e, sc=scratch, x1dst=x1dst:
                             e.scalar_tensor_tensor(sc, x1dst, 0.0, x1dst,
                                                    ALU.add, ALU.mult,
                                                    accum_out=stz[:, 2:3]),
                             (tok_p1, eT_free[0], eT_free[1],
                              eT_free[2], eT_free[3]), 1, True)
                tok_ve = VV(lambda e, s=s:
                            e.tensor_scalar(stz[:, 3 + s:4 + s], stz[:, 2:3],
                                            1.0 / 1024.0, 1e-5, ALU.mult, ALU.add),
                            (tok_var, last_tailA), 1, True)
                a_epi[(tile, s)] = (tok_ve, x1dst, slot)
                x1tok[k] = tok_p1
                if s == 0 and k + 3 < 16:
                    xf_load(k + 3, tok_p1)
                    if tl + 1 < n_a[half]:
                        a_cast[(tile + 1, 1)] = mk_acast(tile + 1, 1)
                for i_ in range(4):
                    eT_free[i_] = tok_var
        if n_b[half]:
            src2 = sap(x1h, [[8192, 128], [1, 2048]])
            b_cast[0] = SS(lambda e, s_=src2: e.copy(xcast[:, :], s_),
                           (x1slot_ready[0], x1slot_ready[1], xcast_free), 1)
            xcast_free = b_cast[0]
        if n_a[half]:
            a_epi_tail(half * 4 + n_a[half] - 1)
        for i in range(NG):
            ht_free[i] = wout_free67

        if dump == "x1h" and half == 0:
            E.op("Y", lambda e: e.dma_start(dbg_d.ap(), x1h[:, :]),
                 tuple(x1slot_ready), 16, "O", True)
        if dump == "ln" and half == 0:
            gates = (x1slot_ready[0], x1slot_ready[1])
            E.op("Y", lambda e: e.dma_start(bass.AP(dbg_d, 0, [[8192, 128], [1, 1024]]),
                                            xf32[:, 0:1024]), gates, 16, "O", True)
            E.op("Y", lambda e: e.dma_start(bass.AP(dbg_d, 2048, [[8192, 128], [1, 2048]]),
                                            x1h[:, 0:2048]), (), 16, "O", True)
            E.op("Y", lambda e: e.dma_start(
                bass.AP(dbg_d, 4096, [[8192, 128], [1, 8]]), stz[:, :],
                allow_slow_non_contiguous=True),
                (), 16, "O", True)
        if dump == "all" and half == 0:
            g0 = tuple(x1slot_ready[0:2])
            E.op("Y", lambda e: e.dma_start(dball["d_xT"].ap(), xT[:, :]), g0, 16, "O", True)
            E.op("Y", lambda e: e.dma_start(dball["d_qT"].ap(), qT[:, :]), (), 16, "O", True)
            E.op("Y", lambda e: e.dma_start(dball["d_kTs"].ap(), kTs[:, :]), (), 16, "O", True)
            E.op("Y", lambda e: e.dma_start(dball["d_vtok"].ap(), vtok[:, :]), (), 16, "O", True)
            E.op("Y", lambda e: e.dma_start(dball["d_eT"].ap(), eT[:, :]), (), 16, "O", True)
            E.op("Y", lambda e: e.dma_start(dball["d_aoT"].ap(), aoT[:, :]), (), 16, "O", True)
            E.op("Y", lambda e: e.dma_start(dball["d_lam"].ap(), lam_sb[:, :]), (), 16, "O", True)
            E.op("Y", lambda e: e.dma_start(dball["d_rot"].ap(), rot[:, :]), (), 16, "O", True)
            E.op("Y", lambda e: e.dma_start(dball["d_z1"].ap(), xf32[:, 0:1024]), (), 16, "O", True)
            E.op("Y", lambda e: e.dma_start(bass.AP(dball["d_x1h"], 0, [[8192, 128], [1, 2048]]),
                                            sap(x1h, [[8192, 128], [1, 2048]])), (), 16, "O", True)
            E.op("Y", lambda e: e.dma_start(
                bass.AP(dball["d_st"], 0, [[16, 128], [1, 1]]),
                sap(stz, [[8, 128], [1, 1]]), allow_slow_non_contiguous=True), (), 16, "O", True)
        if dump == "rot0" and half == 0:
            dbg_bf = dbg_d.ap().bitcast(BF16)
            E.op("Y", lambda e: e.dma_start(dbg_bf[:, 0:12288],
                                            sap(rot, [[32768, 128], [1, 12288]])),
                 rot_ready0, 16, "O", True)
            E.op("Y", lambda e: e.dma_start(dbg_bf[:, 12288:12416], identbf[:, :]),
                 (), 16, "O", True)
            E.op("Y", lambda e: e.dma_start(dbg_bf[:, 12416:13440], stag[:, :]),
                 (), 16, "O", True)
        if dump == "rot" and half == 0:
            gates = (x1slot_ready[0], x1slot_ready[1])
            for cc_ in range(8):
                E.op("G", lambda e, cc_=cc_: e.dma_start(
                    bass.AP(dbg_d, cc_ * 1024, [[8192, 128], [1, 1024]]),
                    sap(rot, [[32768, 128], [1, 1024]], offset=cc_ * 3072)),
                    gates, 16, "O", True)
        if dump == "mid" and half == 0:
            dd = dbg_d.ap()
            gates = tuple(t for t in x1slot_ready if t) + (t_lambf,)
            E.op("G", lambda e: e.dma_start(bass.AP(dbg_d, 0, [[8192, 128], [1, 2048]]), xT[:, :]),
                 gates, 16, "O", True)
            E.op("G", lambda e: e.dma_start(bass.AP(dbg_d, 2048, [[8192, 128], [1, 2048]]), qT[:, :]),
                 (), 16, "O", True)
            E.op("G", lambda e: e.dma_start(bass.AP(dbg_d, 4096, [[8192, 128], [1, 2048]]), kTs[:, :]),
                 (), 16, "O", True)
            E.op("G", lambda e: e.dma_start(bass.AP(dbg_d, 6144, [[8192, 128], [1, 2048]]),
                                            aoT[:, :]),
                 (), 16, "O", True)

        w2_chunks = []
        for ch in range(8 if n_b[half] else 0):
            gate = last_A_qkv_tok if ch < 6 else last_A_wout_tok
            dst = sap(rot, [[32768, 128], [1, 4096]], offset=ch * 4096)
            src = bass.AP(w2img_d, ch * 4096, [[32768, 128], [1, 4096]])
            gimg = t_w2img[ch] if t_w2img else tok_gs
            deps = (gate, gimg, t_rotimg)
            w2_chunks.append(dmaY(dst, src, deps, sem=f"W2C{ch}"))


        def mk_bcast(tl2):
            sl2 = tl2 * 2
            src2 = sap(x1h, [[8192, 128], [1, 2048]], offset=sl2 * 1024)
            b_cast[tl2] = SS(lambda e, s_=src2: e.copy(xcast[:, :], s_),
                             (x1slot_ready[sl2], x1slot_ready[sl2 + 1],
                              xcast_free), 1)

        def b_head1(tl):
            nonlocal xcast_free, xT_ready, last_te, last_tp
            tile = half * 4 + tl
            if tl not in b_cast:
                mk_bcast(tl)
            tok_cast = b_cast[tl]
            xcast_free = tok_cast

            trev = []
            for g in range(4):
                s, cq = divmod(g, 2)
                bk = g % 2
                mts = []
                for b in range(4):
                    cc = cq * 4 + b
                    lhsT = sap(xcast, [[2048, 128], [1, 128]], offset=s * 1024 + cc * 128)
                    deps = ()
                    if b == 0:
                        deps = (tok_cast, tr_free[bk], av_free[bk], router_free[bk],
                                hbank_free[bk])
                    tokm = mm(pk(bk, b * 128, 128), lhsT, identbf[:, :],
                              b == 0, b == 3, deps, 1 if b == 3 else 0, new_gen=(b == 0))
                    mts.append(tokm)
                dst_ap = sap(xT, [[2048, 128], [256, 4], [1, 128]],
                             offset=cq * 4 * 256 + s * 128)
                src_ap = bass.AP(ps[0], (bk % 2) * 512, [[1024, 128], [128, 4], [1, 128]])
                tokev = SS(lambda e, d=dst_ap, s_=src_ap: e.copy(d, s_),
                           (mts[-1], xT_free), 1)
                trev.append(tokev)
                tr_free[bk] = tokev
            mm_flush()
            xT_ready = (trev[-1],)
            xcast_free = trev[-1]

            p7_toks = []
            rtm_toks = []
            for s in range(2):
                bk = s
                last = None
                for cc in range(8):
                    lhsT = sap(xT, [[2048, 128], [1, 128]], offset=cc * 256 + s * 128)
                    rhs = sap(wrT, [[64, 128], [1, 8]], offset=cc * 8)
                    deps = ()
                    if cc == 0:
                        deps = (tr_free[bk], t_wrT[-1]) + xT_ready
                    last = mm(pk(bk, 0, 8), lhsT, rhs, cc == 0, cc == 7, deps,
                              1 if cc == 7 else 0, new_gen=(cc == 0))
                tok_m = VV(lambda e, bk=bk:
                           e.tensor_reduce(rmax[:, :], pk(bk, 0, 8), AX.X, ALU.max,
                                           negate=True),
                           (last, last_te), 1, True)
                rtm_toks.append(tok_m)
                tok_e = SS(lambda e, bk=bk:
                           e.activation(er[:, :], pk(bk, 0, 8), AF.Exp,
                                        bias=rmax[:, :], scale=1.0),
                           (tok_m, last_tp), 1)
                router_free[bk] = tok_e
                last_te = tok_e
                tok_s = VV(lambda e: e.reduce_sum(rsum[:, :], er[:, :], AX.X),
                           (tok_e,), 1, True)
                tok_r = VV(lambda e: e.reciprocal(rrec[:, :], rsum[:, :]), (tok_s,), 1, True)
                pcol = (tl % 2) * 2 + s
                tok_p = VV(lambda e, pcol=pcol: e.tensor_mul(p7[:, pcol:pcol + 1],
                                                             er[:, 7:8], rrec[:, :]),
                           (tok_r,), 1, True)
                last_tp = tok_p
                p7_toks.append(tok_p)
            epi_data[tl] = {"p7": p7_toks, "rtm": rtm_toks}

        def b_head2(tl):
            nonlocal xT_free, aoT_free
            gelu_toks = [None] * 32
            for fcs in range(36):
                if fcs < 32:
                    fc = fcs
                    bk = fc % 4
                    last = None
                    for cc in range(8):
                        lhsT = sap(w1a, [[32768, 128], [1, 128]], offset=cc * 4096 + fc * 128)
                        rhs = sap(xT, [[2048, 128], [1, 256]], offset=cc * 256)
                        deps = ()
                        if cc == 0:
                            deps = (h_free[fc % 4], qk_free[0], qk_free[1],
                                    tok_w1) + xT_ready
                            if fc < 2:
                                deps = deps + (tr_free[fc], router_free[fc],
                                               epi_data[tl]["rtm"][fc])
                        last = mm(pk(bk, 0, 256), lhsT, rhs, cc == 0, cc == 7, deps,
                                  1 if cc == 7 else 0, new_gen=(cc == 0))
                    slot = fc % NG
                    gelu_toks[fc] = SS(lambda e, slot=slot, fc=fc, bk=bk:
                               e.activation(hT[:, slot * 256:(slot + 1) * 256],
                                            pk(bk, 0, 256), AF.Gelu,
                                            bias=b1col[:, fc:fc + 1], scale=1.0),
                               (last, ht_free[slot], t_b1), 1)
                    h_free[fc % 4] = gelu_toks[fc]
                    hbank_free[fc % 4] = gelu_toks[fc]
                    if fc == 31:
                        xT_free = last
                if fcs >= 4:
                    fc = fcs - 4
                    slot = fc % NG
                    w2ch = w2_chunks[fc // 4]
                    ylast = None
                    for s in range(2):
                        for jh in range(2):
                            ybk = 4 + 2 * s + jh
                            lhsT = sap(hT, [[2048, 128], [1, 128]], offset=slot * 256 + s * 128)
                            rhs = sap(rot, [[32768, 128], [1, 512]], offset=fc * 1024 + jh * 512)
                            deps = [gelu_toks[fc], w2ch]
                            if fc == 0:
                                deps += [y_free[2 * s + jh], score_free[jh],
                                         sums_free[jh], wout_free45, wout_free67]
                            is_last = (s == 1 and jh == 1)
                            ylast = mm(pk(ybk, 0, 512), lhsT, rhs, fc == 0, False,
                                       tuple(deps), 1 if is_last else 0, new_gen=(fc == 0))
                    ht_free[slot] = ylast
                    last_B_ymm_tok[fc] = ylast
            mm_flush()
            aoT_free = last_B_ymm_tok[31]
            if tl + 1 < n_b[half]:
                mk_bcast(tl + 1)

            for s in range(2):
                grp = []
                for jh in range(2):
                    ybk = 4 + 2 * s + jh
                    mm(pk(ybk, 0, 512), onesbf[64:65, 0:128],
                       b2row(jh * 512, 512), False, False,
                       (t_b2_lo, t_ones), 0, new_gen=False)
                    tok = mm(pk(ybk, 0, 512), onesbf[64:65, 0:128],
                             b2row(1024 + jh * 512, 512),
                             False, True, (), 1, new_gen=False)
                    grp.append(tok)
                mm_flush()
                epi_data[tl][f"grp{s}"] = grp

        def b_epi1(tl):
            nonlocal wout_free45, wout_free67
            dat = epi_data[tl]
            for s in range(2):
                grp = dat[f"grp{s}"]
                slot = tl * 2 + s
                buf = slot % 2
                pcol = (tl % 2) * 2 + s
                py = bass.AP(ps[2 + s], 0, [[1024, 128], [1, 1024]])
                x1src = sap(x1h, [[8192, 128], [1, 1024]], offset=slot * 1024)
                y2ap = (qT if buf == 0 else vtok)[:, :].bitcast(F32)
                stb = stzb[(tl % 2) * 2 + s]
                tok_z2 = VV(lambda e, py=py, x1src=x1src, y2ap=y2ap, pcol=pcol, stb=stb:
                            e.scalar_tensor_tensor(y2ap, py, p7[:, pcol:pcol + 1], x1src,
                                                   ALU.mult, ALU.add,
                                                   accum_out=stb[:, 0:1]),
                            tuple(grp) + (dat["p7"][s], y2_free[buf],
                                          x1slot_ready[slot]),
                            1, True)
                dat[f"z2_{s}"] = tok_z2
                x1slot_consumed[slot] = [tok_z2]
                y_free[2 * s] = tok_z2
                y_free[2 * s + 1] = tok_z2
                if s == 0:
                    wout_free45 = tok_z2
                else:
                    wout_free67 = tok_z2

        def b_epi2a(tl):
            dat = epi_data[tl]
            for s in range(2):
                buf = (tl * 2 + s) % 2
                y2ap = (qT if buf == 0 else vtok)[:, :].bitcast(F32)
                scratch = kTs[:, :].bitcast(F32)
                tok_ve, _ = ln_stats_a(y2ap, scratch, dat[f"z2_{s}"],
                                       st=stzb[(tl % 2) * 2 + s])
                dat[f"ve_{s}"] = tok_ve

        def b_epi2b(tl):
            tile = half * 4 + tl
            t0 = tile * CS
            dat = epi_data[tl]
            for s in range(2):
                buf = (tl * 2 + s) % 2
                y2ap = (qT if buf == 0 else vtok)[:, :].bitcast(F32)
                stb = stzb[(tl % 2) * 2 + s]
                tok_rs = ln_stats_b(dat[f"ve_{s}"], st=stb)
                tok_o = VV(lambda e, y2ap=y2ap, stb=stb:
                           e.tensor_scalar(y2ap, y2ap, stb[:, 1:2],
                                           stb[:, 7:8],
                                           ALU.subtract, ALU.mult),
                           (tok_rs,), 1)
                tok_dma = dmaO(bass.AP(out_d, (t0 + s * 128) * 1024, [[1024, 128], [1, 1024]]),
                               y2ap, (tok_o,), sem=("O" if buf == 0 else "O2"))
                y2_free[buf] = tok_dma

        epi_data = {}
        for tl in range(n_b[half]):
            b_head1(tl)
            if tl >= 2:
                b_epi2b(tl - 2)
            if tl > 0:
                b_epi1(tl - 1)
                b_epi2a(tl - 1)
            b_head2(tl)
        if n_b[half]:
            L = n_b[half] - 1
            if L >= 1:
                b_epi2b(L - 1)
            b_epi1(L)
            b_epi2a(L)
            if half + 1 < n_halves:
                a_cast[(4, 0)] = mk_acast(4, 0)
                a_cast[(4, 1)] = mk_acast(4, 1)
            b_epi2b(L)

    E.op("Y", lambda e: e.nop(),
         (("O", E.cnt["O"], True), ("O2", E.cnt["O2"], True)), 0)

    with nc.Block() as block, nc.allow_low_precision(
            reason="bf16 matmul operands by design; fp32 accumulation stays in PSUM"):
        @block.tensor
        def _(eng):
            E.replay("T", eng, sems)

        @block.vector
        def _(eng):
            E.replay("V", eng, sems)

        @block.scalar
        def _(eng):
            E.replay("S", eng, sems)

        @block.gpsimd
        def _(eng):
            E.replay("G", eng, sems)

        @block.sync
        def _(eng):
            E.replay("Y", eng, sems)

    return nc


_NC_CACHE = None


def kernel(L0_attn_x, L0_attn_Wqkv, L0_attn_Wout, L0_attn_bout, L0_attn_Wl1,
           L0_attn_bl1, L0_attn_Wl2, L0_attn_bl2, L0_ffn_Wr, L0_ffn_W1, L0_ffn_b1,
           L0_ffn_W2, L0_ffn_b2):
    nc = build_nc()
    x = np.ascontiguousarray(np.asarray(L0_attn_x, dtype=np.float32))
    common = {
        "wqkv": np.ascontiguousarray(np.asarray(L0_attn_Wqkv, np.float32)),
        "wout": np.ascontiguousarray(np.asarray(L0_attn_Wout, np.float32)),
        "bout": np.ascontiguousarray(np.asarray(L0_attn_bout, np.float32)),
        "wl1": np.ascontiguousarray(np.asarray(L0_attn_Wl1, np.float32)),
        "bl1": np.ascontiguousarray(np.asarray(L0_attn_bl1, np.float32)),
        "wl2": np.ascontiguousarray(np.asarray(L0_attn_Wl2, np.float32)),
        "bl2": np.ascontiguousarray(np.asarray(L0_attn_bl2, np.float32)),
        "wr": np.ascontiguousarray(np.asarray(L0_ffn_Wr, np.float32)),
        "w1": np.ascontiguousarray(np.asarray(L0_ffn_W1, np.float32)[NE - 1]),
        "b1": np.ascontiguousarray(np.asarray(L0_ffn_b1, np.float32)[NE - 1]),
        "w2": np.ascontiguousarray(np.asarray(L0_ffn_W2, np.float32)[NE - 1]),
        "b2": np.ascontiguousarray(np.asarray(L0_ffn_b2, np.float32)[NE - 1]),
    }
    in_maps = [dict(common, x=np.ascontiguousarray(x[b])) for b in range(8)]
    res = run_bass_kernel_spmd(nc, in_maps, core_ids=list(range(8)))
    out = np.stack([np.asarray(res.results[b]["out"], dtype=np.float32)
                    for b in range(8)], axis=0)
    return out


# version 20
# speedup vs baseline: 1.0325x; 1.0059x over previous
import numpy as np
import ml_dtypes
import concourse.bass as bass
import concourse.mybir as mybir
from concourse.bass_utils import run_bass_kernel_spmd

F32 = mybir.dt.float32
BF16 = mybir.dt.bfloat16
AF = mybir.ActivationFunctionType
ALU = mybir.AluOpType
AX = mybir.AxisListType

T = 2048
C = 1024
H = 16
CS = 256
FF = 4096
NE = 8
NG = 8

SEMS = ("T", "V", "S", "G", "Y", "O", "CONST", "BIAS", "WR", "RIMG", "ROTL",
        "GS", "XR0", "XR1", "XB0", "XB1", "SG0", "SG1",
        "W2C0", "W2C1", "W2C2", "W2C3", "W2C4", "W2C5", "W2C6", "W2C7",
        "W2I0", "W2I1", "W2I2", "W2I3", "W2I4", "W2I5", "W2I6", "W2I7", "O2",
        "SG2")


class Em:
    def __init__(self):
        self.q = {e: [] for e in ("T", "V", "S", "G", "Y")}
        self.cnt = {s: 0 for s in SEMS}
        self.synced = {e: {s: 0 for s in SEMS} for e in self.q}

    def op(self, eng, fn, deps=(), inc=0, inc_sem=None, hazard=False):
        need = {}
        for d in deps:
            if d is None:
                continue
            sem, val, dhaz = d
            if sem == eng and not dhaz:
                continue
            if val > need.get(sem, -1):
                need[sem] = val
        waits = []
        for sem, val in need.items():
            if self.synced[eng][sem] >= val:
                continue
            waits.append((sem, val))
            self.synced[eng][sem] = val
        if inc:
            s = inc_sem or eng
            self.cnt[s] += inc
            self.q[eng].append((waits, fn, s, inc))
            return (s, self.cnt[s], hazard)
        self.q[eng].append((waits, fn, None, 0))
        return None

    def replay(self, eng_name, eng, sems):
        for waits, fn, incsem, amt in self.q[eng_name]:
            for sem, val in waits:
                eng.wait_ge(sems[sem], val)
            inst = fn(eng)
            if incsem is not None:
                inst.then_inc(sems[incsem], amt)


def sap(t, dims, offset=0):
    return bass.AP(t, offset, [list(d) for d in dims])


def build_nc(n_halves=2, n_a=(4, 4), n_b=(4, 4), dump=None, attn_serial=False, no_gs=False, hwstag=False, skips=()):
    if isinstance(n_a, int):
        n_a = (n_a, n_a)
    if isinstance(n_b, int):
        n_b = (n_b, n_b)
    nc = bass.Bass("TRN2", target_bir_lowering=False,
                   dynamic_dma_scratch_size=16384)

    x_d = nc.dram_tensor("x", [T, C], F32, kind="ExternalInput")
    wqkv_d = nc.dram_tensor("wqkv", [3 * C, C], F32, kind="ExternalInput")
    wout_d = nc.dram_tensor("wout", [C, C], F32, kind="ExternalInput")
    bout_d = nc.dram_tensor("bout", [C], F32, kind="ExternalInput")
    wl1_d = nc.dram_tensor("wl1", [64, 1], F32, kind="ExternalInput")
    bl1_d = nc.dram_tensor("bl1", [64], F32, kind="ExternalInput")
    wl2_d = nc.dram_tensor("wl2", [H, 64], F32, kind="ExternalInput")
    bl2_d = nc.dram_tensor("bl2", [H], F32, kind="ExternalInput")
    wr_d = nc.dram_tensor("wr", [NE, C], F32, kind="ExternalInput")
    w1_d = nc.dram_tensor("w1", [C, FF], F32, kind="ExternalInput")
    b1_d = nc.dram_tensor("b1", [FF], F32, kind="ExternalInput")
    w2_d = nc.dram_tensor("w2", [FF, C], F32, kind="ExternalInput")
    b2_d = nc.dram_tensor("b2", [C], F32, kind="ExternalInput")
    out_d = nc.dram_tensor("out", [T, C], F32, kind="ExternalOutput")
    dbg_d = nc.dram_tensor("dbg", [128, 8192], F32, kind="ExternalOutput") if dump else None
    dball = {}
    if dump == "all":
        for nm, dt_ in (("d_xT", BF16), ("d_qT", BF16), ("d_kTs", BF16), ("d_vtok", BF16),
                        ("d_eT", BF16), ("d_aoT", BF16), ("d_lam", BF16), ("d_rot", BF16),
                        ("d_z1", F32), ("d_x1h", F32), ("d_st", F32)):
            shape = [128, 32768] if nm == "d_rot" else ([128, 2048] if dt_ == BF16 else [128, 2048])
            if nm == "d_lam":
                shape = [128, 512]
            if nm == "d_z1":
                shape = [128, 1024]
            if nm == "d_st":
                shape = [128, 16]
            if nm == "d_x1h":
                shape = [128, 8192]
            dball[nm] = nc.dram_tensor(nm, shape, dt_, kind="ExternalOutput")

    rotimg_d = nc.dram_tensor("rotimg", [128, 32768], BF16, kind="Internal")
    w2img_d = nc.dram_tensor("w2img", [128, 32768], BF16, kind="Internal")

    ident_bf_d = nc.inline_tensor(np.eye(128, dtype=ml_dtypes.bfloat16), name="cidbf")
    ident_f_d = nc.inline_tensor(np.eye(32, dtype=np.float32), name="cidf")
    ones_bf_d = nc.inline_tensor(np.ones((128, 128), dtype=ml_dtypes.bfloat16), name="conesbf")
    sel_np = np.zeros((H, 8 * 128), dtype=ml_dtypes.bfloat16)
    for cc in range(8):
        for p in range(128):
            sel_np[2 * cc + p // 64, cc * 128 + p] = 1.0
    sel_d = nc.inline_tensor(sel_np, name="cselbf")

    A = nc.alloc_sbuf_tensor
    rot = A("rot", [128, 32768], BF16)
    w1a = A("w1a", [128, 32768], BF16)
    x1h = A("x1h", [128, 8192], F32)
    xcast = A("xcast", [128, 2048], BF16)
    xT = A("xT", [128, 2048], BF16)
    xf32 = A("xf32", [128, 3072], F32)
    qT = A("qT", [128, 2048], BF16)
    kTs = A("kTs", [128, 2048], BF16)
    vtok = A("vtok", [128, 2048], BF16)
    eT = A("eT", [128, 2048], BF16)
    aoT = A("aoT", [128, 2048], BF16)
    lam_sb = A("lam_sb", [128, 512], BF16)
    stag = nc.alloc_sbuf_tensor_at("stag", [128, 1024], BF16,
                                   offset=nc.lookup_mloc(eT).addr)
    lambf = A("lambf", [16, 2048], BF16)
    identbf = A("identbf", [128, 128], BF16)
    identf = nc.alloc_sbuf_tensor_at("identf", [32, 32], F32,
                                     offset=nc.lookup_mloc(eT).addr + 2624)
    onesbf = A("onesbf", [128, 128], BF16)
    selbf = A("selbf", [16, 1024], BF16)

    b1col = A("b1col", [128, 32], F32)
    bl1col = nc.alloc_sbuf_tensor_at("bl1col", [64, 1], F32,
                                     offset=nc.lookup_mloc(eT).addr + 2752)
    bl2col = nc.alloc_sbuf_tensor_at("bl2col", [16, 1], F32,
                                     offset=nc.lookup_mloc(eT).addr + 2784)
    stz = A("stz", [128, 8], F32)
    wl1row = nc.alloc_sbuf_tensor_at("wl1row", [1, 64], F32,
                                     offset=nc.lookup_mloc(eT).addr + 2048)
    wl2nat = nc.alloc_sbuf_tensor_at("wl2nat", [16, 64], F32,
                                     offset=nc.lookup_mloc(eT).addr + 2304)
    wl2T = nc.alloc_sbuf_tensor_at("wl2T", [64, 16], F32,
                                   offset=nc.lookup_mloc(eT).addr + 2560)
    wrT = A("wrT", [128, 64], BF16)
    _et2 = nc.lookup_mloc(eT).addr
    er = nc.alloc_sbuf_tensor_at("er", [128, 8], F32, offset=_et2 + 2816)
    p7 = nc.alloc_sbuf_tensor_at("p7", [128, 4], F32, offset=_et2 + 2848)
    rmax = nc.alloc_sbuf_tensor_at("rmax", [128, 1], F32, offset=_et2 + 2880)
    rsum = nc.alloc_sbuf_tensor_at("rsum", [128, 1], F32, offset=_et2 + 2912)
    rrec = nc.alloc_sbuf_tensor_at("rrec", [128, 1], F32, offset=_et2 + 2944)
    stzb = [nc.alloc_sbuf_tensor_at(f"stzb{i}", [128, 8], F32,
                                    offset=_et2 + 2976 + 32 * i) for i in range(4)]
    cconst = A("cconst", [128, 2], F32)
    hT = nc.alloc_sbuf_tensor_at("hT", [128, 2048], BF16,
                                 offset=nc.lookup_mloc(aoT).addr)
    _lam_addr = nc.lookup_mloc(lambf).addr
    boutrow_t = nc.alloc_sbuf_tensor_at("boutrow", [33, 2048], BF16, offset=_lam_addr)
    b2row_t = nc.alloc_sbuf_tensor_at("b2row", [65, 2048], BF16, offset=_lam_addr)
    _sel_addr = nc.lookup_mloc(selbf).addr
    rTbf_t = nc.alloc_sbuf_tensor_at("rTbf", [33, 1024], BF16, offset=_sel_addr)
    boutrow = lambda c0, n: sap(boutrow_t, [[2048, 1], [1, n]], offset=32 * 2048 + c0)
    b2row = lambda c0, n: sap(b2row_t, [[2048, 1], [1, n]], offset=64 * 2048 + c0)
    rTbf = lambda c0, n: sap(rTbf_t, [[1024, 1], [1, n]], offset=32 * 1024 + c0)

    ps = [nc.alloc_psum_tensor(f"ps{i}", [128, 1024], F32) for i in range(4)]

    def pk(bank, c0, n, p0=0, np_=128):
        return bass.AP(ps[bank // 2], p0 * 1024 + (bank % 2) * 512 + c0,
                       [[1024, np_], [1, n]])

    sems = {s: nc.alloc_semaphore(name=f"sem{s}") for s in SEMS}
    E = Em()

    def TT_(fn, deps=(), inc=0, hz=False):
        return E.op("T", fn, deps, inc, None, hz)

    def VV(fn, deps=(), inc=0, hz=True):
        return E.op("V", fn, deps, inc, None, True)

    def SS(fn, deps=(), inc=0, hz=True):
        return E.op("S", fn, deps, inc, None, True)

    def GG(fn, deps=(), inc=0, hz=True):
        return E.op("G", fn, deps, inc, None, True)

    def dmaG(dst, src, deps=(), sem="G"):
        return E.op("G", lambda e, d=dst, s_=src: e.dma_start(d, s_), deps, 16, sem, True)

    def dmaY(dst, src, deps=(), sem="Y"):
        return E.op("Y", lambda e, d=dst, s_=src: e.dma_start(d, s_), deps, 16, sem, True)

    def dmaO(dst, src, deps=(), sem="O"):
        return E.op("G", lambda e, d=dst, s_=src: e.dma_start(d, s_), deps, 16, sem, True)

    pendT = {"amt": 0}

    def mm(out_ap, lhsT, rhs, first, last, deps=(), inc=0, new_gen=False,
           flush=False):
        st = bool(first and new_gen)

        def fn(e, o=out_ap, l=lhsT, r=rhs, s_=st, la=last):
            return e.matmul(o, l, r, start=s_, stop=la, skip_group_check=True)
        tok = None
        attach = 0
        if inc:
            attach = pendT["amt"]
            pendT["amt"] = inc
            E.cnt["T"] += inc
            tok = ("T", E.cnt["T"], False)
        need = {}
        for d in deps:
            if d is None:
                continue
            sem, val, dhaz = d
            if sem == "T" and not dhaz:
                continue
            if val > need.get(sem, -1):
                need[sem] = val
        waits = []
        for sem, val in need.items():
            if E.synced["T"][sem] >= val:
                continue
            waits.append((sem, val))
            E.synced["T"][sem] = val
        E.q["T"].append((waits, fn, "T" if attach else None, attach))
        if flush:
            mm_flush()
        return tok

    def mm_flush():
        if pendT["amt"]:
            a = pendT["amt"]
            pendT["amt"] = 0
            E.q["T"].append(([], lambda e: e.drain(), "T", a))

    t_id = dmaY(identbf[:, :], ident_bf_d.ap(), sem="CONST")
    t_idf = dmaY(identf[:, :], ident_f_d.ap(), sem="CONST")
    t_ones = dmaY(onesbf[:, :], ones_bf_d.ap(), sem="CONST")
    t_sel = dmaY(selbf[:, :], sel_d.ap(), sem="CONST")
    t_wl1 = dmaY(sap(wl1row, [[64, 1], [1, 64]]), bass.AP(wl1_d, 0, [[64, 1], [1, 64]]), sem="CONST")
    t_bl1 = dmaY(sap(bl1col, [[1, 64], [1, 1]]), bass.AP(bl1_d, 0, [[1, 64], [1, 1]]), sem="CONST")
    t_wl2 = dmaY(wl2nat[:, :], wl2_d.ap(), sem="CONST")
    t_bl2 = dmaY(sap(bl2col, [[1, 16], [1, 1]]), bass.AP(bl2_d, 0, [[1, 16], [1, 1]]), sem="CONST")
    wrnat = None
    stag_f = stag[:, :].bitcast(F32)
    t_b1n = dmaY(sap(stag, [[1024, 32], [1, 256]]).bitcast(F32),
                 bass.AP(b1_d, 0, [[128, 32], [1, 128]]), sem="CONST")
    rowtmp = sap(x1h, [[8192, 1], [1, 1024]], offset=64 * 8192 + 2048)
    t_bout = dmaY(rowtmp, bass.AP(bout_d, 0, [[1024, 1], [1, 1024]]), sem="CONST")
    tok_const = ("CONST", E.cnt["CONST"], True)
    t_id = t_idf = t_ones = t_sel = t_wl1 = t_bl1 = t_wl2 = t_bl2 = tok_const
    t_b1n = t_bout = tok_const

    t_c12 = VV(lambda e: e.memset(cconst[:, 0:1], -12.0), (), 1, True)
    t_c18 = VV(lambda e: e.memset(cconst[:, 1:2], 0.125), (), 1, True)

    xft = [None] * 16
    x1tok = [None] * 16

    def xf_load(k, dep):
        slot = k % 3
        dst = sap(xf32, [[3072, 128], [1, 1024]], offset=slot * 1024)
        src = bass.AP(x_d, k * 128 * 1024, [[1024, 128], [1, 1024]])
        xft[k] = dmaY(dst, src, (dep,), sem=("XR0", "XR1", "XB0")[k % 3])

    for k0 in range(3):
        xf_load(k0, None)

    SKIP = set(skips)
    logpos = sap(x1h, [[8192, 1], [1, 2048]], offset=0)
    if "lam" in SKIP:
        t_iota = t_logp = t_lambf = None
        t_h1 = [None] * 4
        t_lam = [None] * 4
        t_wl2T = None
    t_iota = GG(lambda e: e.iota(logpos, [[1, 2048]], base=0, channel_multiplier=0,
                                 allow_small_or_imprecise_dtypes=True), (), 1, True) if "lam" not in SKIP else None
    if "lam" not in SKIP:
        t_logp = SS(lambda e: e.activation(logpos, logpos, AF.Ln, bias=1.0, scale=1.0),
                    (t_iota,), 1)

    t_h1 = []
    for i in range(0 if "lam" in SKIP else 4):
        prev_ev = t_h1[-1] if t_h1 else None
        tokm = mm(pk(0, 0, 512, 0, 64), wl1row[0:1, :],
                  sap(x1h, [[8192, 1], [1, 512]], offset=i * 512),
                  True, True, (t_wl1, t_logp, prev_ev), 1, new_gen=True, flush=True)
        toke = SS(lambda e, i=i: e.activation(
            sap(x1h, [[8192, 64], [1, 512]], offset=2048 + i * 512),
            pk(0, 0, 512, 0, 64), AF.Relu, bias=bl1col[:, :], scale=1.0),
            (tokm, t_bl1), 1)
        t_h1.append(toke)

    if "lam" not in SKIP:
        t_wl2T_mm = mm(pk(2, 0, 16, 0, 64), wl2nat[:, :], identf[0:16, 0:16],
                       True, True, (t_wl2, t_idf), 1, new_gen=True, flush=True)
        t_wl2T = VV(lambda e: e.tensor_copy(wl2T[:, :], pk(2, 0, 16, 0, 64)), (t_wl2T_mm,), 1)
    else:
        t_wl2T = None

    t_lam = []
    for i in range(0 if "lam" in SKIP else 4):
        prev_ev = t_lam[-1] if t_lam else None
        tokm = mm(pk(1, 0, 512, 0, 16), wl2T[:, :],
                  sap(x1h, [[8192, 64], [1, 512]], offset=2048 + i * 512),
                  True, True, (t_wl2T, t_h1[i], prev_ev), 1, new_gen=True, flush=True)
        toke = SS(lambda e, i=i: e.activation(
            sap(x1h, [[8192, 16], [1, 512]], offset=96 * 8192 + 2048 + i * 512),
            pk(1, 0, 512, 0, 16), AF.Sigmoid, bias=bl2col[:, :], scale=1.0),
            (tokm, t_bl2), 1)
        t_lam.append(toke)
    t_lambf = VV(lambda e: e.tensor_copy(
        lambf[:, :], sap(x1h, [[8192, 16], [1, 2048]], offset=96 * 8192 + 2048)),
        tuple(t_lam), 1) if "lam" not in SKIP else None

    x1h_bf = x1h[:, :].bitcast(BF16)
    hi_bf = x1h_bf[64:65, 6144:7168]
    lo_bf = x1h_bf[64:65, 7168:8192]
    hilo_bf = x1h_bf[64:65, 6144:8192]
    if "bias" in SKIP:
        t_bo_hi = t_bo_lo0 = t_bo_lo = t_b2r = t_b2_hi = t_b2_lo0 = t_b2_lo = None
        tok_bias = None
    t_bo_hi = VV(lambda e: e.tensor_copy(hi_bf, rowtmp), (t_bout,), 1) if "bias" not in SKIP else None
    if "bias" not in SKIP:
        t_bo_lo0 = VV(lambda e: e.tensor_sub(lo_bf, rowtmp, hi_bf), (t_bo_hi,), 1)
        t_bo_lo = dmaY(boutrow(0, 2048), hilo_bf, (t_bo_lo0,), sem="BIAS")
        t_b2r = dmaY(rowtmp, bass.AP(b2_d, 0, [[1024, 1], [1, 1024]]), (t_bo_lo,), sem="BIAS")
        t_b2_hi = VV(lambda e: e.tensor_copy(hi_bf, rowtmp), (t_b2r, t_bo_lo), 1)
        t_b2_lo0 = VV(lambda e: e.tensor_sub(lo_bf, rowtmp, hi_bf), (t_b2_hi,), 1)
        t_b2_lo = dmaY(b2row(0, 2048), hilo_bf, (t_b2_lo0,), sem="BIAS")
        tok_bias = ("BIAS", E.cnt["BIAS"], True)
        t_bo_lo = t_b2_lo = tok_bias

    t_b1m = mm(pk(3, 0, 32), stag_f[0:32, 0:128], identf[0:32, 0:32],
               True, True, (t_b1n, t_idf), 1, new_gen=True, flush=True)
    t_b1 = VV(lambda e: e.tensor_copy(b1col[:, :], pk(3, 0, 32)), (t_b1m,), 1)
    t_wr = dmaY(sap(x1h, [[8192, 8], [1, 1024]], offset=4096), wr_d.ap(),
                (t_b1,), sem="WR")

    t_wrT = []
    prev_wrev = t_b1
    for cc in range(8):
        tokm = mm(pk(3, 0, 8), sap(x1h, [[8192, 8], [1, 128]], offset=4096 + cc * 128),
                  identf[0:8, 0:8],
                  True, True, (t_wr, t_idf, prev_wrev), 1, new_gen=True, flush=True)
        tokc = VV(lambda e, cc=cc: e.tensor_copy(wrT[:, cc * 8:(cc + 1) * 8], pk(3, 0, 8)),
                  (tokm,), 1)
        prev_wrev = tokc
        t_wrT.append(tokc)

    stag2 = nc.alloc_sbuf_tensor_at("stag2", [128, 1024], BF16,
                                    offset=nc.lookup_mloc(aoT).addr)
    stag3 = nc.alloc_sbuf_tensor_at("stag3", [128, 1024], BF16,
                                    offset=nc.lookup_mloc(aoT).addr + 2048)
    stags = [stag, stag2, stag3]
    stag_tok = [t_wrT[-1], None, None]
    tr45_free = [None, None]
    rot_evac = []
    jc_glob = [0]

    def load_transpose(jc_total, src_dram, dst_base, dst_ccstride):
        for jc in range(jc_total):
            sl = jc_glob[0] % 3
            jc_glob[0] += 1
            src = bass.AP(src_dram, jc * 128 * 1024, [[1024, 128], [1, 1024]])
            dst = sap(stags[sl], [[1024, 128], [1, 1024]])
            tdma = dmaG(dst, src, (stag_tok[sl],), sem=f"SG{sl}")
            last_mm = None
            for g in range(2):
                bank = 4 + g
                mts = []
                for b in range(4):
                    cc = g * 4 + b
                    lhsT = sap(stags[sl], [[1024, 128], [1, 128]], offset=cc * 128)
                    deps = (tdma, t_id, tr45_free[g]) if b == 0 else ()
                    tokm = mm(pk(bank, b * 128, 128), lhsT, identbf[:, :],
                              b == 0, b == 3, deps, 1 if b == 3 else 0, new_gen=(b == 0),
                              flush=(b == 3))
                    mts.append(tokm)
                dst_ap = sap(rot, [[32768, 128], [dst_ccstride, 4], [1, 128]],
                             offset=dst_base + (g * 4) * dst_ccstride + jc * 128)
                src_ap = bass.AP(ps[2], (bank % 2) * 512, [[1024, 128], [128, 4], [1, 128]])
                tokev = VV(lambda e, d=dst_ap, s_=src_ap: e.tensor_copy(d, s_),
                           (mts[-1],), 1)
                tr45_free[g] = tokev
                rot_evac.append(tokev)
                last_mm = mts[-1]
            stag_tok[sl] = last_mm

    load_transpose(24, wqkv_d, 0, 3072)
    load_transpose(8, wout_d, 24576, 1024)
    rot_ready0 = tuple(rot_evac[-2:])

    t_w1 = []
    for q4 in range(0 if no_gs else 4):
        for cc in range(8):
            dst = sap(w1a, [[32768, 128], [1, 1024]], offset=cc * 4096 + q4 * 1024)
            src = bass.AP(w1_d, cc * 128 * 4096 + q4 * 1024, [[4096, 128], [1, 1024]])
            t_w1.append(dmaG(dst, src, sem="GS"))
    tok_w1 = ("GS", E.cnt["GS"], True)

    t_w2img = []
    for fc in range(0 if no_gs else 32):
        dst = bass.AP(w2img_d, fc * 1024, [[32768, 128], [1, 1024]])
        src = bass.AP(w2_d, fc * 128 * 1024, [[1024, 128], [1, 1024]])
        t_w2img.append(dmaG(dst, src, sem=f"W2I{fc // 4}"))
    t_w2img = [t_w2img[4 * g + 3] for g in range(8)] if t_w2img else []
    t_rotimg = dmaG(rotimg_d.ap(), rot[:, :], tuple(rot_evac), sem="RIMG") \
        if "rimg" not in SKIP else None
    tok_gs = ("GS", E.cnt["GS"], True)
    if no_gs:
        tok_w1 = tok_gs

    x1slot_consumed = [[] for _ in range(8)]
    x1slot_consumed[0] = [t_lambf]
    x1slot_consumed[1] = [t_lambf]
    x1slot_consumed[2] = [t_lambf, t_b2_lo]
    x1slot_consumed[3] = [t_lambf, t_b2_lo]
    x1slot_consumed[4] = [t_wrT[-1]]
    x1slot_ready = [None] * 8
    xT_free = None
    xcast_free = None
    y2_free = [None, None]
    tr_free = [t_h1[-1], t_lam[-1]]
    qk_free = [t_wl2T, t_wrT[-1]]
    wout_free45 = rot_evac[-1]
    wout_free67 = None
    last_sdA = None
    last_tailA = None
    a_cast = {}
    a_epi = {}
    b_cast = {}
    h_free = [None] * 4
    hbank_free = [None] * 4
    y_free = [None, None, None, None]
    ht_free = [None] * NG
    router_free = [None, None]
    av_free = [None, None]
    score_free = [None, None]
    sums_free = [None, None]
    eT_free = [None, None, None, None]
    aoT_free = None
    last_sd = None
    last_r0 = None
    last_te = None
    last_tp = None
    last_A_qkv_tok = None
    last_A_wout_tok = None
    last_B_ymm_tok = [None] * 32
    rot_ready = {0: rot_ready0}

    def ln_stats_a(z_ap, scratch_ap, stt_tok, extra_var_deps=(), st=None):
        nonlocal last_sd
        if st is None:
            st = stz
        tok_mu = VV(lambda e, st=st: e.tensor_scalar_mul(st[:, 1:2], st[:, 0:1],
                                                         1.0 / 1024.0),
                    (stt_tok,), 1, True)
        tok_var = VV(lambda e, z=z_ap, sc=scratch_ap, st=st:
                     e.scalar_tensor_tensor(sc, z, st[:, 1:2], z, ALU.subtract, ALU.mult,
                                            accum_out=st[:, 2:3]),
                     (tok_mu,) + tuple(extra_var_deps), 1, True)
        tok_ve = VV(lambda e, st=st: e.tensor_scalar(st[:, 3:4], st[:, 2:3], 1.0 / 1024.0,
                                                     1e-5, ALU.mult, ALU.add),
                    (tok_var, last_sd), 1, True)
        return tok_ve, tok_var

    def ln_stats_b(tok_ve, st=None):
        nonlocal last_sd, last_r0
        if st is None:
            st = stz
        tok_sd = SS(lambda e, st=st: e.activation(st[:, 4:5], st[:, 3:4], AF.Sqrt,
                                                  bias=0.0, scale=1.0),
                    (tok_ve, last_r0), 1)
        tok_r0 = VV(lambda e, st=st: e.reciprocal(st[:, 5:6], st[:, 4:5]),
                    (tok_sd,), 1, True)
        tok_aa = VV(lambda e, st=st: e.tensor_mul(st[:, 6:7], st[:, 5:6], st[:, 5:6]),
                    (tok_r0,), 1, True)
        tok_aa2 = VV(lambda e, st=st: e.tensor_mul(st[:, 6:7], st[:, 6:7], st[:, 3:4]),
                     (tok_aa,), 1, True)
        tok_aa3 = VV(lambda e, st=st: e.tensor_scalar(st[:, 6:7], st[:, 6:7], -0.5, 1.5,
                                                      ALU.mult, ALU.add),
                     (tok_aa2,), 1, True)
        tok_rs = VV(lambda e, st=st: e.tensor_mul(st[:, 7:8], st[:, 5:6], st[:, 6:7]),
                    (tok_aa3,), 1, True)
        last_sd = tok_sd
        last_r0 = tok_r0
        return tok_rs

    def ln_stats(z_ap, scratch_ap, stt_tok, extra_var_deps=(), st=None):
        tok_ve, tok_var = ln_stats_a(z_ap, scratch_ap, stt_tok, extra_var_deps, st)
        tok_rs = ln_stats_b(tok_ve, st)
        return tok_rs, tok_var

    for half in range(n_halves):
        if half == 1:
            toks = []
            for ch in range(8):
                gate = last_B_ymm_tok[4 * ch + 3]
                dst = sap(rot, [[32768, 128], [1, 4096]], offset=ch * 4096)
                src = bass.AP(rotimg_d, ch * 4096, [[32768, 128], [1, 4096]])
                toks.append(dmaY(dst, src, (gate, t_rotimg), sem="ROTL"))
            rot_ready[1] = tuple(toks)
        rr = rot_ready[half]


        def a_epi_tail(ti):
            nonlocal last_sdA, last_tailA
            ve_both = (a_epi[(ti, 0)][0], a_epi[(ti, 1)][0])
            for sb in range(2):
                tok_ve, x1dst, slot = a_epi.pop((ti, sb))
                tok_sd = SS(lambda e, sb=sb: e.activation(stz[:, 0:1],
                                                          stz[:, 3 + sb:4 + sb],
                                                          AF.Sqrt, bias=0.0, scale=1.0),
                            ve_both + (last_tailA,), 1)
                tok_r0 = VV(lambda e: e.reciprocal(stz[:, 1:2], stz[:, 0:1]),
                            (tok_sd,), 1, True)
                tok_aa = VV(lambda e: e.tensor_mul(stz[:, 2:3], stz[:, 1:2],
                                                   stz[:, 1:2]),
                            (tok_r0,), 1, True)
                tok_aa2 = VV(lambda e, sb=sb: e.tensor_mul(stz[:, 2:3], stz[:, 2:3],
                                                           stz[:, 3 + sb:4 + sb]),
                             (tok_aa,), 1, True)
                tok_aa3 = VV(lambda e: e.tensor_scalar(stz[:, 2:3], stz[:, 2:3],
                                                       -0.5, 1.5, ALU.mult, ALU.add),
                             (tok_aa2,), 1, True)
                tok_rs = VV(lambda e: e.tensor_mul(stz[:, 5:6], stz[:, 1:2],
                                                   stz[:, 2:3]),
                            (tok_aa3,), 1, True)
                tok_x1 = VV(lambda e, x1dst=x1dst:
                            e.tensor_scalar_mul(x1dst, x1dst, stz[:, 5:6]),
                            (tok_rs,), 1, True)
                x1slot_ready[slot] = tok_x1
                last_sdA = tok_sd
                last_tailA = tok_x1

        def mk_acast(ti, sb):
            kk = ti * 2 + sb
            dst = sap(xcast, [[2048, 128], [1, 1024]], offset=sb * 1024)
            src = sap(xf32, [[3072, 128], [1, 1024]], offset=(kk % 3) * 1024)
            return SS(lambda e, d=dst, s_=src: e.copy(d, s_),
                      (xft[kk], xcast_free), 1)

        for tl in range(n_a[half]):
            tile = half * 4 + tl
            n = tile
            t0 = tile * CS

            k0 = tile * 2
            if 3 <= k0 + 2 < 16:
                xf_load(k0 + 2, x1tok[k0 - 1])
            xbf_toks = []
            for s in range(2):
                k = k0 + s
                if (tile, s) in a_cast:
                    xbf_toks.append(a_cast.pop((tile, s)))
                else:
                    xbf_toks.append(mk_acast(tile, s))

            trev = []
            for g in range(4):
                s, cq = divmod(g, 2)
                bk = g % 2
                mts = []
                for b in range(4):
                    cc = cq * 4 + b
                    lhsT = sap(xcast, [[2048, 128], [1, 128]], offset=s * 1024 + cc * 128)
                    deps = ()
                    if b == 0:
                        deps = (xbf_toks[s], tr_free[bk], av_free[bk], t_id,
                                router_free[bk], hbank_free[bk])
                    tokm = mm(pk(bk, b * 128, 128), lhsT, identbf[:, :],
                              b == 0, b == 3, deps, 1 if b == 3 else 0, new_gen=(b == 0))
                    mts.append(tokm)
                dst_ap = sap(xT, [[2048, 128], [256, 4], [1, 128]],
                             offset=cq * 4 * 256 + s * 128)
                src_ap = bass.AP(ps[0], (bk % 2) * 512, [[1024, 128], [128, 4], [1, 128]])
                tokev = SS(lambda e, d=dst_ap, s_=src_ap: e.copy(d, s_),
                           (mts[-1], xT_free), 1)
                trev.append(tokev)
                tr_free[bk] = tokev
            xT_ready = (trev[-1],)
            xcast_free = trev[-1]

            lam_m = [None] * 4
            lam_copy = [None] * 4
            lam_banks = [4, 5, 6, 7]
            for pr in range(4):
                bkk = lam_banks[pr]
                deps0 = [t_sel, t_lambf]
                deps0 += [wout_free45 if bkk in (4, 5) else wout_free67]
                m0 = mm(pk(bkk, 0, 256), selbf[:, (2 * pr) * 128:(2 * pr + 1) * 128],
                        lambf[:, n * 256:(n + 1) * 256], True, True, tuple(deps0),
                        0, new_gen=True)
                m1 = mm(pk(bkk, 256, 256), selbf[:, (2 * pr + 1) * 128:(2 * pr + 2) * 128],
                        lambf[:, n * 256:(n + 1) * 256], True, True, (), 1, new_gen=False)
                lam_m[pr] = m1

            qev = []
            deps_half = list(rr) + [h_free[0], h_free[1]]
            for pi in range(4):
                bk = pi % 4
                grp_last = None
                for sub_j in range(2):
                    jc = pi * 2 + sub_j
                    for cc in range(8):
                        lhsT = sap(rot, [[32768, 128], [1, 128]], offset=cc * 3072 + jc * 128)
                        rhs = sap(xT, [[2048, 128], [1, 256]], offset=cc * 256)
                        first = (cc == 0)
                        deps = ()
                        if first and sub_j == 0:
                            if bk < 2:
                                bdep = (tr_free[bk], av_free[bk])
                            else:
                                bdep = (qk_free[bk - 2],)
                            deps = tuple(deps_half) + bdep + \
                                (hbank_free[bk],) + xT_ready
                        grp_last = mm(pk(bk, sub_j * 256, 256), lhsT, rhs, first, cc == 7,
                                      deps, 1 if (cc == 7 and sub_j == 1) else 0,
                                      new_gen=(first and sub_j == 0))
                tokev = SS(lambda e, pi=pi, bk=bk:
                           e.activation(qT[:, pi * 512:(pi + 1) * 512], pk(bk, 0, 512),
                                        AF.Copy, bias=0.0, scale=cconst[:, 1:2]),
                           (grp_last, y2_free[0], t_c18), 1)
                qev.append(tokev)

            kev = []
            lam_copy[0] = VV(lambda e: e.tensor_copy(lam_sb[:, 0:512],
                                                     pk(lam_banks[0], 0, 512)),
                             (lam_m[0],), 1)

            for pr in range(4):
                bk = pr % 4
                grp_last = None
                for sub_j in range(2):
                    jc = 8 + pr * 2 + sub_j
                    for cc in range(8):
                        lhsT = sap(rot, [[32768, 128], [1, 128]], offset=cc * 3072 + jc * 128)
                        rhs = sap(xT, [[2048, 128], [1, 256]], offset=cc * 256)
                        first = (cc == 0)
                        deps = (qev[pr],) if (first and sub_j == 0) else ()
                        grp_last = mm(pk(bk, sub_j * 256, 256), lhsT, rhs, first, cc == 7,
                                      deps, 1 if (cc == 7 and sub_j == 1) else 0,
                                      new_gen=(first and sub_j == 0))
                tokk = VV(lambda e, pr=pr, bk=bk:
                          e.tensor_mul(kTs[:, pr * 512:(pr + 1) * 512],
                                       pk(bk, 0, 512),
                                       lam_sb[:, 0:512]),
                          (grp_last, lam_copy[pr]), 1)
                kev.append(tokk)
                if pr + 1 < 4:
                    nxt = pr + 1
                    lam_copy[nxt] = VV(lambda e, nxt=nxt, bkk=lam_banks[nxt]:
                                       e.tensor_copy(lam_sb[:, 0:512],
                                                     pk(bkk, 0, 512)),
                                       (lam_m[nxt], tokk), 1)
            sums_free[0] = lam_copy[2]
            sums_free[1] = lam_copy[3]

            vev = []
            for g in range(4):
                s, dvh = divmod(g, 2)
                bk = 2 + (g % 2)
                grp_last = None
                for cc in range(8):
                    lhsT = sap(xT, [[2048, 128], [1, 128]], offset=cc * 256 + s * 128)
                    rhs = sap(rot, [[32768, 128], [1, 512]], offset=cc * 3072 + 2048 + dvh * 512)
                    deps = ()
                    if cc == 0:
                        deps = (kev[2 + (g % 2)],) if g < 2 else (vev[g - 2],)
                    grp_last = mm(pk(bk, 0, 512), lhsT, rhs, cc == 0, cc == 7,
                                  deps, 1 if cc == 7 else 0, new_gen=(cc == 0))
                tokev = SS(lambda e, s=s, dvh=dvh, bk=bk:
                           e.copy(vtok[:, s * 1024 + dvh * 512:s * 1024 + (dvh + 1) * 512],
                                  pk(bk, 0, 512)),
                           (grp_last, y2_free[1]), 1)
                vev.append(tokev)
            last_A_qkv_tok = vev[-1]
            xT_free = vev[-1]

            prev_scores = {}
            prev_exp = {}
            prev_sums = {}
            prev_recip = {}
            prev_av = {}
            rb_done = {}
            bulk_toks = []
            ao_toks = []

            def do_scores(h):
                par = h % 2
                bk = 4 + par
                cc_h, ho = divmod(h, 2)
                po = ho * 64
                last = None
                for ks in range(2):
                    lhsT = bass.AP(kTs, po * 2048 + cc_h * 256 + ks * 128,
                                   [[2048, 64], [1, 128]])
                    rhs = bass.AP(qT, po * 2048 + cc_h * 256, [[2048, 64], [1, 256]])
                    deps = ()
                    if ks == 0:
                        deps = (score_free[par], kev[cc_h // 2], qev[cc_h // 2],
                                lam_copy[par])
                    last = mm(pk(bk, ks * 256, 256), lhsT, rhs, True, True,
                              deps, 1 if ks == 1 else 0, new_gen=(ks == 0))
                prev_scores[h] = last

            def do_exp(h):
                par = h % 2
                slot = h % 4
                tok = SS(lambda e, par=par, slot=slot:
                         e.activation(bass.AP(eT, slot * 512, [[2048, 128], [1, 512]]),
                                      pk(4 + par, 0, 512), AF.Exp, bias=cconst[:, 0:1], scale=1.0),
                         (prev_scores[h], eT_free[slot], t_c12), 1)
                prev_exp[h] = tok
                score_free[par] = tok

            def do_sums(h):
                par = h % 2
                slot = h % 4
                last = None
                for ks in range(2):
                    rhs = bass.AP(eT, slot * 512 + ks * 256, [[2048, 128], [1, 256]])
                    deps = ()
                    if ks == 0:
                        deps = (prev_exp[h], sums_free[par], t_ones)
                    last = mm(pk(6 + par, 0, 256, 0, 1), onesbf[:, 0:1], rhs,
                              ks == 0, ks == 1, deps, 1 if ks == 1 else 0,
                              new_gen=(ks == 0))
                prev_sums[h] = last

            def do_recip(h):
                par = h % 2
                c0 = (h % 2) * 512 + ((h // 2) % 2) * 256
                tok = VV(lambda e, par=par, c0=c0:
                         e.reciprocal(rTbf(c0, 256),
                                      pk(6 + par, 0, 256, 0, 1)),
                         (prev_sums[h], rb_done.get(h // 4 - 1)), 1, True)
                prev_recip[h] = tok
                sums_free[par] = tok

            def do_bulk(g):
                bk = 2 + (g % 2)
                bdep = vev[2 + (g % 2)] if g < 2 else qk_free[g % 2]
                mm(pk(bk, 0, 512, 0, 64), onesbf[32:33, 0:64], rTbf(0, 512),
                   True, True, (prev_recip[4 * g + 3], bdep), 0,
                   new_gen=True)
                rbtok = mm(pk(bk, 0, 512, 64, 64), onesbf[32:33, 0:64],
                           rTbf(512, 512), True, True, (), 1, new_gen=True)
                rb_done[g] = rbtok
                src = bass.AP(aoT, g * 512, [[2048, 128], [1, 512]])
                rb = bass.AP(ps[1], (g % 2) * 512, [[1024, 128], [1, 512]])
                tok = VV(lambda e, src=src, rb=rb: e.tensor_mul(src, src, rb),
                         (rbtok, ao_toks[-1]), 1)
                qk_free[g % 2] = tok
                bulk_toks.append(tok)

            def do_av(h):
                par = h % 2
                slot = h % 4
                last = None
                for ks in range(2):
                    lhsT = bass.AP(vtok, ks * 1024 + h * 64, [[2048, 128], [1, 64]])
                    rhs = bass.AP(eT, slot * 512 + ks * 256, [[2048, 128], [1, 256]])
                    deps = ()
                    if ks == 0:
                        deps = (prev_exp[h], av_free[par], kev[par],
                                lam_copy[par], lam_copy[2 + par], vev[3],
                                router_free[par])
                    last = mm(pk(par, 0, 256, 0, 64), lhsT, rhs, ks == 0, ks == 1,
                              deps, 1 if ks == 1 else 0, new_gen=(ks == 0))
                prev_av[h] = last
                eT_free[slot] = last

            def do_ao(h):
                par = h % 2
                cc_h, ho = divmod(h, 2)
                dst = bass.AP(aoT, ho * 64 * 2048 + cc_h * 256, [[2048, 64], [1, 256]])
                tok = SS(lambda e, dst=dst, par=par:
                         e.copy(dst, pk(par, 0, 256, 0, 64)),
                         (prev_av[h], aoT_free), 1)
                av_free[par] = tok
                return tok

            if attn_serial:
                for h_ in range(H):
                    do_scores(h_)
                    mm_flush()
                    do_exp(h_)
                    do_sums(h_)
                    mm_flush()
                    do_recip(h_)
                    do_av(h_)
                    mm_flush()
                    ao_toks.append(do_ao(h_))
                    if h_ % 4 == 3:
                        do_bulk(h_ // 4)
                        mm_flush()
            else:
                for step in range(H + 2):
                    if step < H:
                        do_scores(step)
                        do_exp(step)
                    if 2 <= step < H + 2:
                        do_av(step - 2)
                        ao_toks.append(do_ao(step - 2))
                        if (step - 2) % 4 == 3:
                            do_bulk((step - 2) // 4)
                    if 1 <= step < H + 1:
                        do_sums(step - 1)
                        do_recip(step - 1)
            mm_flush()
            if tl + 1 < n_a[half]:
                a_cast[(tile + 1, 0)] = mk_acast(tile + 1, 0)
            if tl >= 1:
                a_epi_tail(tile - 1)

            for s in range(2):
                k = tile * 2 + s
                grp = []
                for jh in range(2):
                    bk = 4 + 2 * s + jh
                    last = None
                    for cc in range(8):
                        lhsT = sap(aoT, [[2048, 128], [1, 128]], offset=cc * 256 + s * 128)
                        rhs = sap(rot, [[32768, 128], [1, 512]],
                                  offset=24576 + cc * 1024 + jh * 512)
                        deps = ()
                        if cc == 0:
                            deps = ((score_free[jh], bulk_toks[-1]) if s == 0 else
                                    (sums_free[0], sums_free[1], bulk_toks[-1]))
                        last = mm(pk(bk, 0, 512), lhsT, rhs, cc == 0, False, deps, 0,
                                  new_gen=(cc == 0))
                    mm(pk(bk, 0, 512), onesbf[32:33, 0:128],
                       boutrow(jh * 512, 512), False, False,
                       (t_bo_lo, t_ones), 0, new_gen=False)
                    last = mm(pk(bk, 0, 512), onesbf[32:33, 0:128],
                              boutrow(1024 + jh * 512, 512),
                              False, True, (), 1, new_gen=False)
                    grp.append(last)
                mm_flush()
                last_A_wout_tok = grp[-1]
                pw = bass.AP(ps[2 + s], 0, [[1024, 128], [1, 1024]])
                z1 = sap(xf32, [[3072, 128], [1, 1024]], offset=(k % 3) * 1024)
                tok_z1 = VV(lambda e, pw=pw, z1=z1:
                            e.scalar_tensor_tensor(z1, pw, 0.0, z1, ALU.add, ALU.add,
                                                   accum_out=stz[:, 0:1]),
                            tuple(grp) + (xft[k], xbf_toks[s], last_sdA), 1, True)
                if s == 0:
                    wout_free45 = tok_z1
                else:
                    wout_free67 = tok_z1
                slot = tl * 2 + s
                tok_mu = VV(lambda e:
                            e.tensor_scalar_mul(stz[:, 1:2], stz[:, 0:1], 1.0 / 1024.0),
                            (tok_z1,), 1, True)
                x1dst = sap(x1h, [[8192, 128], [1, 1024]], offset=slot * 1024)
                tok_p1 = VV(lambda e, z1=z1, x1dst=x1dst:
                            e.tensor_scalar(x1dst, z1, stz[:, 1:2], None,
                                            ALU.subtract),
                            (tok_mu,) + tuple(x1slot_consumed[slot]), 1, True)
                scratch = eT[:, :].bitcast(F32)
                tok_var = VV(lambda e, sc=scratch, x1dst=x1dst:
                             e.scalar_tensor_tensor(sc, x1dst, 0.0, x1dst,
                                                    ALU.add, ALU.mult,
                                                    accum_out=stz[:, 2:3]),
                             (tok_p1, eT_free[0], eT_free[1],
                              eT_free[2], eT_free[3]), 1, True)
                tok_ve = VV(lambda e, s=s:
                            e.tensor_scalar(stz[:, 3 + s:4 + s], stz[:, 2:3],
                                            1.0 / 1024.0, 1e-5, ALU.mult, ALU.add),
                            (tok_var, last_tailA), 1, True)
                a_epi[(tile, s)] = (tok_ve, x1dst, slot)
                x1tok[k] = tok_p1
                if s == 0 and k + 3 < 16:
                    xf_load(k + 3, tok_p1)
                    if tl + 1 < n_a[half]:
                        a_cast[(tile + 1, 1)] = mk_acast(tile + 1, 1)
                for i_ in range(4):
                    eT_free[i_] = tok_var
        if n_b[half]:
            src2 = sap(x1h, [[8192, 128], [1, 2048]])
            b_cast[0] = SS(lambda e, s_=src2: e.copy(xcast[:, :], s_),
                           (x1slot_ready[0], x1slot_ready[1], xcast_free), 1)
            xcast_free = b_cast[0]
        if n_a[half]:
            a_epi_tail(half * 4 + n_a[half] - 1)
        for i in range(NG):
            ht_free[i] = wout_free67

        if dump == "x1h" and half == 0:
            E.op("Y", lambda e: e.dma_start(dbg_d.ap(), x1h[:, :]),
                 tuple(x1slot_ready), 16, "O", True)
        if dump == "ln" and half == 0:
            gates = (x1slot_ready[0], x1slot_ready[1])
            E.op("Y", lambda e: e.dma_start(bass.AP(dbg_d, 0, [[8192, 128], [1, 1024]]),
                                            xf32[:, 0:1024]), gates, 16, "O", True)
            E.op("Y", lambda e: e.dma_start(bass.AP(dbg_d, 2048, [[8192, 128], [1, 2048]]),
                                            x1h[:, 0:2048]), (), 16, "O", True)
            E.op("Y", lambda e: e.dma_start(
                bass.AP(dbg_d, 4096, [[8192, 128], [1, 8]]), stz[:, :],
                allow_slow_non_contiguous=True),
                (), 16, "O", True)
        if dump == "all" and half == 0:
            g0 = tuple(x1slot_ready[0:2])
            E.op("Y", lambda e: e.dma_start(dball["d_xT"].ap(), xT[:, :]), g0, 16, "O", True)
            E.op("Y", lambda e: e.dma_start(dball["d_qT"].ap(), qT[:, :]), (), 16, "O", True)
            E.op("Y", lambda e: e.dma_start(dball["d_kTs"].ap(), kTs[:, :]), (), 16, "O", True)
            E.op("Y", lambda e: e.dma_start(dball["d_vtok"].ap(), vtok[:, :]), (), 16, "O", True)
            E.op("Y", lambda e: e.dma_start(dball["d_eT"].ap(), eT[:, :]), (), 16, "O", True)
            E.op("Y", lambda e: e.dma_start(dball["d_aoT"].ap(), aoT[:, :]), (), 16, "O", True)
            E.op("Y", lambda e: e.dma_start(dball["d_lam"].ap(), lam_sb[:, :]), (), 16, "O", True)
            E.op("Y", lambda e: e.dma_start(dball["d_rot"].ap(), rot[:, :]), (), 16, "O", True)
            E.op("Y", lambda e: e.dma_start(dball["d_z1"].ap(), xf32[:, 0:1024]), (), 16, "O", True)
            E.op("Y", lambda e: e.dma_start(bass.AP(dball["d_x1h"], 0, [[8192, 128], [1, 2048]]),
                                            sap(x1h, [[8192, 128], [1, 2048]])), (), 16, "O", True)
            E.op("Y", lambda e: e.dma_start(
                bass.AP(dball["d_st"], 0, [[16, 128], [1, 1]]),
                sap(stz, [[8, 128], [1, 1]]), allow_slow_non_contiguous=True), (), 16, "O", True)
        if dump == "rot0" and half == 0:
            dbg_bf = dbg_d.ap().bitcast(BF16)
            E.op("Y", lambda e: e.dma_start(dbg_bf[:, 0:12288],
                                            sap(rot, [[32768, 128], [1, 12288]])),
                 rot_ready0, 16, "O", True)
            E.op("Y", lambda e: e.dma_start(dbg_bf[:, 12288:12416], identbf[:, :]),
                 (), 16, "O", True)
            E.op("Y", lambda e: e.dma_start(dbg_bf[:, 12416:13440], stag[:, :]),
                 (), 16, "O", True)
        if dump == "rot" and half == 0:
            gates = (x1slot_ready[0], x1slot_ready[1])
            for cc_ in range(8):
                E.op("G", lambda e, cc_=cc_: e.dma_start(
                    bass.AP(dbg_d, cc_ * 1024, [[8192, 128], [1, 1024]]),
                    sap(rot, [[32768, 128], [1, 1024]], offset=cc_ * 3072)),
                    gates, 16, "O", True)
        if dump == "mid" and half == 0:
            dd = dbg_d.ap()
            gates = tuple(t for t in x1slot_ready if t) + (t_lambf,)
            E.op("G", lambda e: e.dma_start(bass.AP(dbg_d, 0, [[8192, 128], [1, 2048]]), xT[:, :]),
                 gates, 16, "O", True)
            E.op("G", lambda e: e.dma_start(bass.AP(dbg_d, 2048, [[8192, 128], [1, 2048]]), qT[:, :]),
                 (), 16, "O", True)
            E.op("G", lambda e: e.dma_start(bass.AP(dbg_d, 4096, [[8192, 128], [1, 2048]]), kTs[:, :]),
                 (), 16, "O", True)
            E.op("G", lambda e: e.dma_start(bass.AP(dbg_d, 6144, [[8192, 128], [1, 2048]]),
                                            aoT[:, :]),
                 (), 16, "O", True)

        w2_chunks = []
        for ch in range(8 if n_b[half] else 0):
            gate = last_A_qkv_tok if ch < 6 else last_A_wout_tok
            dst = sap(rot, [[32768, 128], [1, 4096]], offset=ch * 4096)
            src = bass.AP(w2img_d, ch * 4096, [[32768, 128], [1, 4096]])
            gimg = t_w2img[ch] if t_w2img else tok_gs
            deps = (gate, gimg, t_rotimg)
            w2_chunks.append(dmaY(dst, src, deps, sem=f"W2C{ch}"))


        def mk_bcast(tl2):
            sl2 = tl2 * 2
            src2 = sap(x1h, [[8192, 128], [1, 2048]], offset=sl2 * 1024)
            b_cast[tl2] = SS(lambda e, s_=src2: e.copy(xcast[:, :], s_),
                             (x1slot_ready[sl2], x1slot_ready[sl2 + 1],
                              xcast_free), 1)

        def b_head1(tl):
            nonlocal xcast_free, xT_ready, last_te, last_tp
            tile = half * 4 + tl
            if tl not in b_cast:
                mk_bcast(tl)
            tok_cast = b_cast[tl]
            xcast_free = tok_cast

            trev = []
            for g in range(4):
                s, cq = divmod(g, 2)
                bk = g % 2
                mts = []
                for b in range(4):
                    cc = cq * 4 + b
                    lhsT = sap(xcast, [[2048, 128], [1, 128]], offset=s * 1024 + cc * 128)
                    deps = ()
                    if b == 0:
                        deps = (tok_cast, tr_free[bk], av_free[bk], router_free[bk],
                                hbank_free[bk])
                    tokm = mm(pk(bk, b * 128, 128), lhsT, identbf[:, :],
                              b == 0, b == 3, deps, 1 if b == 3 else 0, new_gen=(b == 0))
                    mts.append(tokm)
                dst_ap = sap(xT, [[2048, 128], [256, 4], [1, 128]],
                             offset=cq * 4 * 256 + s * 128)
                src_ap = bass.AP(ps[0], (bk % 2) * 512, [[1024, 128], [128, 4], [1, 128]])
                tokev = SS(lambda e, d=dst_ap, s_=src_ap: e.copy(d, s_),
                           (mts[-1], xT_free), 1)
                trev.append(tokev)
                tr_free[bk] = tokev
            mm_flush()
            xT_ready = (trev[-1],)
            xcast_free = trev[-1]

            p7_toks = []
            rtm_toks = []
            for s in range(2):
                bk = s
                last = None
                for cc in range(8):
                    lhsT = sap(xT, [[2048, 128], [1, 128]], offset=cc * 256 + s * 128)
                    rhs = sap(wrT, [[64, 128], [1, 8]], offset=cc * 8)
                    deps = ()
                    if cc == 0:
                        deps = (tr_free[bk], t_wrT[-1]) + xT_ready
                    last = mm(pk(bk, 0, 8), lhsT, rhs, cc == 0, cc == 7, deps,
                              1 if cc == 7 else 0, new_gen=(cc == 0))
                tok_m = VV(lambda e, bk=bk:
                           e.tensor_reduce(rmax[:, :], pk(bk, 0, 8), AX.X, ALU.max,
                                           negate=True),
                           (last, last_te), 1, True)
                rtm_toks.append(tok_m)
                tok_e = SS(lambda e, bk=bk:
                           e.activation(er[:, :], pk(bk, 0, 8), AF.Exp,
                                        bias=rmax[:, :], scale=1.0),
                           (tok_m, last_tp), 1)
                router_free[bk] = tok_e
                last_te = tok_e
                tok_s = VV(lambda e: e.reduce_sum(rsum[:, :], er[:, :], AX.X),
                           (tok_e,), 1, True)
                tok_r = VV(lambda e: e.reciprocal(rrec[:, :], rsum[:, :]), (tok_s,), 1, True)
                pcol = (tl % 2) * 2 + s
                tok_p = VV(lambda e, pcol=pcol: e.tensor_mul(p7[:, pcol:pcol + 1],
                                                             er[:, 7:8], rrec[:, :]),
                           (tok_r,), 1, True)
                last_tp = tok_p
                p7_toks.append(tok_p)
            epi_data[tl] = {"p7": p7_toks, "rtm": rtm_toks}

        def b_head2(tl):
            nonlocal xT_free, aoT_free
            gelu_toks = [None] * 32
            for fcs in range(36):
                if fcs < 32:
                    fc = fcs
                    bk = fc % 4
                    last = None
                    for cc in range(8):
                        lhsT = sap(w1a, [[32768, 128], [1, 128]], offset=cc * 4096 + fc * 128)
                        rhs = sap(xT, [[2048, 128], [1, 256]], offset=cc * 256)
                        deps = ()
                        if cc == 0:
                            deps = (h_free[fc % 4], qk_free[0], qk_free[1],
                                    tok_w1) + xT_ready
                            if fc < 2:
                                deps = deps + (tr_free[fc], router_free[fc],
                                               epi_data[tl]["rtm"][fc])
                        last = mm(pk(bk, 0, 256), lhsT, rhs, cc == 0, cc == 7, deps,
                                  1 if cc == 7 else 0, new_gen=(cc == 0))
                    slot = fc % NG
                    gelu_toks[fc] = SS(lambda e, slot=slot, fc=fc, bk=bk:
                               e.activation(hT[:, slot * 256:(slot + 1) * 256],
                                            pk(bk, 0, 256), AF.Gelu,
                                            bias=b1col[:, fc:fc + 1], scale=1.0),
                               (last, ht_free[slot], t_b1), 1)
                    h_free[fc % 4] = gelu_toks[fc]
                    hbank_free[fc % 4] = gelu_toks[fc]
                    if fc == 31:
                        xT_free = last
                if fcs >= 4:
                    fc = fcs - 4
                    slot = fc % NG
                    w2ch = w2_chunks[fc // 4]
                    ylast = None
                    for s in range(2):
                        for jh in range(2):
                            ybk = 4 + 2 * s + jh
                            lhsT = sap(hT, [[2048, 128], [1, 128]], offset=slot * 256 + s * 128)
                            rhs = sap(rot, [[32768, 128], [1, 512]], offset=fc * 1024 + jh * 512)
                            deps = [gelu_toks[fc], w2ch]
                            if fc == 0:
                                deps += [y_free[2 * s + jh], score_free[jh],
                                         sums_free[jh], wout_free45, wout_free67]
                            is_last = (s == 1 and jh == 1)
                            ylast = mm(pk(ybk, 0, 512), lhsT, rhs, fc == 0, False,
                                       tuple(deps), 1 if is_last else 0, new_gen=(fc == 0))
                    ht_free[slot] = ylast
                    last_B_ymm_tok[fc] = ylast
            mm_flush()
            aoT_free = last_B_ymm_tok[31]
            if tl + 1 < n_b[half]:
                mk_bcast(tl + 1)

            for s in range(2):
                grp = []
                for jh in range(2):
                    ybk = 4 + 2 * s + jh
                    mm(pk(ybk, 0, 512), onesbf[64:65, 0:128],
                       b2row(jh * 512, 512), False, False,
                       (t_b2_lo, t_ones), 0, new_gen=False)
                    tok = mm(pk(ybk, 0, 512), onesbf[64:65, 0:128],
                             b2row(1024 + jh * 512, 512),
                             False, True, (), 1, new_gen=False)
                    grp.append(tok)
                mm_flush()
                epi_data[tl][f"grp{s}"] = grp

        def b_epi1(tl):
            nonlocal wout_free45, wout_free67
            dat = epi_data[tl]
            for s in range(2):
                grp = dat[f"grp{s}"]
                slot = tl * 2 + s
                buf = slot % 2
                pcol = (tl % 2) * 2 + s
                py = bass.AP(ps[2 + s], 0, [[1024, 128], [1, 1024]])
                x1src = sap(x1h, [[8192, 128], [1, 1024]], offset=slot * 1024)
                y2ap = (qT if buf == 0 else vtok)[:, :].bitcast(F32)
                stb = stzb[(tl % 2) * 2 + s]
                tok_z2 = VV(lambda e, py=py, x1src=x1src, y2ap=y2ap, pcol=pcol, stb=stb:
                            e.scalar_tensor_tensor(y2ap, py, p7[:, pcol:pcol + 1], x1src,
                                                   ALU.mult, ALU.add,
                                                   accum_out=stb[:, 0:1]),
                            tuple(grp) + (dat["p7"][s], y2_free[buf],
                                          x1slot_ready[slot]),
                            1, True)
                dat[f"z2_{s}"] = tok_z2
                x1slot_consumed[slot] = [tok_z2]
                y_free[2 * s] = tok_z2
                y_free[2 * s + 1] = tok_z2
                if s == 0:
                    wout_free45 = tok_z2
                else:
                    wout_free67 = tok_z2

        def b_epi2a(tl):
            dat = epi_data[tl]
            for s in range(2):
                buf = (tl * 2 + s) % 2
                y2ap = (qT if buf == 0 else vtok)[:, :].bitcast(F32)
                scratch = kTs[:, :].bitcast(F32)
                tok_ve, _ = ln_stats_a(y2ap, scratch, dat[f"z2_{s}"],
                                       st=stzb[(tl % 2) * 2 + s])
                dat[f"ve_{s}"] = tok_ve

        def b_epi2b(tl):
            tile = half * 4 + tl
            t0 = tile * CS
            dat = epi_data[tl]
            for s in range(2):
                buf = (tl * 2 + s) % 2
                y2ap = (qT if buf == 0 else vtok)[:, :].bitcast(F32)
                stb = stzb[(tl % 2) * 2 + s]
                tok_rs = ln_stats_b(dat[f"ve_{s}"], st=stb)
                tok_o = VV(lambda e, y2ap=y2ap, stb=stb:
                           e.tensor_scalar(y2ap, y2ap, stb[:, 1:2],
                                           stb[:, 7:8],
                                           ALU.subtract, ALU.mult),
                           (tok_rs,), 1)
                tok_dma = dmaO(bass.AP(out_d, (t0 + s * 128) * 1024, [[1024, 128], [1, 1024]]),
                               y2ap, (tok_o,), sem=("O" if buf == 0 else "O2"))
                y2_free[buf] = tok_dma

        epi_data = {}
        for tl in range(n_b[half]):
            b_head1(tl)
            if tl >= 2:
                b_epi2b(tl - 2)
            if tl > 0:
                b_epi1(tl - 1)
                b_epi2a(tl - 1)
            b_head2(tl)
        if n_b[half]:
            L = n_b[half] - 1
            if L >= 1:
                b_epi2b(L - 1)
            b_epi1(L)
            b_epi2a(L)
            if half + 1 < n_halves:
                a_cast[(4, 0)] = mk_acast(4, 0)
                a_cast[(4, 1)] = mk_acast(4, 1)
            b_epi2b(L)

    E.op("Y", lambda e: e.nop(),
         (("O", E.cnt["O"], True), ("O2", E.cnt["O2"], True)), 0)

    with nc.Block() as block, nc.allow_low_precision(
            reason="bf16 matmul operands by design; fp32 accumulation stays in PSUM"):
        @block.tensor
        def _(eng):
            E.replay("T", eng, sems)

        @block.vector
        def _(eng):
            E.replay("V", eng, sems)

        @block.scalar
        def _(eng):
            E.replay("S", eng, sems)

        @block.gpsimd
        def _(eng):
            E.replay("G", eng, sems)

        @block.sync
        def _(eng):
            E.replay("Y", eng, sems)

    return nc


_NC_CACHE = None


def kernel(L0_attn_x, L0_attn_Wqkv, L0_attn_Wout, L0_attn_bout, L0_attn_Wl1,
           L0_attn_bl1, L0_attn_Wl2, L0_attn_bl2, L0_ffn_Wr, L0_ffn_W1, L0_ffn_b1,
           L0_ffn_W2, L0_ffn_b2):
    nc = build_nc()
    x = np.ascontiguousarray(np.asarray(L0_attn_x, dtype=np.float32))
    common = {
        "wqkv": np.ascontiguousarray(np.asarray(L0_attn_Wqkv, np.float32)),
        "wout": np.ascontiguousarray(np.asarray(L0_attn_Wout, np.float32)),
        "bout": np.ascontiguousarray(np.asarray(L0_attn_bout, np.float32)),
        "wl1": np.ascontiguousarray(np.asarray(L0_attn_Wl1, np.float32)),
        "bl1": np.ascontiguousarray(np.asarray(L0_attn_bl1, np.float32)),
        "wl2": np.ascontiguousarray(np.asarray(L0_attn_Wl2, np.float32)),
        "bl2": np.ascontiguousarray(np.asarray(L0_attn_bl2, np.float32)),
        "wr": np.ascontiguousarray(np.asarray(L0_ffn_Wr, np.float32)),
        "w1": np.ascontiguousarray(np.asarray(L0_ffn_W1, np.float32)[NE - 1]),
        "b1": np.ascontiguousarray(np.asarray(L0_ffn_b1, np.float32)[NE - 1]),
        "w2": np.ascontiguousarray(np.asarray(L0_ffn_W2, np.float32)[NE - 1]),
        "b2": np.ascontiguousarray(np.asarray(L0_ffn_b2, np.float32)[NE - 1]),
    }
    in_maps = [dict(common, x=np.ascontiguousarray(x[b])) for b in range(8)]
    res = run_bass_kernel_spmd(nc, in_maps, core_ids=list(range(8)))
    out = np.stack([np.asarray(res.results[b]["out"], dtype=np.float32)
                    for b in range(8)], axis=0)
    return out
